# version 21
# baseline (speedup 1.0000x reference)
import numpy as np
import ml_dtypes
import concourse.bass as bass
import concourse.mybir as mybir
from concourse.bass_utils import run_bass_kernel_spmd

F32 = mybir.dt.float32
BF16 = mybir.dt.bfloat16
AF = mybir.ActivationFunctionType
ALU = mybir.AluOpType
AX = mybir.AxisListType.X
BF = ml_dtypes.bfloat16

N_CORES = 8
ROWS_PER_CORE = 2
S = 2048
D = 1024
NT = S // 128
KD = D // 128



class Sem:
    def __init__(self, handle):
        self.h = handle
        self.n = 0

    def bump(self, k):
        self.n += k
        return self.n


class Prog:
    ENGS = ("t", "v", "s", "g", "y")

    def __init__(self, nc):
        self.nc = nc
        self.q = {e: [] for e in self.ENGS}

    def emit(self, eng, fn):
        self.q[eng].append(fn)

    def wait(self, eng, sem, val=None):
        v = sem.n if val is None else val
        self.q[eng].append(lambda e, s=sem.h, v=v: e.wait_ge(s, v))

    def run(self, block):
        if self.q["g"]:
            @block.gpsimd
            def _(eng):
                for fn in self.q["g"]:
                    fn(eng)
        if self.q["t"]:
            @block.tensor
            def _(eng):
                for fn in self.q["t"]:
                    fn(eng)
        if self.q["s"]:
            @block.scalar
            def _(eng):
                for fn in self.q["s"]:
                    fn(eng)
        if self.q["v"]:
            @block.vector
            def _(eng):
                for fn in self.q["v"]:
                    fn(eng)
        if self.q["y"]:
            @block.sync
            def _(eng):
                for fn in self.q["y"]:
                    fn(eng)


def _dma(p, eng, sem, out, in_):
    p.emit(eng, lambda e, o=out, i=in_, s=sem.h: e.dma_start(out=o, in_=i).then_inc(s, 16))
    return sem.bump(16)


def _emit_ln_tile(p, x_f32, xn_bf, scr, sc, ve, eps=1e-5, gcol=None, bcol=None,
                  out_extra=None):
    nc = p.nc
    sq, c0, c1, c2, c3, c4 = scr["sq"], scr["c0"], scr["c1"], scr["c2"], scr["c3"], scr["c4"]
    p.emit("s", lambda e: e.activation(sq, x_f32, AF.Square).then_inc(sc.h, 1))
    sc.bump(1)
    p.wait("v", sc)
    def vchain(e):
        e.reduce_sum(c0, x_f32, axis=AX)
        e.reduce_sum(c1, sq, axis=AX)
        e.drain()
        e.tensor_scalar(c0, c0, 1.0 / D, None, ALU.mult)
        e.tensor_scalar(c1, c1, 1.0 / D, None, ALU.mult)
        e.drain()
        e.tensor_tensor(c2, c0, c0, ALU.mult)
        e.drain()
        e.tensor_tensor(c1, c1, c2, ALU.subtract)
        e.drain()
        e.tensor_scalar(c1, c1, eps, None, ALU.add).then_inc(ve.h, 1)
    p.emit("v", vchain)
    ve.bump(1)
    p.wait("s", ve)
    p.emit("s", lambda e: e.activation(c3, c1, AF.Sqrt).then_inc(sc.h, 1))
    sc.bump(1)
    p.wait("v", sc)
    def vnorm(e):
        e.reciprocal(c4, c3)
        e.drain()
        e.tensor_scalar(xn_bf, x_f32, c0, c4, ALU.subtract, ALU.mult).then_inc(ve.h, 1)
    p.emit("v", vnorm)
    ve.bump(1)



def emit_pre(cx, x_in, id_in, xnT_out, pfx="pre"):
    nc, p, E = cx.nc, cx.p, cx.E
    ld, sc, ve, te, so = cx.ld, cx.sc, cx.ve, cx.te, cx.sop[0]
    if True:
        xt = [E(nc.sbuf_tensor(pfx + f"_xt{i}", [128, D], F32)) for i in range(2)]
        xnt = E(nc.sbuf_tensor(pfx + "_xnt", [128, D], F32))
        sq = E(nc.sbuf_tensor(pfx + "_sq", [128, D], F32))
        cols = [E(nc.sbuf_tensor(pfx + f"_c{i}", [128, 1], F32)) for i in range(5)]
        ident = E(nc.sbuf_tensor(pfx + "_ident", [128, 128], F32))
        xnT_sb = E(nc.sbuf_tensor(pfx + "_xnTsb", [128, KD * S], BF16))
        pt = [E(nc.psum_tensor(pfx + f"_pt{i}", [128, 512], F32)) for i in range(2)]
        scr = {"sq": sq[:], "c0": cols[0][:], "c1": cols[1][:], "c2": cols[2][:],
               "c3": cols[3][:], "c4": cols[4][:]}

        _dma(p, "y", ld, ident[:], id_in)
        p.wait("y", ld)
        p.wait("t", ld)
        ve0 = ve.n

        for b in range(ROWS_PER_CORE):
            for ti in range(NT):
                it = b * NT + ti
                buf = xt[it % 2][:]
                if it >= 2:
                    p.wait("y", ve, ve0 + (it - 2) * 3 + 2)
                _dma(p, "y", ld, buf, x_in[b, ti * 128:(ti + 1) * 128, :])
                p.wait("y", ld)
                p.wait("s", ld)
                p.wait("v", ld)
                _emit_ln_tile(p, buf, xnt[:], scr, sc, ve)
                p.wait("t", ve)
                def tchain(e):
                    for kt in range(KD):
                        bank = pt[kt // 4]
                        ins = e.transpose(bank[:, (kt % 4) * 128:(kt % 4 + 1) * 128],
                                          xnt[:, kt * 128:(kt + 1) * 128], ident[:])
                        if kt == KD - 1:
                            ins.then_inc(te.h, 1)
                p.emit("t", tchain)
                te.bump(1)
                p.wait("v", te)
                def evac(e, ti=ti):
                    for g in range(2):
                        for j in range(4):
                            kt = g * 4 + j
                            ins = e.tensor_copy(
                                xnT_sb[:, kt * S + ti * 128: kt * S + (ti + 1) * 128],
                                pt[g][:, j * 128:(j + 1) * 128])
                    ins.then_inc(ve.h, 1)
                p.emit("v", evac)
                ve.bump(1)
                p.wait("t", ve)
            p.wait("y", ve)
            for kt in range(KD):
                _dma(p, "y", so, xnT_out[b, kt], xnT_sb[:, kt * S:(kt + 1) * S])
        p.wait("y", so)



def emit_residual(cx, with_ln, x_in, s_in, g_in, x1_out, xnT_out=None, id_in=None, pfx="res"):
    nc, p, E = cx.nc, cx.p, cx.E
    ld, sc, ve, te, so = cx.ld, cx.sc, cx.ve, cx.te, cx.sop[0]
    if True:
        xt = [E(nc.sbuf_tensor(pfx + f"_xt{i}", [128, D], F32)) for i in range(2)]
        stt = [E(nc.sbuf_tensor(pfx + f"_st{i}", [128, D], F32)) for i in range(2)]
        gt = [E(nc.sbuf_tensor(pfx + f"_gt{i}", [128, D], F32)) for i in range(2)]
        gs2 = [E(nc.sbuf_tensor(pfx + f"_gs{i}", [128, D], F32)) for i in range(2)]
        x1t = [E(nc.sbuf_tensor(pfx + f"_x1t{i}", [128, D], F32)) for i in range(2)]
        if with_ln:
            xnt = E(nc.sbuf_tensor(pfx + "_xnt", [128, D], F32))
            sq = E(nc.sbuf_tensor(pfx + "_sq", [128, D], F32))
            cols = [E(nc.sbuf_tensor(pfx + f"_c{i}", [128, 1], F32)) for i in range(5)]
            ident = E(nc.sbuf_tensor(pfx + "_ident", [128, 128], F32))
            xnT_sb = E(nc.sbuf_tensor(pfx + "_xnTsb", [128, KD * S], BF16))
            pt = [E(nc.psum_tensor(pfx + f"_pt{i}", [128, 512], F32)) for i in range(2)]
            scr = {"sq": sq[:], "c0": cols[0][:], "c1": cols[1][:], "c2": cols[2][:],
                   "c3": cols[3][:], "c4": cols[4][:]}
        if with_ln:
            _dma(p, "y", ld, ident[:], id_in)
            p.wait("y", ld)
            p.wait("t", ld)

        ve_per = 1 + (3 if with_ln else 0)
        so_after_tile = {}
        ve0 = ve.n
        for b in range(ROWS_PER_CORE):
            for ti in range(NT):
                it = b * NT + ti
                bx, bs, bg, bo = xt[it % 2][:], stt[it % 2][:], gt[it % 2][:], x1t[it % 2][:]
                sl = slice(ti * 128, (ti + 1) * 128)
                gbuf = gs2[it % 2][:]
                if it >= 2:
                    p.wait("y", ve, ve0 + (it - 2) * ve_per + ve_per)

                _dma(p, "y", ld, bx, x_in[b, sl, :])
                _dma(p, "y", ld, bs, s_in[b, sl, :])
                _dma(p, "y", ld, bg, g_in[b, sl, :])
                p.wait("y", ld)
                if it >= 2:
                    p.wait("s", ve, ve0 + (it - 2) * ve_per + 1)
                p.wait("s", ld)
                p.emit("s", lambda e, g=bg, gb=gbuf: e.activation(gb, g, AF.Silu).then_inc(sc.h, 1))
                sc.bump(1)
                p.wait("v", sc)
                p.wait("v", ld)
                if it >= 2:
                    p.wait("v", so, so_after_tile[it - 2])
                def resid(e, bs=bs, bx=bx, bo=bo, gb=gbuf):
                    e.tensor_tensor(gb, bs, gb, ALU.mult)
                    e.drain()
                    e.tensor_tensor(bo, bx, gb, ALU.add).then_inc(ve.h, 1)
                    e.drain()
                p.emit("v", resid)
                ve.bump(1)
                p.wait("y", ve)
                _dma(p, "y", so, x1_out[b, sl, :], bo)
                so_after_tile[it] = so.n
                if with_ln:
                    if ti == 0 and b > 0:
                        p.wait("v", so)
                    p.wait("s", ve)
                    _emit_ln_tile(p, bo, xnt[:], scr, sc, ve)
                    p.wait("t", ve)
                    def tchain(e):
                        for kt in range(KD):
                            bank = pt[kt // 4]
                            ins = e.transpose(bank[:, (kt % 4) * 128:(kt % 4 + 1) * 128],
                                              xnt[:, kt * 128:(kt + 1) * 128], ident[:])
                            if kt == KD - 1:
                                ins.then_inc(te.h, 1)
                    p.emit("t", tchain)
                    te.bump(1)
                    p.wait("v", te)
                    def evac(e, ti=ti):
                        for g in range(2):
                            for j in range(4):
                                kt = g * 4 + j
                                ins = e.tensor_copy(
                                    xnT_sb[:, kt * S + ti * 128: kt * S + (ti + 1) * 128],
                                    pt[g][:, j * 128:(j + 1) * 128])
                        ins.then_inc(ve.h, 1)
                    p.emit("v", evac)
                    ve.bump(1)
                    p.wait("t", ve)
            if with_ln:
                p.wait("y", ve)
                for kt in range(KD):
                    _dma(p, "y", so, xnT_out[b, kt], xnT_sb[:, kt * S:(kt + 1) * S])
        p.wait("y", so)



def prep_l0ffn_host(d):
    f32 = np.float32
    out = {
        "Wenc": d["L0_ffn_W_enc"].reshape(8, 128, 2048).transpose(1, 0, 2).reshape(128, 16384).copy(),
        "benc": d["L0_ffn_b_enc"].reshape(16, 128).T.copy(),
        "Ws2c": d["L0_ffn_W_s2c"].reshape(16, 128, 64).transpose(1, 0, 2).reshape(128, 1024).copy(),
        "bs2c": d["L0_ffn_b_s2c"].reshape(64, 1).copy(),
        "Wr": d["L0_ffn_Wr"].copy(),
        "br_row": d["L0_ffn_br"].reshape(1, 8).copy(),
        "W1": d["L0_ffn_W1"].transpose(1, 0, 2).reshape(64, 8192).copy(),
        "b1c": d["L0_ffn_b1"].reshape(8, 8, 128).transpose(2, 0, 1).reshape(128, 64).copy(),
        "W2": d["L0_ffn_W2"].reshape(8, 8, 128, 64).transpose(2, 0, 1, 3).reshape(128, 4096).copy(),
        "b2_row": d["L0_ffn_b2"].reshape(1, 512).copy(),
        "Wc2s": d["L0_ffn_W_c2s"].copy(),
        "bc2s": d["L0_ffn_b_c2s"].reshape(16, 128).T.copy(),
        "Wdec": d["L0_ffn_W_dec"].reshape(16, 128, 1024).transpose(1, 0, 2).reshape(128, 16384).copy(),
        "bdec_row": d["L0_ffn_b_dec"].reshape(1, 1024).copy(),
        "lng_row": d["L0_ffn_ln_g"].reshape(1, 1024).copy(),
        "lnb_row": d["L0_ffn_ln_b"].reshape(1, 1024).copy(),
        "ones_col": np.ones((1, 128), f32),
        "ident_in": np.eye(128, dtype=f32),
    }
    return {k: np.ascontiguousarray(v.astype(f32)) for k, v in out.items()}


L0FFN_SHAPES = [("Wenc", [128, 16384]), ("benc", [128, 16]),
                ("Ws2c", [128, 1024]), ("bs2c", [64, 1]),
                ("Wr", [64, 8]), ("br_row", [1, 8]),
                ("W1", [64, 8192]), ("b1c", [128, 64]),
                ("W2", [128, 4096]), ("b2_row", [1, 512]),
                ("Wc2s", [64, 2048]), ("bc2s", [128, 16]),
                ("Wdec", [128, 16384]), ("bdec_row", [1, 1024]),
                ("lng_row", [1, 1024]), ("lnb_row", [1, 1024]),
                ("ones_col", [1, 128]), ("ident_in", [128, 128])]


def emit_l0ffn(cx, din, xnT_in, g_out, pfx="l0f"):
    nc, p, E = cx.nc, cx.p, cx.E
    ld, lg, sc, ve, te = cx.ld, cx.lg, cx.sc, cx.ve, cx.te
    sop = cx.sop
    if True:
        Wenc = E(nc.sbuf_tensor(pfx + "_Wenc", [128, 16384], BF16))
        Ws2c = E(nc.sbuf_tensor(pfx + "_Ws2c", [128, 1024], BF16))
        Wr = E(nc.sbuf_tensor(pfx + "_Wr", [64, 8], BF16))
        W1 = E(nc.sbuf_tensor(pfx + "_W1", [64, 8192], BF16))
        W2 = E(nc.sbuf_tensor(pfx + "_W2", [128, 4096], BF16))
        Wc2s = E(nc.sbuf_tensor(pfx + "_Wc2s", [64, 2048], BF16))
        Wdec = E(nc.sbuf_tensor(pfx + "_Wdec", [128, 16384], BF16))
        benc = E(nc.sbuf_tensor(pfx + "_benc", [128, 16], F32))
        bs2c = E(nc.sbuf_tensor(pfx + "_bs2c", [64, 1], F32))
        br_row = E(nc.sbuf_tensor(pfx + "_br_row", [1, 8], F32))
        b1c = E(nc.sbuf_tensor(pfx + "_b1c", [128, 64], F32))
        b2_row = E(nc.sbuf_tensor(pfx + "_b2_row", [1, 512], F32))
        bc2s = E(nc.sbuf_tensor(pfx + "_bc2s", [128, 16], F32))
        bdec_row = E(nc.sbuf_tensor(pfx + "_bdec_row", [1, 1024], F32))
        lng_row = E(nc.sbuf_tensor(pfx + "_lng_row", [1, 1024], F32))
        lnb_row = E(nc.sbuf_tensor(pfx + "_lnb_row", [1, 1024], F32))
        ones_col = E(nc.sbuf_tensor(pfx + "_ones_col", [1, 128], F32))
        ident = E(nc.sbuf_tensor(pfx + "_ident", [128, 128], F32))
        lng_brd = E(nc.sbuf_tensor(pfx + "_lng_brd", [128, 1024], F32))
        lnb_brd = E(nc.sbuf_tensor(pfx + "_lnb_brd", [128, 1024], F32))
        xnT = E(nc.sbuf_tensor(pfx + "_xnT_sb", [128, KD * 512], BF16))
        enc = E(nc.sbuf_tensor(pfx + "_enc", [128, 16 * 512], BF16))
        cont = E(nc.sbuf_tensor(pfx + "_cont", [128, 512], BF16))
        hfe = E(nc.sbuf_tensor(pfx + "_hfe", [128, 8 * 512], BF16))
        moe = E(nc.sbuf_tensor(pfx + "_moe", [128, 16 * 512], BF16))
        acc_tok = E(nc.sbuf_tensor(pfx + "_acc_tok", [128, 256], F32))
        acc_feat = E(nc.sbuf_tensor(pfx + "_acc_feat", [128, 512], BF16))
        wT = E(nc.sbuf_tensor(pfx + "_wT", [128, 512], F32))
        w_s = E(nc.sbuf_tensor(pfx + "_w_s", [128, 32], F32))
        lgc = E(nc.sbuf_tensor(pfx + "_lgc", [128, 8], F32))
        elg = E(nc.sbuf_tensor(pfx + "_elg", [128, 8], F32))
        inds = E(nc.sbuf_tensor(pfx + "_inds", [128, 8], F32))
        ccc = E(nc.sbuf_tensor(pfx + "_ccc", [128, 16], F32))
        dec_f32 = E(nc.sbuf_tensor(pfx + "_dec_f32", [128, 1024], F32))
        sqb = E(nc.sbuf_tensor(pfx + "_sqb", [128, 1024], F32))
        gout = [E(nc.sbuf_tensor(pfx + f"_gout{i}", [128, 1024], F32)) for i in range(2)]
        pA = E(nc.psum_tensor(pfx + "_pA", [128, 512], F32))
        pB = E(nc.psum_tensor(pfx + "_pB", [128, 512], F32))
        pR = E(nc.psum_tensor(pfx + "_pR", [128, 512], F32))
        pY = E(nc.psum_tensor(pfx + "_pY", [128, 512], F32))
        pD0 = E(nc.psum_tensor(pfx + "_pD0", [128, 512], F32))
        pD1 = E(nc.psum_tensor(pfx + "_pD1", [128, 512], F32))
        pT = E(nc.psum_tensor(pfx + "_pT", [128, 512], F32))

        for name, dst in [("Wenc", Wenc), ("Ws2c", Ws2c), ("Wr", Wr), ("W1", W1),
                          ("W2", W2), ("Wc2s", Wc2s), ("Wdec", Wdec)]:
            _dma(p, "g", lg, dst[:], din[name])
        for name, dst in [("benc", benc), ("bs2c", bs2c), ("br_row", br_row),
                          ("b1c", b1c), ("b2_row", b2_row), ("bc2s", bc2s),
                          ("bdec_row", bdec_row), ("lng_row", lng_row),
                          ("lnb_row", lnb_row), ("ones_col", ones_col),
                          ("ident_in", ident)]:
            _dma(p, "y", ld, dst[:], din[name])
        p.wait("y", ld)
        p.wait("g", lg)
        ld_w = ld.n
        p.wait("t", ld, ld_w)
        p.wait("t", lg)
        for i, (row, dst) in enumerate([(lng_row, lng_brd), (lnb_row, lnb_brd)]):
            for n2 in range(2):
                p.emit("t", lambda e, r=row, n2=n2: e.matmul(
                    pT[:], ones_col[0:1, :], r[0:1, n2 * 512:(n2 + 1) * 512],
                    start=True, stop=True).then_inc(te.h, 1))
                te.bump(1)
                p.wait("v", te)
                p.emit("v", lambda e, d=dst, n2=n2: e.tensor_copy(
                    d[:, n2 * 512:(n2 + 1) * 512], pT[:]).then_inc(ve.h, 1))
                ve.bump(1)
                p.wait("t", ve)

        free_at = {"pA": ("sc", 0), "pB": ("sc", 0), "pD0": ("sc", 0), "pD1": ("sc", 0)}
        sem_of = {"sc": sc, "ve": ve}
        so_mark = {}

        NSLAB = 4
        for b in range(ROWS_PER_CORE):
            for sl in range(NSLAB):
                tok0 = sl * 512
                if b + sl > 0:
                    p.wait("y", te)
                for kt in range(KD):
                    _dma(p, "y", ld, xnT[:, kt * 512:(kt + 1) * 512],
                         xnT_in[b, kt][:, tok0:tok0 + 512])
                p.wait("y", ld)
                p.wait("t", ld)
                for mt in range(16):
                    bank, bname = [(pA, "pA"), (pB, "pB"), (pD0, "pD0"), (pD1, "pD1")][mt % 4]
                    s_, v_ = free_at[bname]
                    p.wait("t", sem_of[s_], v_)
                    def enc_mm(e, mt=mt, bank=bank):
                        for kt in range(KD):
                            ins = e.matmul(bank[:], Wenc[:, kt * 2048 + mt * 128: kt * 2048 + (mt + 1) * 128],
                                           xnT[:, kt * 512:(kt + 1) * 512],
                                           start=(kt == 0), stop=(kt == KD - 1))
                        ins.then_inc(te.h, 1)
                    p.emit("t", enc_mm)
                    te.bump(1)
                    p.wait("s", te)
                    p.emit("s", lambda e, mt=mt, bank=bank: e.activation(
                        enc[:, mt * 512:(mt + 1) * 512], bank[:], AF.Sigmoid,
                        bias=benc[:, mt:mt + 1]).then_inc(sc.h, 1))
                    sc.bump(1)
                    free_at[bname] = ("sc", sc.n)
                p.wait("t", sc)
                p.wait("t", ve)
                def cont_mm(e):
                    for kt in range(16):
                        ins = e.matmul(pR[0:64, 0:512], Ws2c[:, kt * 64:(kt + 1) * 64],
                                       enc[:, kt * 512:(kt + 1) * 512],
                                       start=(kt == 0), stop=(kt == 15))
                    ins.then_inc(te.h, 1)
                p.emit("t", cont_mm)
                te.bump(1)
                p.wait("v", te)
                p.emit("v", lambda e: e.tensor_scalar(cont[0:64, :], pR[0:64, 0:512],
                                                      bs2c[0:64, :], None, ALU.add).then_inc(ve.h, 1))
                ve.bump(1)
                cont_done = ve.n
                p.wait("t", ve, cont_done)
                def rt_mm(e):
                    for tt in range(4):
                        e.matmul(pY[:, tt * 8:(tt + 1) * 8], cont[0:64, tt * 128:(tt + 1) * 128],
                                 Wr[0:64, :], start=True, stop=False)
                        ins = e.matmul(pY[:, tt * 8:(tt + 1) * 8], ones_col[0:1, 0:128],
                                       br_row[0:1, :], start=False, stop=True)
                    ins.then_inc(te.h, 1)
                p.emit("t", rt_mm)
                te.bump(1)
                p.wait("v", te)
                for tt in range(4):
                    def rt_sm1(e, tt=tt):
                        e.tensor_copy(lgc[:], pY[:, tt * 8:(tt + 1) * 8])
                        e.drain()
                        e.reduce_max(ccc[:, 0:1], lgc[:], axis=AX)
                        e.drain()
                        ins = e.tensor_scalar(lgc[:], lgc[:], ccc[:, 0:1], None, ALU.subtract)
                        ins.then_inc(ve.h, 1)
                        e.drain()
                    p.emit("v", rt_sm1)
                    ve.bump(1)
                    p.wait("s", ve)
                    p.emit("s", lambda e: e.activation(elg[:], lgc[:], AF.Exp).then_inc(sc.h, 1))
                    sc.bump(1)
                    p.wait("v", sc)
                    def rt_sm2(e, tt=tt):
                        e.reduce_sum(ccc[:, 1:2], elg[:], axis=AX)
                        e.drain()
                        e.reciprocal(ccc[:, 2:3], ccc[:, 1:2])
                        e.drain()
                        e.tensor_scalar(lgc[:], elg[:], ccc[:, 2:3], None, ALU.mult)
                        e.drain()
                        e.reduce_max(ccc[:, 3:4], lgc[:], axis=AX)
                        e.drain()
                        e.tensor_scalar(inds[:], lgc[:], ccc[:, 3:4], None, ALU.is_ge)
                        e.drain()
                        e.scalar_tensor_tensor(elg[:], inds[:], -1e9, lgc[:], ALU.mult, ALU.add)
                        e.drain()
                        e.reduce_max(ccc[:, 4:5], elg[:], axis=AX)
                        e.drain()
                        e.tensor_scalar(inds[:], lgc[:], ccc[:, 4:5], None, ALU.is_ge)
                        e.tensor_tensor(ccc[:, 5:6], ccc[:, 3:4], ccc[:, 4:5], ALU.add)
                        e.drain()
                        e.reciprocal(ccc[:, 6:7], ccc[:, 5:6])
                        e.drain()
                        ins = e.scalar_tensor_tensor(w_s[:, tt * 8:(tt + 1) * 8], lgc[:],
                                                     ccc[:, 6:7], inds[:], ALU.mult, ALU.mult)
                        ins.then_inc(ve.h, 1)
                        e.drain()
                    p.emit("v", rt_sm2)
                    ve.bump(1)
                w_done = ve.n
                p.wait("t", ve, w_done)
                p.wait("t", sc)
                def wt_mm(e):
                    for tt in range(4):
                        ins = e.transpose(pT[0:8, tt * 128:(tt + 1) * 128],
                                          w_s[:, tt * 8:(tt + 1) * 8], ident[:])
                    ins.then_inc(te.h, 1)
                p.emit("t", wt_mm)
                te.bump(1)
                p.wait("s", te)
                p.emit("s", lambda e: e.activation(wT[0:8, :], pT[0:8, :],
                                                   AF.Copy).then_inc(sc.h, 1))
                sc.bump(1)
                wT_done = sc.n
                accv = 0
                for ex in range(8):
                    for mt in range(8):
                        bank, bname = [(pA, "pA"), (pB, "pB"), (pD0, "pD0"), (pD1, "pD1")][mt % 4]
                        s_, v_ = free_at[bname]
                        p.wait("t", sem_of[s_], v_)
                        p.emit("t", lambda e, ex=ex, mt=mt, bank=bank: e.matmul(
                            bank[:], W1[0:64, ex * 1024 + mt * 128: ex * 1024 + (mt + 1) * 128],
                            cont[0:64, :], start=True, stop=True).then_inc(te.h, 1))
                        te.bump(1)
                        p.wait("s", te)
                        p.emit("s", lambda e, ex=ex, mt=mt, bank=bank: e.activation(
                            hfe[:, mt * 512:(mt + 1) * 512], bank[:], AF.Relu,
                            bias=b1c[:, ex * 8 + mt: ex * 8 + mt + 1]).then_inc(sc.h, 1))
                        sc.bump(1)
                        free_at[bname] = ("sc", sc.n)
                    h_done = sc.n
                    p.wait("t", sc, max(h_done, wT_done))
                    if ex > 0:
                        p.wait("t", ve, accv)
                    def y_mm(e, ex=ex):
                        for tt in range(4):
                            for kt in range(8):
                                e.matmul(pY[:, tt * 64:(tt + 1) * 64],
                                         hfe[:, kt * 512 + tt * 128: kt * 512 + (tt + 1) * 128],
                                         W2[:, (ex * 8 + kt) * 64:(ex * 8 + kt + 1) * 64],
                                         start=(kt == 0), stop=False)
                            ins = e.matmul(pY[:, tt * 64:(tt + 1) * 64], ones_col[0:1, 0:128],
                                           b2_row[0:1, ex * 64:(ex + 1) * 64], start=False, stop=True)
                        ins.then_inc(te.h, 1)
                    p.emit("t", y_mm)
                    te.bump(1)
                    p.wait("v", te)
                    def comb(e, ex=ex):
                        for tt in range(4):
                            if ex == 0:
                                ins = e.tensor_scalar(acc_tok[:, tt * 64:(tt + 1) * 64],
                                                      pY[:, tt * 64:(tt + 1) * 64],
                                                      w_s[:, tt * 8 + ex: tt * 8 + ex + 1],
                                                      None, ALU.mult)
                            else:
                                ins = e.scalar_tensor_tensor(acc_tok[:, tt * 64:(tt + 1) * 64],
                                                             pY[:, tt * 64:(tt + 1) * 64],
                                                             w_s[:, tt * 8 + ex: tt * 8 + ex + 1],
                                                             acc_tok[:, tt * 64:(tt + 1) * 64],
                                                             ALU.mult, ALU.add)
                        ins.then_inc(ve.h, 1)
                        e.drain()
                    p.emit("v", comb)
                    ve.bump(1)
                    accv = ve.n
                p.wait("t", ve, accv)
                p.wait("t", sc, wT_done)
                def at_mm(e):
                    for tt in range(4):
                        ins = e.transpose(pT[0:64, tt * 128:(tt + 1) * 128],
                                          acc_tok[:, tt * 64:(tt + 1) * 64], ident[:])
                    ins.then_inc(te.h, 1)
                p.emit("t", at_mm)
                te.bump(1)
                p.wait("s", te)
                p.emit("s", lambda e: e.activation(acc_feat[0:64, :], pT[0:64, :],
                                                   AF.Copy).then_inc(sc.h, 1))
                sc.bump(1)
                p.wait("t", sc)
                for mt in range(16):
                    bank, bname = [(pA, "pA"), (pB, "pB"), (pD0, "pD0"), (pD1, "pD1")][mt % 4]
                    s_, v_ = free_at[bname]
                    p.wait("t", sem_of[s_], v_)
                    p.emit("t", lambda e, mt=mt, bank=bank: e.matmul(
                        bank[:], Wc2s[0:64, mt * 128:(mt + 1) * 128],
                        acc_feat[0:64, :], start=True, stop=True).then_inc(te.h, 1))
                    te.bump(1)
                    p.wait("s", te)
                    p.emit("s", lambda e, mt=mt, bank=bank: e.activation(
                        moe[:, mt * 512:(mt + 1) * 512], bank[:], AF.Sigmoid,
                        bias=bc2s[:, mt:mt + 1]).then_inc(sc.h, 1))
                    sc.bump(1)
                    free_at[bname] = ("sc", sc.n)
                moe_done = sc.n
                for tt in range(4):
                    it = (b * NSLAB + sl) * 4 + tt
                    gb = gout[it % 2][:]
                    p.wait("t", sc, moe_done)
                    for bname in ("pD0", "pD1"):
                        s_, v_ = free_at[bname]
                        p.wait("t", sem_of[s_], v_)
                    if it >= 1:
                        p.wait("t", ve, dec_free)
                    def dec_mm(e, tt=tt):
                        for n2 in range(2):
                            pD = pD0 if n2 == 0 else pD1
                            for kt in range(16):
                                e.matmul(pD[:], moe[:, kt * 512 + tt * 128: kt * 512 + (tt + 1) * 128],
                                         Wdec[:, kt * 1024 + n2 * 512: kt * 1024 + (n2 + 1) * 512],
                                         start=(kt == 0), stop=False)
                            ins = e.matmul(pD[:], ones_col[0:1, 0:128],
                                           bdec_row[0:1, n2 * 512:(n2 + 1) * 512],
                                           start=False, stop=True)
                        ins.then_inc(te.h, 1)
                    p.emit("t", dec_mm)
                    te.bump(1)
                    p.wait("v", te)
                    def dcopy(e):
                        e.tensor_copy(dec_f32[:, 0:512], pD0[:])
                        ins = e.tensor_copy(dec_f32[:, 512:1024], pD1[:])
                        ins.then_inc(ve.h, 1)
                        e.drain()
                    p.emit("v", dcopy)
                    ve.bump(1)
                    dec_free = ve.n
                    free_at["pD0"] = ("ve", ve.n)
                    free_at["pD1"] = ("ve", ve.n)
                    p.wait("s", ve, dec_free)
                    p.emit("s", lambda e: e.activation(sqb[:], dec_f32[:], AF.Square).then_inc(sc.h, 1))
                    sc.bump(1)
                    p.wait("v", sc)
                    cc = ccc
                    def ln_v1(e):
                        e.reduce_sum(cc[:, 8:9], dec_f32[:], axis=AX)
                        e.reduce_sum(cc[:, 9:10], sqb[:], axis=AX)
                        e.drain()
                        e.tensor_scalar(cc[:, 8:9], cc[:, 8:9], 1.0 / D, None, ALU.mult)
                        e.tensor_scalar(cc[:, 9:10], cc[:, 9:10], 1.0 / D, None, ALU.mult)
                        e.drain()
                        e.tensor_tensor(cc[:, 10:11], cc[:, 8:9], cc[:, 8:9], ALU.mult)
                        e.drain()
                        e.tensor_tensor(cc[:, 9:10], cc[:, 9:10], cc[:, 10:11], ALU.subtract)
                        e.drain()
                        ins = e.tensor_scalar(cc[:, 9:10], cc[:, 9:10], 1e-5, None, ALU.add)
                        ins.then_inc(ve.h, 1)
                        e.drain()
                    p.emit("v", ln_v1)
                    ve.bump(1)
                    p.wait("s", ve)
                    p.emit("s", lambda e: e.activation(cc[:, 11:12], cc[:, 9:10], AF.Sqrt).then_inc(sc.h, 1))
                    sc.bump(1)
                    p.wait("v", sc)
                    if it >= 2:
                        p.wait("v", sop[it % 2], so_mark[it - 2])
                    def ln_v2(e, gb=gb):
                        e.reciprocal(cc[:, 12:13], cc[:, 11:12])
                        e.drain()
                        e.tensor_scalar(gb, dec_f32[:], cc[:, 8:9], cc[:, 12:13],
                                        ALU.subtract, ALU.mult)
                        e.drain()
                        e.tensor_tensor(gb, gb, lng_brd[:], ALU.mult)
                        e.drain()
                        ins = e.tensor_tensor(gb, gb, lnb_brd[:], ALU.add)
                        ins.then_inc(ve.h, 1)
                        e.drain()
                    p.emit("v", ln_v2)
                    ve.bump(1)
                    p.wait("y", ve)
                    ti_glob = sl * 4 + tt
                    _dma(p, "y", sop[it % 2], g_out[b, ti_glob * 128:(ti_glob + 1) * 128, :], gb)
                    so_mark[it] = sop[it % 2].n
        p.wait("y", sop[0])
        p.wait("y", sop[1])



def _emit_top2(p, psrc, wdst, lgc, elg, inds, ccc, sc, ve):
    def sm1(e):
        e.tensor_copy(lgc, psrc)
        e.drain()
        e.reduce_max(ccc[:, 0:1], lgc, axis=AX)
        e.drain()
        ins = e.tensor_scalar(lgc, lgc, ccc[:, 0:1], None, ALU.subtract)
        ins.then_inc(ve.h, 1)
        e.drain()
    p.emit("v", sm1)
    ve.bump(1)
    p.wait("s", ve)
    p.emit("s", lambda e: e.activation(elg, lgc, AF.Exp).then_inc(sc.h, 1))
    sc.bump(1)
    p.wait("v", sc)
    def sm2(e):
        e.reduce_sum(ccc[:, 1:2], elg, axis=AX)
        e.drain()
        e.reciprocal(ccc[:, 2:3], ccc[:, 1:2])
        e.drain()
        e.tensor_scalar(lgc, elg, ccc[:, 2:3], None, ALU.mult)
        e.drain()
        e.reduce_max(ccc[:, 3:4], lgc, axis=AX)
        e.drain()
        e.tensor_scalar(inds, lgc, ccc[:, 3:4], None, ALU.is_ge)
        e.drain()
        e.scalar_tensor_tensor(elg, inds, -1e9, lgc, ALU.mult, ALU.add)
        e.drain()
        e.reduce_max(ccc[:, 4:5], elg, axis=AX)
        e.drain()
        e.tensor_scalar(inds, lgc, ccc[:, 4:5], None, ALU.is_ge)
        e.tensor_tensor(ccc[:, 5:6], ccc[:, 3:4], ccc[:, 4:5], ALU.add)
        e.drain()
        e.reciprocal(ccc[:, 6:7], ccc[:, 5:6])
        e.drain()
        ins = e.scalar_tensor_tensor(wdst, lgc, ccc[:, 6:7], inds, ALU.mult, ALU.mult)
        ins.then_inc(ve.h, 1)
        e.drain()
    p.emit("v", sm2)
    ve.bump(1)


def _emit_top2_batch(p, psrc, wdst, ind_dst, scr, sc, ve, ntt):
    W = ntt * 8
    lg = scr[:, 0:W]
    eg = scr[:, W:2 * W]
    cols = scr[:, 2 * W:]
    m0 = cols[:, 0:ntt]
    m1 = cols[:, ntt:2 * ntt]
    m2 = cols[:, 2 * ntt:3 * ntt]
    den = cols[:, 3 * ntt:4 * ntt]
    def seg(ap, c=8):
        return ap.rearrange("p (h c) -> p h c", c=c)
    def bc(ap):
        return ap.rearrange("p (h c) -> p h c", c=1).broadcast_to([128, ntt, 8])
    def sm1(e):
        e.tensor_copy(lg, psrc)
        e.drain()
        e.reduce_max(seg(m0, 1), seg(lg), axis=AX)
        e.drain()
        ins = e.tensor_tensor(seg(lg), seg(lg), bc(m0), ALU.subtract)
        ins.then_inc(ve.h, 1)
        e.drain()
    p.emit("v", sm1)
    ve.bump(1)
    p.wait("s", ve)
    p.emit("s", lambda e: e.activation(eg, lg, AF.Exp).then_inc(sc.h, 1))
    sc.bump(1)
    p.wait("v", sc)
    def sm2(e):
        e.reduce_sum(seg(m0, 1), seg(eg), axis=AX)
        e.drain()
        e.reciprocal(m0, m0)
        e.drain()
        e.tensor_tensor(seg(lg), seg(eg), bc(m0), ALU.mult)
        e.drain()
        e.reduce_max(seg(m1, 1), seg(lg), axis=AX)
        e.drain()
        e.tensor_tensor(seg(eg), seg(lg), bc(m1), ALU.is_ge)
        e.drain()
        e.scalar_tensor_tensor(eg, eg, -1e9, lg, ALU.mult, ALU.add)
        e.drain()
        e.reduce_max(seg(m2, 1), seg(eg), axis=AX)
        e.drain()
        e.tensor_tensor(seg(eg), seg(lg), bc(m2), ALU.is_ge)
        e.tensor_tensor(den, m1, m2, ALU.add)
        e.drain()
        e.reciprocal(den, den)
        e.drain()
        e.tensor_tensor(seg(lg), seg(lg), bc(den), ALU.mult)
        e.drain()
        ins = e.tensor_tensor(wdst, lg, eg, ALU.mult)
        ins.then_inc(ve.h, 1)
        e.drain()
        if ind_dst is not None:
            ins = e.tensor_copy(ind_dst, eg)
            ins.then_inc(ve.h, 1)
            e.drain()
    p.emit("v", sm2)
    ve.bump(1 + (1 if ind_dst is not None else 0))


def prep_l1ffn_host(d):
    f32 = np.float32
    out = {
        "gateW": d["L1_ffn_gate_w"].reshape(8, 128, 8).transpose(1, 0, 2).reshape(128, 64).copy(),
        "Wgu": d["L1_ffn_gate_up_w"].reshape(8, 8, 128, 32, 128).transpose(0, 2, 3, 1, 4).reshape(8 * 128, 32768).copy(),
        "Wdn": d["L1_ffn_down_w"].reshape(8, 16, 128, 1024).transpose(0, 2, 1, 3).reshape(8 * 128, 16384).copy(),
    }
    return {k: np.ascontiguousarray(v.astype(f32)) for k, v in out.items()}


def emit_l1ffn(cx, gw_in, wgu_in, wdn_in, xnT_in, g_out, pfx="l1f"):
    nc, p, E = cx.nc, cx.p, cx.E
    ld, lg, sc, ve, te = cx.ld, cx.lg, cx.sc, cx.ve, cx.te
    sop = cx.sop
    if True:
        xnT = E(nc.sbuf_tensor(pfx + "_xnT_sb", [128, KD * S], BF16))
        gateW = E(nc.sbuf_tensor(pfx + "_gateW", [128, 64], BF16))
        Wgu = E(nc.sbuf_tensor(pfx + "_Wgu", [128, 32768], BF16))
        Wdn = E(nc.sbuf_tensor(pfx + "_Wdn", [128, 16384], BF16))
        hid = E(nc.sbuf_tensor(pfx + "_hid", [128, 16 * 512], BF16))
        tmp = E(nc.sbuf_tensor(pfx + "_tmpf", [128, 512], F32))
        tmp2 = E(nc.sbuf_tensor(pfx + "_tmpf2", [128, 512], F32))
        w_s = E(nc.sbuf_tensor(pfx + "_w_s", [128, 128], F32))
        lgc = E(nc.sbuf_tensor(pfx + "_lgc", [128, 8], F32))
        elg = E(nc.sbuf_tensor(pfx + "_elg", [128, 8], F32))
        inds = E(nc.sbuf_tensor(pfx + "_inds", [128, 8], F32))
        ccc = E(nc.sbuf_tensor(pfx + "_ccc", [128, 16], F32))
        gout = [E(nc.sbuf_tensor(pfx + f"_gout{i}", [128, 1024], F32)) for i in range(2)]
        pA = E(nc.psum_tensor(pfx + "_pA", [128, 512], F32))
        pB = E(nc.psum_tensor(pfx + "_pB", [128, 512], F32))
        pR = E(nc.psum_tensor(pfx + "_pR", [128, 512], F32))
        pD0 = E(nc.psum_tensor(pfx + "_pD0", [128, 512], F32))
        pD1 = E(nc.psum_tensor(pfx + "_pD1", [128, 512], F32))

        _dma(p, "g", lg, gateW[:], gw_in)
        p.wait("g", lg)
        gw_done = lg.n

        so_mark = {}
        NSLAB = 4
        for b in range(ROWS_PER_CORE):
            if b > 0:
                p.wait("y", te)
            for kt in range(KD):
                _dma(p, "y", ld, xnT[:, kt * S:(kt + 1) * S], xnT_in[b, kt])
            p.wait("y", ld)
            xn_done = ld.n
            p.wait("t", ld, xn_done)
            p.wait("t", lg, gw_done)
            if b > 0:
                p.wait("t", ve)
            def rt_mm(e):
                for tt in range(NT):
                    for kt in range(KD):
                        ins = e.matmul(pR[:, tt * 8:(tt + 1) * 8],
                                       xnT[:, kt * S + tt * 128: kt * S + (tt + 1) * 128],
                                       gateW[:, kt * 8:(kt + 1) * 8],
                                       start=(kt == 0), stop=(kt == KD - 1))
                ins.then_inc(te.h, 1)
            p.emit("t", rt_mm)
            te.bump(1)
            p.wait("v", te)
            _emit_top2_batch(p, pR[:, 0:128], w_s[:], None, tmp2[:], sc, ve, NT)
            w_done = ve.n
            pass
            for ex in range(8):
                p.wait("g", te)
                wgu_marks = []
                for ch in range(4):
                    _dma(p, "g", lg, Wgu[:, ch * 8192:(ch + 1) * 8192],
                         wgu_in[ex * 128:(ex + 1) * 128, ch * 8192:(ch + 1) * 8192])
                    p.wait("g", lg)
                    wgu_marks.append(lg.n)
                _dma(p, "g", lg, Wdn[:], wdn_in[ex * 128:(ex + 1) * 128, :])
                p.wait("g", lg)
                w_ready = lg.n
                gsc_mark = {}
                for sl in range(NSLAB):
                    tok0 = sl * 512
                    silu_mark = {}
                    hmul_mark = {}
                    for fm in range(16):
                        bg = pA if fm % 2 == 0 else pD0
                        bu = pB if fm % 2 == 0 else pD1
                        if fm >= 2:
                            p.wait("t", sc, silu_mark[fm - 2])
                            p.wait("t", ve, hmul_mark[fm - 2])
                        else:
                            p.wait("t", ve)
                        p.wait("t", lg, wgu_marks[fm // 4])
                        def gu_mm(e, fm=fm, tok0=tok0, bg=bg, bu=bu):
                            base = fm * 2048
                            for kt in range(KD):
                                e.matmul(bg[:], Wgu[:, base + kt * 128: base + (kt + 1) * 128],
                                         xnT[:, kt * S + tok0: kt * S + tok0 + 512],
                                         start=(kt == 0), stop=(kt == KD - 1))
                            for kt in range(KD):
                                ins = e.matmul(bu[:], Wgu[:, base + 1024 + kt * 128: base + 1024 + (kt + 1) * 128],
                                               xnT[:, kt * S + tok0: kt * S + tok0 + 512],
                                               start=(kt == 0), stop=(kt == KD - 1))
                            ins.then_inc(te.h, 1)
                        p.emit("t", gu_mm)
                        te.bump(1)
                        p.wait("s", te)
                        tb = tmp[:, 0:512] if fm % 2 == 0 else tmp2[:, 0:512]
                        p.emit("s", lambda e, bg=bg, tb=tb: e.activation(tb, bg[:], AF.Silu).then_inc(sc.h, 1))
                        sc.bump(1)
                        silu_mark[fm] = sc.n
                        p.wait("v", sc)
                        def hmul(e, fm=fm, bu=bu, tb=tb):
                            ins = e.tensor_tensor(hid[:, fm * 512:(fm + 1) * 512], tb, bu[:], ALU.mult)
                            ins.then_inc(ve.h, 1)
                        p.emit("v", hmul)
                        ve.bump(1)
                        hmul_mark[fm] = ve.n
                    hid_done = ve.n
                    for tt in range(4):
                        it = ((b * 8 + ex) * NSLAB + sl) * 4 + tt
                        gb = gout[it % 2][:]
                        ti_glob = sl * 4 + tt
                        bd0 = pD0 if tt % 2 == 0 else pA
                        bd1 = pD1 if tt % 2 == 0 else pB
                        p.wait("t", ve, hid_done)
                        p.wait("t", lg, w_ready)
                        if tt >= 2:
                            p.wait("t", ve, gsc_mark[tt - 2])
                        else:
                            p.wait("t", sc, silu_mark[15])
                        def dn_mm(e, tt=tt, bd0=bd0, bd1=bd1):
                            for n2 in range(2):
                                pD = bd0 if n2 == 0 else bd1
                                for kt in range(16):
                                    ins = e.matmul(pD[:], hid[:, kt * 512 + tt * 128: kt * 512 + (tt + 1) * 128],
                                                   Wdn[:, kt * 1024 + n2 * 512: kt * 1024 + (n2 + 1) * 512],
                                                   start=(kt == 0), stop=(kt == 15))
                            ins.then_inc(te.h, 1)
                        p.emit("t", dn_mm)
                        te.bump(1)
                        if it >= 2:
                            p.wait("v", sop[it % 2], so_mark[it - 2])
                        p.wait("v", te)
                        def gsc(e, gb=gb, ti_glob=ti_glob, ex=ex, bd0=bd0, bd1=bd1):
                            e.tensor_scalar(gb[:, 0:512], bd0[:],
                                            w_s[:, ti_glob * 8 + ex: ti_glob * 8 + ex + 1],
                                            None, ALU.mult)
                            ins = e.tensor_scalar(gb[:, 512:1024], bd1[:],
                                                  w_s[:, ti_glob * 8 + ex: ti_glob * 8 + ex + 1],
                                                  None, ALU.mult)
                            ins.then_inc(ve.h, 1)
                            e.drain()
                        p.emit("v", gsc)
                        ve.bump(1)
                        gsc_mark[tt] = ve.n
                        p.wait("g", ve)
                        sem = sop[it % 2]
                        if ex == 0:
                            p.emit("g", lambda e, gb=gb, b=b, t0=ti_glob * 128, s=sem.h:
                                   e.dma_start(out=g_out[b, t0:t0 + 128, :], in_=gb).then_inc(s, 16))
                        else:
                            p.emit("g", lambda e, gb=gb, b=b, t0=ti_glob * 128, s=sem.h:
                                   e.dma_start(out=g_out[b, t0:t0 + 128, :], in_=gb,
                                               accum_op=ALU.add).then_inc(s, 16))
                        sem.bump(16)
                        so_mark[it] = sem.n
        p.wait("y", sop[0])
        p.wait("y", sop[1])



def prep_l1attn_host(d):
    f32 = np.float32
    def wlay(w):
        return w.transpose(1, 0, 2).reshape(1024, 1024).reshape(8, 128, 1024).transpose(1, 0, 2).reshape(128, 8192)
    p_ = np.arange(128)[:, None]
    f_ = np.arange(512)[None, :]
    masks = np.concatenate([(f_ >= p_ + o * 128).astype(f32) for o in range(4)], axis=1)
    out = {
        "WqT": wlay(d["L1_attn_Wq"]),
        "WkT": wlay(d["L1_attn_Wk"]),
        "WvT": wlay(d["L1_attn_Wv"]),
        "WoT": d["L1_attn_Wo"].T.reshape(8, 128, 1024).transpose(1, 0, 2).reshape(128, 8192).copy(),
        "bqc": d["L1_attn_bq"].reshape(1024).reshape(8, 128).T.copy(),
        "bkc": d["L1_attn_bk"].reshape(1024).reshape(8, 128).T.copy(),
        "bv_row": d["L1_attn_bv"].reshape(1, 1024).copy(),
        "bo_row": d["L1_attn_bo"].reshape(1, 1024).copy(),
        "masks": masks,
        "ones_col": np.ones((1, 128), f32),
    }
    return {k: np.ascontiguousarray(v.astype(f32)) for k, v in out.items()}


L1ATTN_SHAPES = [("WqT", [128, 8192]), ("WkT", [128, 8192]),
                 ("WvT", [128, 8192]), ("WoT", [128, 8192]),
                 ("bqc", [128, 8]), ("bkc", [128, 8]),
                 ("bv_row", [1, 1024]), ("bo_row", [1, 1024]),
                 ("masks", [128, 2048]), ("ones_col", [1, 128])]


def emit_l1attn(cx, din, xnT_in, s_out, pfx="l1a"):
    nc, p, E = cx.nc, cx.p, cx.E
    ld, lg, sc, ve, te = cx.ld, cx.lg, cx.sc, cx.ve, cx.te
    sop = cx.sop
    if True:
        QT = E(nc.sbuf_tensor(pfx + "_QT", [128, 8 * S], BF16))
        KT = E(nc.sbuf_tensor(pfx + "_KT", [128, 8 * S], BF16))
        V65 = E(nc.sbuf_tensor(pfx + "_V65", [128, 16 * 1040], BF16))
        xctx = E(nc.sbuf_tensor(pfx + "_xctx", [128, 8 * S], BF16))
        Wst = E(nc.sbuf_tensor(pfx + "_Wst", [128, 8192], BF16))
        masks = E(nc.sbuf_tensor(pfx + "_masks", [128, 2048], F32))
        EST = [E(nc.sbuf_tensor(pfx + f"_EST{i}", [128, 512], BF16)) for i in range(2)]
        bqc = E(nc.sbuf_tensor(pfx + "_bqc", [128, 8], F32))
        bkc = E(nc.sbuf_tensor(pfx + "_bkc", [128, 8], F32))
        bv_row = E(nc.sbuf_tensor(pfx + "_bv_row", [1, 1024], F32))
        bo_row = E(nc.sbuf_tensor(pfx + "_bo_row", [1, 1024], F32))
        ones_col = E(nc.sbuf_tensor(pfx + "_ones_col", [1, 128], F32))
        dinv = E(nc.sbuf_tensor(pfx + "_dinv", [1, 512], F32))
        ctmp = E(nc.sbuf_tensor(pfx + "_ctmp", [128, 512], F32))
        s_stage = [E(nc.sbuf_tensor(pfx + f"_sst{i}", [128, D], F32)) for i in range(2)]
        pP0 = E(nc.psum_tensor(pfx + "_pP0", [128, 512], F32))
        pP1 = E(nc.psum_tensor(pfx + "_pP1", [128, 512], F32))
        pST0 = E(nc.psum_tensor(pfx + "_pST0", [128, 512], F32))
        pST1 = E(nc.psum_tensor(pfx + "_pST1", [128, 512], F32))
        pCTX = E(nc.psum_tensor(pfx + "_pCTX", [128, 512], F32))
        pBRD = E(nc.psum_tensor(pfx + "_pBRD", [128, 512], F32))

        for name, dst in [("bqc", bqc), ("bkc", bkc), ("bv_row", bv_row),
                          ("bo_row", bo_row), ("ones_col", ones_col),
                          ("masks", masks)]:
            _dma(p, "y", ld, dst[:], din[name])
        p.wait("y", ld)
        consts_done = ld.n

        so_mark = {}
        sti = 0
        for b in range(ROWS_PER_CORE):
            if b > 0:
                p.wait("y", ve)
            for kt in range(KD):
                _dma(p, "y", ld, xctx[:, kt * S:(kt + 1) * S], xnT_in[b, kt])
            p.wait("y", ld)
            xn_done = ld.n
            p.wait("t", sc)
            proj_done = {}
            for wi, (wname, bias_mode) in enumerate([("WqT", "q"), ("WkT", "k"), ("WvT", "v")]):
                p.wait("g", te)
                _dma(p, "g", lg, Wst[:], din[wname])
                p.wait("g", lg)
                w_rdy = lg.n
                p.wait("t", lg, w_rdy)
                p.wait("t", ld, xn_done)
                p.wait("t", ld, consts_done)
                if bias_mode in ("q", "k"):
                    dst = QT if bias_mode == "q" else KT
                    bcol = bqc if bias_mode == "q" else bkc
                    for mt in range(8):
                        for nt in range(4):
                            bank = pP0 if (mt * 4 + nt) % 2 == 0 else pP1
                            p.wait("t", ve)
                            def qk_mm(e, mt=mt, nt=nt, bank=bank):
                                for kt in range(KD):
                                    ins = e.matmul(bank[:], Wst[:, kt * 1024 + mt * 128:kt * 1024 + (mt + 1) * 128],
                                                   xctx[:, kt * S + nt * 512:kt * S + (nt + 1) * 512],
                                                   start=(kt == 0), stop=(kt == KD - 1))
                                ins.then_inc(te.h, 1)
                            p.emit("t", qk_mm)
                            te.bump(1)
                            p.wait("v", te)
                            def qk_ev(e, mt=mt, nt=nt, bank=bank, dst=dst, bcol=bcol):
                                ins = e.tensor_scalar(dst[:, mt * S + nt * 512:mt * S + (nt + 1) * 512],
                                                      bank[:], bcol[:, mt:mt + 1], None, ALU.add)
                                ins.then_inc(ve.h, 1)
                            p.emit("v", qk_ev)
                            ve.bump(1)
                else:
                    for ti in range(NT):
                        for n2 in range(2):
                            bank = pP0 if (ti * 2 + n2) % 2 == 0 else pP1
                            p.wait("t", ve)
                            def v_mm(e, ti=ti, n2=n2, bank=bank):
                                for kt in range(KD):
                                    e.matmul(bank[:], xctx[:, kt * S + ti * 128:kt * S + (ti + 1) * 128],
                                             Wst[:, kt * 1024 + n2 * 512:kt * 1024 + (n2 + 1) * 512],
                                             start=(kt == 0), stop=False)
                                ins = e.matmul(bank[:], ones_col[0:1, 0:128],
                                               bv_row[0:1, n2 * 512:(n2 + 1) * 512],
                                               start=False, stop=True)
                                ins.then_inc(te.h, 1)
                            p.emit("t", v_mm)
                            te.bump(1)
                            p.wait("v", te)
                            def v_ev(e, ti=ti, n2=n2, bank=bank):
                                dstv = V65[:, ti * 1040:(ti + 1) * 1040].rearrange(
                                    "p (h c) -> p h c", c=65)[:, n2 * 8:(n2 + 1) * 8, 0:64]
                                srcv = bank[:].rearrange("p (h c) -> p h c", c=64)
                                if n2 == 0:
                                    onev = V65[:, ti * 1040:(ti + 1) * 1040].rearrange(
                                        "p (h c) -> p h c", c=65)[:, :, 64:65]
                                    e.memset(onev, 1.0)
                                ins = e.tensor_copy(dstv, srcv)
                                ins.then_inc(ve.h, 1)
                            p.emit("v", v_ev)
                            ve.bump(1)
            qkv_done_te = te.n
            qkv_done_ve = ve.n
            for h in range(16):
                kt_half = h // 2
                poff = (h % 2) * 64
                for qb in range(4):
                    nkt = qb * 4 + 4
                    est_mark = {}
                    for kt in range(nkt):
                        stb = pST0 if kt % 2 == 0 else pST1
                        eb = EST[kt % 2][:]
                        if kt >= 2:
                            sm, vm = est_mark[kt - 2]
                            p.wait("t", sc, sm)
                            if vm is not None:
                                p.wait("t", ve, vm)
                        elif kt == 0:
                            p.wait("t", sc)
                            p.wait("t", ve)
                        def st_mm(e, h=h, qb=qb, kt=kt, stb=stb, kt_half=kt_half, poff=poff):
                            ins = e.matmul(stb[:], KT[poff:poff + 64, kt_half * S + kt * 128:kt_half * S + (kt + 1) * 128],
                                           QT[poff:poff + 64, kt_half * S + qb * 512:kt_half * S + (qb + 1) * 512],
                                           start=True, stop=True)
                            ins.then_inc(te.h, 1)
                        p.emit("t", st_mm)
                        te.bump(1)
                        p.wait("s", te)
                        p.emit("s", lambda e, eb=eb, stb=stb: e.activation(
                            eb, stb[:], AF.Exp, scale=0.125).then_inc(sc.h, 1))
                        sc.bump(1)
                        diag_o = kt - qb * 4
                        if diag_o >= 0:
                            p.wait("v", sc)
                            def msk(e, eb=eb, diag_o=diag_o):
                                ins = e.tensor_tensor(eb, eb, masks[:, diag_o * 512:(diag_o + 1) * 512], ALU.mult)
                                ins.then_inc(ve.h, 1)
                                e.drain()
                            p.emit("v", msk)
                            ve.bump(1)
                            est_mark[kt] = (sc.n, ve.n)
                            p.wait("t", ve)
                        else:
                            est_mark[kt] = (sc.n, None)
                            p.wait("t", sc)
                        def cx_mm(e, h=h, kt=kt, eb=eb, first=(kt == 0), last=(kt == nkt - 1)):
                            ins = e.matmul(pCTX[0:65, :], V65[:, kt * 1040 + h * 65:kt * 1040 + h * 65 + 65],
                                           eb, start=first, stop=last)
                            if last:
                                ins.then_inc(te.h, 1)
                        p.emit("t", cx_mm)
                        if kt == nkt - 1:
                            te.bump(1)
                    p.wait("v", te)
                    def dv1(e):
                        ins = e.reciprocal(dinv[0:1, :], pCTX[64:65, :])
                        ins.then_inc(ve.h, 1)
                        e.drain()
                    p.emit("v", dv1)
                    ve.bump(1)
                    p.wait("t", ve)
                    def brd_mm(e):
                        ins = e.matmul(pBRD[0:64, :], ones_col[0:1, 0:64], dinv[0:1, :],
                                       start=True, stop=True)
                        ins.then_inc(te.h, 1)
                    p.emit("t", brd_mm)
                    te.bump(1)
                    p.wait("s", te)
                    p.emit("s", lambda e: e.activation(ctmp[0:64, :], pCTX[0:64, :],
                                                       AF.Copy).then_inc(sc.h, 1))
                    sc.bump(1)
                    p.wait("v", sc)
                    def cdiv(e, h=h, qb=qb, kt_half=kt_half, poff=poff):
                        ins = e.tensor_tensor(
                            xctx[poff:poff + 64, kt_half * S + qb * 512:kt_half * S + (qb + 1) * 512],
                            ctmp[0:64, :], pBRD[0:64, :], ALU.mult)
                        ins.then_inc(ve.h, 1)
                        e.drain()
                    p.emit("v", cdiv)
                    ve.bump(1)
                    p.wait("t", ve)
            ctx_done_ve = ve.n
            p.wait("g", te)
            _dma(p, "g", lg, Wst[:], din["WoT"])
            p.wait("g", lg)
            wo_rdy = lg.n
            p.wait("t", lg, wo_rdy)
            p.wait("t", ve, ctx_done_ve)
            p.wait("t", sc)
            for ti in range(NT):
                sb = s_stage[ti % 2][:]
                for n2 in range(2):
                    bank = pP0 if n2 == 0 else pP1
                    p.wait("t", ve)
                    def wo_mm(e, ti=ti, n2=n2, bank=bank):
                        for kt in range(KD):
                            e.matmul(bank[:], xctx[:, kt * S + ti * 128:kt * S + (ti + 1) * 128],
                                     Wst[:, kt * 1024 + n2 * 512:kt * 1024 + (n2 + 1) * 512],
                                     start=(kt == 0), stop=False)
                        ins = e.matmul(bank[:], ones_col[0:1, 0:128],
                                       bo_row[0:1, n2 * 512:(n2 + 1) * 512],
                                       start=False, stop=True)
                        ins.then_inc(te.h, 1)
                    p.emit("t", wo_mm)
                    te.bump(1)
                if sti >= 2:
                    p.wait("v", sop[sti % 2], so_mark[sti - 2])
                p.wait("v", te)
                def s_ev(e, sb=sb):
                    e.tensor_copy(sb[:, 0:512], pP0[:])
                    ins = e.tensor_copy(sb[:, 512:1024], pP1[:])
                    ins.then_inc(ve.h, 1)
                    e.drain()
                p.emit("v", s_ev)
                ve.bump(1)
                p.wait("y", ve)
                _dma(p, "y", sop[sti % 2], s_out[b, ti * 128:(ti + 1) * 128, :], sb)
                so_mark[sti] = sop[sti % 2].n
                sti += 1
        p.wait("y", sop[0])
        p.wait("y", sop[1])



def prep_l0attn_host(d):
    f32 = np.float32
    def lay(wT):
        return wT.reshape(8, 128, 1024).transpose(1, 0, 2).reshape(128, 8192)
    p_ = np.arange(128)[:, None]
    f_ = np.arange(128)[None, :]
    out = {
        "gateWT": d["L0_attn_gate_w"].T.reshape(8, 128, 8).transpose(1, 0, 2).reshape(128, 64).copy(),
        "qwT": lay(d["L0_attn_q_w"].T),
        "kwT": np.stack([lay(d["L0_attn_k_w"][e].T) for e in range(8)]).reshape(8 * 128, 8192),
        "vwT": np.stack([lay(d["L0_attn_v_w"][e].T) for e in range(8)]).reshape(8 * 128, 8192),
        "fmq1T": d["L0_attn_fmq1_w"].T.copy(),
        "fmq2T": d["L0_attn_fmq2_w"].T.copy(),
        "fmk1T": d["L0_attn_fmk1_w"].T.copy(),
        "fmk2T": d["L0_attn_fmk2_w"].T.copy(),
        "normw_row": d["L0_attn_norm_w"].reshape(1, 128).copy(),
        "owT": lay(d["L0_attn_o_w"].T),
        "diagmask4": np.tile((f_ >= p_).astype(f32), (1, 4)),
        "ones_col": np.ones((1, 128), f32),
    }
    return {k: np.ascontiguousarray(v.astype(f32)) for k, v in out.items()}


L0ATTN_SHAPES = [("gateWT", [128, 64]), ("qwT", [128, 8192]),
                 ("kwT", [1024, 8192]), ("vwT", [1024, 8192]),
                 ("fmq1T", [128, 128]), ("fmq2T", [128, 128]),
                 ("fmk1T", [128, 128]), ("fmk2T", [128, 128]),
                 ("normw_row", [1, 128]), ("owT", [128, 8192]),
                 ("diagmask4", [128, 512]), ("ones_col", [1, 128])]


def emit_l0attn(cx, din, xnT_in, s_out, v_spill, pfx="l0a"):
    nc, p, E = cx.nc, cx.p, cx.E
    ld, lg, sc, ve, te, tr = cx.ld, cx.lg, cx.sc, cx.ve, cx.te, cx.tr
    sop = cx.sop
    H = 8
    if True:
        xnT = E(nc.sbuf_tensor(pfx + "_xnT_sb", [128, KD * S], BF16))
        qp = E(nc.sbuf_tensor(pfx + "_qp", [128, H * S], BF16))
        kpf = E(nc.sbuf_tensor(pfx + "_kpf", [128, H * S], BF16))
        kpt = E(nc.sbuf_tensor(pfx + "_kpt", [128, NT * D], BF16))
        oac = E(nc.sbuf_tensor(pfx + "_oac", [128, NT * D], BF16))
        Wkv = E(nc.sbuf_tensor(pfx + "_Wkv", [128, 8192], BF16))
        vsl = E(nc.sbuf_tensor(pfx + "_vsl", [128, NT * 512], BF16))
        qf = [vsl[:, 0:S], vsl[:, S:2 * S]]
        vst = [E(nc.sbuf_tensor(pfx + f"_vst{i}", [128, D], BF16)) for i in range(2)]
        aT = [E(nc.sbuf_tensor(pfx + f"_aT{i}", [128, 512], BF16)) for i in range(1)]
        Ssb = [E(nc.sbuf_tensor(pfx + f"_Ssb{i}", [128, 128], BF16)) for i in range(4)]
        gateWT = E(nc.sbuf_tensor(pfx + "_gateWT", [128, 64], BF16))
        fmw = {n: E(nc.sbuf_tensor(pfx + "_" + n, [128, 128], BF16))
               for n in ("fmq1T", "fmq2T", "fmk1T", "fmk2T")}
        diagmask4 = E(nc.sbuf_tensor(pfx + "_diagmask4", [128, 512], BF16))
        normw_row = E(nc.sbuf_tensor(pfx + "_normw_row", [1, 128], F32))
        normw_brd = E(nc.sbuf_tensor(pfx + "_normw_brd", [128, 128], F32))
        ones_col = E(nc.sbuf_tensor(pfx + "_ones_col", [1, 128], F32))
        w_s = E(nc.sbuf_tensor(pfx + "_w_s", [128, NT * 8], F32))
        ind_s = E(nc.sbuf_tensor(pfx + "_ind_s", [128, NT * 8], F32))
        lgc = E(nc.sbuf_tensor(pfx + "_lgc", [128, 8], F32))
        elg = E(nc.sbuf_tensor(pfx + "_elg", [128, 8], F32))
        indc = E(nc.sbuf_tensor(pfx + "_indc", [128, 8], F32))
        ccc = E(nc.sbuf_tensor(pfx + "_ccc", [128, 16], F32))
        tmpf = E(nc.sbuf_tensor(pfx + "_tmpf", [128, 512], F32))
        sqf = E(nc.sbuf_tensor(pfx + "_sqf", [128, 128], F32))
        s_stage = [E(nc.sbuf_tensor(pfx + f"_sst{i}", [128, D], BF16)) for i in range(1)]
        pA = E(nc.psum_tensor(pfx + "_pA", [128, 512], F32))
        pB = E(nc.psum_tensor(pfx + "_pB", [128, 512], F32))
        pAT = E(nc.psum_tensor(pfx + "_pAT", [128, 512], F32))
        pO = E(nc.psum_tensor(pfx + "_pO", [128, 512], F32))
        pS = [E(nc.psum_tensor(pfx + f"_pS{i}", [128, 512], F32)) for i in range(4)]

        banks = [pA, pB, pAT, pO, pS[0], pS[1]]
        bfree = {i: ("v", 0) for i in range(6)}
        semmap = {"v": ve, "s": sc}

        def bank_wait(i):
            s_, v_ = bfree[i]
            p.wait("t", semmap[s_], v_)

        for name, dst in [("normw_row", normw_row),
                          ("ones_col", ones_col)]:
            _dma(p, "y", ld, dst[:], din[name])
        p.wait("y", ld)
        for name, dst in [("diagmask4", diagmask4), ("gateWT", gateWT), ("fmq1T", fmw["fmq1T"]),
                          ("fmq2T", fmw["fmq2T"]), ("fmk1T", fmw["fmk1T"]),
                          ("fmk2T", fmw["fmk2T"])]:
            _dma(p, "g", lg, dst[:], din[name])
        p.wait("g", lg)
        consts_y = ld.n
        consts_g = lg.n
        p.wait("t", ld, consts_y)
        p.emit("t", lambda e: e.matmul(pA[:, 0:128], ones_col[0:1, :], normw_row[0:1, :],
                                       start=True, stop=True).then_inc(te.h, 1))
        te.bump(1)
        p.wait("v", te)
        p.emit("v", lambda e: e.tensor_copy(normw_brd[:], pA[:, 0:128]).then_inc(ve.h, 1))
        ve.bump(1)
        bfree[0] = ("v", ve.n)

        so_mark = {}
        sti = 0
        for b in range(ROWS_PER_CORE):
            if b > 0:
                p.wait("y", te)
            for kt in range(KD):
                _dma(p, "y", ld, xnT[:, kt * S:(kt + 1) * S], xnT_in[b, kt])
            p.wait("y", ld)
            xn_done = ld.n
            p.wait("t", ld, xn_done)
            p.wait("t", lg, consts_g)
            bank_wait(0)
            def rt_mm(e):
                for ti in range(NT):
                    for kt in range(KD):
                        ins = e.matmul(pA[:, ti * 8:(ti + 1) * 8],
                                       xnT[:, kt * S + ti * 128:kt * S + (ti + 1) * 128],
                                       gateWT[:, kt * 8:(kt + 1) * 8],
                                       start=(kt == 0), stop=(kt == KD - 1))
                ins.then_inc(te.h, 1)
            p.emit("t", rt_mm)
            te.bump(1)
            p.wait("v", te)
            _emit_top2_batch(p, pA[:, 0:128], w_s[:], ind_s[:], tmpf[:], sc, ve, NT)
            bfree[0] = ("v", ve.n)
            p.wait("g", te)
            _dma(p, "g", lg, Wkv[:], din["qwT"])
            p.wait("g", lg)
            p.wait("t", lg)
            def emit_fm(h, dst, fm1, fm2, scale):
                bq = qf[h % 2]
                for nt in range(4):
                    bank_wait(2)
                    bank_wait(3)
                    def fmm(e, nt=nt, bq=bq):
                        e.matmul(banks[2][:], fmw[fm1][:], bq[:, nt * 512:(nt + 1) * 512],
                                 start=True, stop=True)
                        ins = e.matmul(banks[3][:], fmw[fm2][:], bq[:, nt * 512:(nt + 1) * 512],
                                       start=True, stop=True)
                        ins.then_inc(te.h, 1)
                    p.emit("t", fmm)
                    te.bump(1)
                    p.wait("s", te)
                    if scale is None:
                        p.emit("s", lambda e: e.activation(tmpf[:], banks[2][:],
                                                           AF.Copy).then_inc(sc.h, 1))
                    else:
                        p.emit("s", lambda e, sca=scale: e.activation(
                            tmpf[:], banks[2][:], AF.Copy, scale=sca).then_inc(sc.h, 1))
                    sc.bump(1)
                    bfree[2] = ("s", sc.n)
                    p.wait("v", sc)
                    def fme(e, h=h, nt=nt, dst=dst):
                        ins = e.tensor_tensor(dst[:, h * S + nt * 512:h * S + (nt + 1) * 512],
                                              tmpf[:], banks[3][:], ALU.mult)
                        ins.then_inc(ve.h, 1)
                        e.drain()
                    p.emit("v", fme)
                    ve.bump(1)
                    bfree[3] = ("v", ve.n)
                fm_done[h] = ve.n

            def emit_projfm(dst, fm1, fm2, scale):
                for h in range(H):
                    bq = qf[h % 2]
                    if h >= 2:
                        p.wait("t", ve, fm_done[h - 2])
                    for np2 in range(2):
                        bi0 = (0, 1, 4, 5)[np2 * 2]
                        bi1 = (0, 1, 4, 5)[np2 * 2 + 1]
                        bank_wait(bi0)
                        bank_wait(bi1)
                        def prj(e, h=h, np2=np2, bk0=banks[bi0], bk1=banks[bi1]):
                            for nt, bk in ((np2 * 2, bk0), (np2 * 2 + 1, bk1)):
                                for kt in range(KD):
                                    ins = e.matmul(bk[:], Wkv[:, kt * 1024 + h * 128:kt * 1024 + (h + 1) * 128],
                                                   xnT[:, kt * S + nt * 512:kt * S + (nt + 1) * 512],
                                                   start=(kt == 0), stop=(kt == KD - 1))
                            ins.then_inc(te.h, 1)
                        p.emit("t", prj)
                        te.bump(1)
                        p.wait("v", te)
                        def prj_ev(e, np2=np2, bk0=banks[bi0], bk1=banks[bi1], bq=bq):
                            e.tensor_copy(bq[:, (np2 * 2) * 512:(np2 * 2 + 1) * 512], bk0[:])
                            ins = e.tensor_copy(bq[:, (np2 * 2 + 1) * 512:(np2 * 2 + 2) * 512], bk1[:])
                            ins.then_inc(ve.h, 1)
                        p.emit("v", prj_ev)
                        ve.bump(1)
                        bfree[bi0] = ("v", ve.n)
                        bfree[bi1] = ("v", ve.n)
                    if h >= 1:
                        emit_fm(h - 1, dst, fm1, fm2, scale)
                emit_fm(H - 1, dst, fm1, fm2, scale)
            fm_done = {}
            emit_projfm(qp[:], "fmq1T", "fmq2T", 0.08838834764831845)
            for ex in range(8):
                p.wait("g", te)
                p.wait("g", tr)
                _dma(p, "g", lg, Wkv[:], din["kwT"][ex * 128:(ex + 1) * 128, :])
                p.wait("g", lg)
                p.wait("t", lg)
                fm_done = {}
                emit_projfm(kpf[:], "fmk1T", "fmk2T", None)
                kp_done = ve.n
                p.wait("g", te)
                _dma(p, "g", lg, Wkv[:], din["vwT"][ex * 128:(ex + 1) * 128, :])
                p.wait("g", lg)
                p.wait("t", lg)
                p.wait("y", ve, kp_done)
                p.wait("y", lg)
                p.wait("y", sop[0])
                p.wait("y", sop[1])
                spill_mark = {}
                for ti in range(NT):
                    bv = vst[ti % 2][:]
                    if ti >= 2:
                        p.wait("v", ld, spill_mark[ti - 2])
                    b0 = (0, 1, 4, 5)[(ti * 2) % 4]
                    b1 = (0, 1, 4, 5)[(ti * 2 + 1) % 4]
                    bank_wait(b0)
                    bank_wait(b1)
                    def v_mm(e, ti=ti, bk0=banks[b0], bk1=banks[b1]):
                        for n2, bk in ((0, bk0), (1, bk1)):
                            for kt in range(KD):
                                ins = e.matmul(bk[:], xnT[:, kt * S + ti * 128:kt * S + (ti + 1) * 128],
                                               Wkv[:, kt * 1024 + n2 * 512:kt * 1024 + (n2 + 1) * 512],
                                               start=(kt == 0), stop=(kt == KD - 1))
                        ins.then_inc(te.h, 1)
                    p.emit("t", v_mm)
                    te.bump(1)
                    p.wait("v", te)
                    def v_ev(e, ti=ti, bk0=banks[b0], bk1=banks[b1], bv=bv, ex=ex):
                        e.tensor_scalar(vsl[:, ti * 512:(ti + 1) * 512], bk0[:],
                                        ind_s[:, ti * 8 + ex:ti * 8 + ex + 1],
                                        None, ALU.mult)
                        ins = e.tensor_scalar(bv[:, 0:512], bk1[:],
                                              ind_s[:, ti * 8 + ex:ti * 8 + ex + 1],
                                              None, ALU.mult)
                        ins.then_inc(ve.h, 1)
                        e.drain()
                    p.emit("v", v_ev)
                    ve.bump(1)
                    bfree[b0] = ("v", ve.n)
                    bfree[b1] = ("v", ve.n)
                    p.wait("y", ve)
                    _dma(p, "y", ld, v_spill[ti * 128:(ti + 1) * 128, :], bv[:, 0:512])
                    spill_mark[ti] = ld.n
                    if ti % 2 == 1 and ti // 2 < H:
                        kd = ti // 2
                        p.wait("y", ld)
                        inv = kpf[:, kd * S:(kd + 1) * S].rearrange("p (t c) -> p t c", c=128)
                        outv = kpt[:, :].rearrange("p (t c) -> p t c", c=D)[:, :, kd * 128:(kd + 1) * 128]
                        p.emit("y", lambda e, i=inv, o=outv: e.dma_start_transpose(o, i).then_inc(tr.h, 16))
                        tr.bump(16)
                        p.wait("y", tr)
                        if kd == 3:
                            tr_half = tr.n
                tr_done = tr.n
                for hg in (0, 4):
                    p.wait("t", tr, tr_half if hg == 0 else tr_done)
                    if hg > 0:
                        p.wait("y", ve)
                        p.wait("y", te)
                        vslv = vsl[:].rearrange("p (t c) -> p t c", c=512)
                        vsp = v_spill.rearrange("(t p) d -> p t d", p=128)
                        _dma(p, "y", ld, vslv, vsp)
                        p.wait("y", ld)
                        vsl_rdy = ld.n
                        p.wait("t", ld, vsl_rdy)
                    snap_mark = {}
                    oev_mark = {}
                    atev_mark = {}
                    for c in range(NT):
                        atb = aT[0][:]
                        patc = pAT if c % 2 == 0 else pB
                        poc = pO if c % 2 == 0 else pA
                        if c == 0:
                            p.wait("t", ve)
                            p.wait("t", sc)
                        if c >= 2:
                            p.wait("t", ve, atev_mark[c - 2])
                        def at_mm(e, hg=hg, c=c, patc=patc):
                            for j in range(4):
                                h = hg + j
                                ins = e.matmul(patc[:, j * 128:(j + 1) * 128],
                                               kpf[:, h * S + c * 128:h * S + (c + 1) * 128],
                                               qp[:, h * S + c * 128:h * S + (c + 1) * 128],
                                               start=True, stop=True,
                                               skip_group_check=True)
                            ins.then_inc(te.h, 1)
                        p.emit("t", at_mm)
                        te.bump(1)
                        p.wait("v", te)
                        def at_ev(e, atb=atb, patc=patc):
                            ins = e.tensor_tensor(atb, patc[:], diagmask4[:], ALU.mult)
                            ins.then_inc(ve.h, 1)
                            e.drain()
                        p.emit("v", at_ev)
                        ve.bump(1)
                        atev_mark[c] = ve.n
                        p.wait("t", ve, atev_mark[c])
                        if c >= 1:
                            p.wait("t", sc, snap_mark[c - 1][0])
                            p.wait("t", ve, snap_mark[c - 1][1])
                        if c >= 2:
                            p.wait("t", ve, oev_mark[c - 2])
                        def oi_mm(e, hg=hg, c=c, atb=atb, poc=poc):
                            for j in range(4):
                                h = hg + j
                                ins = e.matmul(pS[j][:, 0:128],
                                               kpt[:, c * D + h * 128:c * D + (h + 1) * 128],
                                               vsl[:, c * 512 + j * 128:c * 512 + (j + 1) * 128],
                                               start=(c == 0), stop=True,
                                               skip_group_check=True)
                            ins.then_inc(te.h, 1)
                            for j in range(4):
                                h = hg + j
                                ins = e.matmul(poc[:, j * 128:(j + 1) * 128],
                                               atb[:, j * 128:(j + 1) * 128],
                                               vsl[:, c * 512 + j * 128:c * 512 + (j + 1) * 128],
                                               start=True, stop=(c == 0),
                                               skip_group_check=True)
                                if c > 0:
                                    ins = e.matmul(poc[:, j * 128:(j + 1) * 128],
                                                   qp[:, h * S + c * 128:h * S + (c + 1) * 128],
                                                   Ssb[j][:], start=False, stop=True,
                                                   skip_group_check=True)
                            ins.then_inc(te.h, 1)
                        p.emit("t", oi_mm)
                        te.bump(2)
                        p.wait("s", te)
                        def snap_s(e):
                            e.activation(Ssb[0][:], pS[0][:, 0:128], AF.Copy)
                            ins = e.activation(Ssb[1][:], pS[1][:, 0:128], AF.Copy)
                            ins.then_inc(sc.h, 1)
                        p.emit("s", snap_s)
                        sc.bump(1)
                        p.wait("v", te)
                        def snap_v(e):
                            e.tensor_copy(Ssb[2][:], pS[2][:, 0:128])
                            ins = e.tensor_copy(Ssb[3][:], pS[3][:, 0:128])
                            ins.then_inc(ve.h, 1)
                            e.drain()
                        p.emit("v", snap_v)
                        ve.bump(1)
                        snap_mark[c] = (sc.n, ve.n)
                        p.wait("v", te)
                        def o_ev(e, hg=hg, c=c, ex=ex, poc=poc):
                            wc = w_s[:, c * 8 + ex:c * 8 + ex + 1]
                            dst = oac[:, c * D + hg * 128:c * D + hg * 128 + 512]
                            if ex == 0:
                                ins = e.tensor_scalar(dst, poc[:], wc, None, ALU.mult)
                            else:
                                ins = e.scalar_tensor_tensor(dst, poc[:], wc, dst,
                                                             ALU.mult, ALU.add)
                            ins.then_inc(ve.h, 1)
                            e.drain()
                        p.emit("v", o_ev)
                        ve.bump(1)
                        oev_mark[c] = ve.n
                for _bi in range(6):
                    bfree[_bi] = ("v", ve.n)
            sq_al = vsl[:, 0:1024]
            for ti in range(NT):
                tile = oac[:, ti * D:(ti + 1) * D]
                tile3 = tile.rearrange("p (h c) -> p h c", c=128)
                p.wait("s", ve)
                p.emit("s", lambda e, tile=tile, sq_al=sq_al: e.activation(
                    sq_al, tile, AF.Square).then_inc(sc.h, 1))
                sc.bump(1)
                p.wait("v", sc)
                def rms1(e, sq_al=sq_al):
                    sq3 = sq_al.rearrange("p (h c) -> p h c", c=128)
                    e.reduce_sum(ccc[:, 0:8].rearrange("p (h c) -> p h c", c=1),
                                 sq3, axis=AX)
                    e.drain()
                    ins = e.tensor_scalar(ccc[:, 0:8], ccc[:, 0:8], 1.0 / 128, 1e-5,
                                          ALU.mult, ALU.add)
                    ins.then_inc(ve.h, 1)
                    e.drain()
                p.emit("v", rms1)
                ve.bump(1)
                p.wait("s", ve)
                p.emit("s", lambda e: e.activation(ccc[:, 8:16], ccc[:, 0:8],
                                                   AF.Sqrt).then_inc(sc.h, 1))
                sc.bump(1)
                p.wait("v", sc)
                def rms2(e, tile=tile, tile3=tile3):
                    e.reciprocal(ccc[:, 0:8], ccc[:, 8:16])
                    e.drain()
                    rb = ccc[:, 0:8].rearrange("p (h c) -> p h c", c=1).broadcast_to([128, 8, 128])
                    e.tensor_tensor(tile3, tile3, rb, ALU.mult)
                    e.drain()
                    nb = normw_brd[:].rearrange("(q p) c -> p q c", q=1).broadcast_to([128, 8, 128]) if False else \
                         normw_brd[:].rearrange("p (q c) -> p q c", q=1).broadcast_to([128, 8, 128])
                    ins = e.tensor_tensor(tile3, tile3, nb, ALU.mult)
                    ins.then_inc(ve.h, 1)
                    e.drain()
                p.emit("v", rms2)
                ve.bump(1)
            onorm_done = ve.n
            p.wait("y", ve, onorm_done)
            p.wait("y", lg)
            p.wait("y", ld)
            p.wait("y", sop[0])
            p.wait("y", sop[1])
            for ti in range(NT):
                inv = oac[:, ti * D:(ti + 1) * D]
                outv = kpf[:, :].rearrange("p (kd s) -> p kd s", s=S)[:, :, ti * 128:(ti + 1) * 128]
                p.emit("y", lambda e, i=inv, o=outv: e.dma_start_transpose(o, i).then_inc(tr.h, 16))
                tr.bump(16)
                p.wait("y", tr)
            tr2_done = tr.n
            p.wait("g", tr, tr2_done)
            p.wait("g", te)
            _dma(p, "g", lg, Wkv[:], din["owT"])
            p.wait("g", lg)
            p.wait("t", lg)
            p.wait("t", tr, tr2_done)
            for ti in range(NT):
                sb = s_stage[0][:]
                for n2 in range(2):
                    bi = (0, 1, 4, 5)[(ti * 2 + n2) % 4]
                    bank_wait(bi)
                    def ow_mm(e, ti=ti, n2=n2, bk=banks[bi]):
                        for kt in range(KD):
                            ins = e.matmul(bk[:], kpf[:, kt * S + ti * 128:kt * S + (ti + 1) * 128],
                                           Wkv[:, kt * 1024 + n2 * 512:kt * 1024 + (n2 + 1) * 512],
                                           start=(kt == 0), stop=(kt == KD - 1))
                        ins.then_inc(te.h, 1)
                    p.emit("t", ow_mm)
                    te.bump(1)
                if sti >= 1:
                    p.wait("v", lg, so_mark[sti - 1])
                p.wait("v", te)
                b0 = (0, 1, 4, 5)[(ti * 2) % 4]
                b1 = (0, 1, 4, 5)[(ti * 2 + 1) % 4]
                def s_ev(e, sb=sb, b0=b0, b1=b1):
                    e.tensor_copy(sb[:, 0:512], banks[b0][:])
                    ins = e.tensor_copy(sb[:, 512:1024], banks[b1][:])
                    ins.then_inc(ve.h, 1)
                    e.drain()
                p.emit("v", s_ev)
                ve.bump(1)
                bfree[b0] = ("v", ve.n)
                bfree[b1] = ("v", ve.n)
                p.wait("g", ve)
                _dma(p, "g", lg, s_out[b, ti * 128:(ti + 1) * 128, :], sb)
                sti += 1
                so_mark[sti - 1] = lg.n
        p.wait("y", sop[0])
        p.wait("y", sop[1])



def _split_rows(a):
    return [np.ascontiguousarray(a[c * 2:(c + 1) * 2]) for c in range(N_CORES)]


def build_one_stage(which):
    import contextlib
    nc = bass.Bass("TRN2", target_bir_lowering=False)
    cx = Ctx()
    cx.nc = nc
    xnT_in = nc.dram_tensor("xnT", [ROWS_PER_CORE, KD, 128, S], BF16, kind="ExternalInput").ap()
    s_out = nc.dram_tensor("s0", [ROWS_PER_CORE, S, D], F32, kind="ExternalOutput").ap()
    v_spill = nc.dram_tensor("i_vsp", [S, 512], BF16, kind="Internal").ap()
    din = {n: nc.dram_tensor("in_" + n, sh, F32, kind="ExternalInput").ap()
           for n, sh in L0ATTN_SHAPES}
    with contextlib.ExitStack() as gst:
        for nm in ("ld", "lg", "te", "sc", "ve", "tr"):
            setattr(cx, nm, Sem(gst.enter_context(nc.semaphore(nm))))
        cx.sop = [Sem(gst.enter_context(nc.semaphore("sop0"))),
                  Sem(gst.enter_context(nc.semaphore("sop1")))]
        cx.p = Prog(nc)
        with contextlib.ExitStack() as sst:
            cx.E = sst.enter_context
            emit_l0attn(cx, din, xnT_in, s_out, v_spill, "l0a")
            with nc.Block() as blk:
                cx.p.run(blk)
    return nc


class Ctx:
    pass


def build_fused():
    import contextlib
    nc = bass.Bass("TRN2", target_bir_lowering=False)
    x_in = nc.dram_tensor("x", [ROWS_PER_CORE, S, D], F32, kind="ExternalInput").ap()
    id_in = nc.dram_tensor("ident_in", [128, 128], F32, kind="ExternalInput").ap()
    out_t = nc.dram_tensor("out", [ROWS_PER_CORE, S, D], F32, kind="ExternalOutput").ap()
    din_l0a = {n: nc.dram_tensor("w0a_" + n, sh, F32, kind="ExternalInput").ap()
               for n, sh in L0ATTN_SHAPES}
    din_l0f = {n: nc.dram_tensor("w0f_" + n, sh, F32, kind="ExternalInput").ap()
               for n, sh in L0FFN_SHAPES}
    din_l1a = {n: nc.dram_tensor("w1a_" + n, sh, F32, kind="ExternalInput").ap()
               for n, sh in L1ATTN_SHAPES}
    gw_in = nc.dram_tensor("w1f_gateW", [128, 64], F32, kind="ExternalInput").ap()
    wgu_in = nc.dram_tensor("w1f_Wgu", [8 * 128, 32768], F32, kind="ExternalInput").ap()
    wdn_in = nc.dram_tensor("w1f_Wdn", [8 * 128, 16384], F32, kind="ExternalInput").ap()
    xnT0 = nc.dram_tensor("i_xnT0", [ROWS_PER_CORE, KD, 128, S], BF16, kind="Internal").ap()
    xnT1 = nc.dram_tensor("i_xnT1", [ROWS_PER_CORE, KD, 128, S], BF16, kind="Internal").ap()
    s0_t = nc.dram_tensor("i_s0", [ROWS_PER_CORE, S, D], F32, kind="Internal").ap()
    g0_t = nc.dram_tensor("i_g0", [ROWS_PER_CORE, S, D], F32, kind="Internal").ap()
    x1_t = nc.dram_tensor("i_x1", [ROWS_PER_CORE, S, D], F32, kind="Internal").ap()
    s1_t = nc.dram_tensor("i_s1", [ROWS_PER_CORE, S, D], F32, kind="Internal").ap()
    g1_t = nc.dram_tensor("i_g1", [ROWS_PER_CORE, S, D], F32, kind="Internal").ap()
    v_spill = nc.dram_tensor("i_vsp", [S, 512], BF16, kind="Internal").ap()

    cx = Ctx()
    cx.nc = nc
    with contextlib.ExitStack() as gst:
        for nm in ("ld", "lg", "te", "sc", "ve", "tr"):
            setattr(cx, nm, Sem(gst.enter_context(nc.semaphore(nm))))
        cx.sop = [Sem(gst.enter_context(nc.semaphore("sop0"))),
                  Sem(gst.enter_context(nc.semaphore("sop1")))]
        bar = Sem(gst.enter_context(nc.semaphore("bar")))

        def stage(fn):
            cx.p = Prog(nc)
            with contextlib.ExitStack() as sst:
                cx.E = sst.enter_context
                if bar.n > 0:
                    for eng in ("t", "v", "s", "g", "y"):
                        cx.p.wait(eng, bar)
                fn(cx)
                for s_ in (cx.ld, cx.lg, cx.te, cx.sc, cx.ve, cx.tr,
                           cx.sop[0], cx.sop[1]):
                    cx.p.wait("y", s_)
                cx.p.emit("y", lambda e, h=bar.h: e.sem_inc(h, 1))
                bar.bump(1)
                with nc.Block() as blk:
                    cx.p.run(blk)

        stage(lambda cx: emit_pre(cx, x_in, id_in, xnT0, "pre"))
        stage(lambda cx: emit_l0attn(cx, din_l0a, xnT0, s0_t, v_spill, "l0a"))
        stage(lambda cx: emit_l0ffn(cx, din_l0f, xnT0, g0_t, "l0f"))
        stage(lambda cx: emit_residual(cx, True, x_in, s0_t, g0_t, x1_t, xnT1,
                                       id_in, "mid"))
        stage(lambda cx: emit_l1attn(cx, din_l1a, xnT1, s1_t, "l1a"))
        stage(lambda cx: emit_l1ffn(cx, gw_in, wgu_in, wdn_in, xnT1, g1_t, "l1f"))
        stage(lambda cx: emit_residual(cx, False, x1_t, s1_t, g1_t, out_t,
                                       None, None, "fin"))
    return nc


_EYE = None

def kernel(**inputs):
    global _EYE
    if _EYE is None:
        _EYE = np.eye(128, dtype=np.float32)
    x = inputs["L0_attn_x"]
    xs = _split_rows(x)
    cores = list(range(N_CORES))
    wmap = {}
    for k, v in prep_l0attn_host(inputs).items():
        wmap["w0a_" + k] = v
    for k, v in prep_l0ffn_host(inputs).items():
        wmap["w0f_" + k] = v
    for k, v in prep_l1attn_host(inputs).items():
        wmap["w1a_" + k] = v
    for k, v in prep_l1ffn_host(inputs).items():
        wmap["w1f_" + k] = v
    nc = build_fused()
    in_maps = [dict(x=xs[c], ident_in=_EYE, **wmap) for c in cores]
    r = run_bass_kernel_spmd(nc, in_maps, core_ids=cores)
    return np.concatenate([np.asarray(r.results[c]["out"]) for c in cores], axis=0)


# version 22
# speedup vs baseline: 1.0071x; 1.0071x over previous
import numpy as np
import ml_dtypes
import concourse.bass as bass
import concourse.mybir as mybir
from concourse.bass_utils import run_bass_kernel_spmd

F32 = mybir.dt.float32
BF16 = mybir.dt.bfloat16
AF = mybir.ActivationFunctionType
ALU = mybir.AluOpType
AX = mybir.AxisListType.X
BF = ml_dtypes.bfloat16

N_CORES = 8
ROWS_PER_CORE = 2
S = 2048
D = 1024
NT = S // 128
KD = D // 128



class Sem:
    def __init__(self, handle):
        self.h = handle
        self.n = 0

    def bump(self, k):
        self.n += k
        return self.n


class Prog:
    ENGS = ("t", "v", "s", "g", "y")

    def __init__(self, nc):
        self.nc = nc
        self.q = {e: [] for e in self.ENGS}

    def emit(self, eng, fn):
        self.q[eng].append(fn)

    def wait(self, eng, sem, val=None):
        v = sem.n if val is None else val
        self.q[eng].append(lambda e, s=sem.h, v=v: e.wait_ge(s, v))

    def run(self, block):
        if self.q["g"]:
            @block.gpsimd
            def _(eng):
                for fn in self.q["g"]:
                    fn(eng)
        if self.q["t"]:
            @block.tensor
            def _(eng):
                for fn in self.q["t"]:
                    fn(eng)
        if self.q["s"]:
            @block.scalar
            def _(eng):
                for fn in self.q["s"]:
                    fn(eng)
        if self.q["v"]:
            @block.vector
            def _(eng):
                for fn in self.q["v"]:
                    fn(eng)
        if self.q["y"]:
            @block.sync
            def _(eng):
                for fn in self.q["y"]:
                    fn(eng)


def _dma(p, eng, sem, out, in_):
    p.emit(eng, lambda e, o=out, i=in_, s=sem.h: e.dma_start(out=o, in_=i).then_inc(s, 16))
    return sem.bump(16)


def _emit_ln_tile(p, x_f32, xn_bf, scr, sc, ve, eps=1e-5, gcol=None, bcol=None,
                  out_extra=None):
    nc = p.nc
    sq, c0, c1, c2, c3, c4 = scr["sq"], scr["c0"], scr["c1"], scr["c2"], scr["c3"], scr["c4"]
    p.emit("s", lambda e: e.activation(sq, x_f32, AF.Square).then_inc(sc.h, 1))
    sc.bump(1)
    p.wait("v", sc)
    def vchain(e):
        e.reduce_sum(c0, x_f32, axis=AX)
        e.reduce_sum(c1, sq, axis=AX)
        e.drain()
        e.tensor_scalar(c0, c0, 1.0 / D, None, ALU.mult)
        e.tensor_scalar(c1, c1, 1.0 / D, None, ALU.mult)
        e.drain()
        e.tensor_tensor(c2, c0, c0, ALU.mult)
        e.drain()
        e.tensor_tensor(c1, c1, c2, ALU.subtract)
        e.drain()
        e.tensor_scalar(c1, c1, eps, None, ALU.add).then_inc(ve.h, 1)
    p.emit("v", vchain)
    ve.bump(1)
    p.wait("s", ve)
    p.emit("s", lambda e: e.activation(c3, c1, AF.Sqrt).then_inc(sc.h, 1))
    sc.bump(1)
    p.wait("v", sc)
    def vnorm(e):
        e.reciprocal(c4, c3)
        e.drain()
        e.tensor_scalar(xn_bf, x_f32, c0, c4, ALU.subtract, ALU.mult).then_inc(ve.h, 1)
    p.emit("v", vnorm)
    ve.bump(1)



def emit_pre(cx, x_in, id_in, xnT_out, pfx="pre"):
    nc, p, E = cx.nc, cx.p, cx.E
    ld, sc, ve, te, so = cx.ld, cx.sc, cx.ve, cx.te, cx.sop[0]
    if True:
        xt = [E(nc.sbuf_tensor(pfx + f"_xt{i}", [128, D], F32)) for i in range(2)]
        xnt = E(nc.sbuf_tensor(pfx + "_xnt", [128, D], F32))
        sq = E(nc.sbuf_tensor(pfx + "_sq", [128, D], F32))
        cols = [E(nc.sbuf_tensor(pfx + f"_c{i}", [128, 1], F32)) for i in range(5)]
        ident = E(nc.sbuf_tensor(pfx + "_ident", [128, 128], F32))
        xnT_sb = E(nc.sbuf_tensor(pfx + "_xnTsb", [128, KD * S], BF16))
        pt = [E(nc.psum_tensor(pfx + f"_pt{i}", [128, 512], F32)) for i in range(2)]
        scr = {"sq": sq[:], "c0": cols[0][:], "c1": cols[1][:], "c2": cols[2][:],
               "c3": cols[3][:], "c4": cols[4][:]}

        _dma(p, "y", ld, ident[:], id_in)
        p.wait("y", ld)
        p.wait("t", ld)
        ve0 = ve.n

        for b in range(ROWS_PER_CORE):
            for ti in range(NT):
                it = b * NT + ti
                buf = xt[it % 2][:]
                if it >= 2:
                    p.wait("y", ve, ve0 + (it - 2) * 3 + 2)
                _dma(p, "y", ld, buf, x_in[b, ti * 128:(ti + 1) * 128, :])
                p.wait("y", ld)
                p.wait("s", ld)
                p.wait("v", ld)
                _emit_ln_tile(p, buf, xnt[:], scr, sc, ve)
                p.wait("t", ve)
                def tchain(e):
                    for kt in range(KD):
                        bank = pt[kt // 4]
                        ins = e.transpose(bank[:, (kt % 4) * 128:(kt % 4 + 1) * 128],
                                          xnt[:, kt * 128:(kt + 1) * 128], ident[:])
                        if kt == KD - 1:
                            ins.then_inc(te.h, 1)
                p.emit("t", tchain)
                te.bump(1)
                if ti == 0 and b > 0:
                    p.wait("v", so)
                p.wait("v", te)
                def evac(e, ti=ti):
                    for g in range(2):
                        for j in range(4):
                            kt = g * 4 + j
                            ins = e.tensor_copy(
                                xnT_sb[:, kt * S + ti * 128: kt * S + (ti + 1) * 128],
                                pt[g][:, j * 128:(j + 1) * 128])
                    ins.then_inc(ve.h, 1)
                p.emit("v", evac)
                ve.bump(1)
                p.wait("t", ve)
            p.wait("y", ve)
            for kt in range(KD):
                _dma(p, "y", so, xnT_out[b, kt], xnT_sb[:, kt * S:(kt + 1) * S])
            p.wait("y", so)
        p.wait("y", so)



def emit_residual(cx, with_ln, x_in, s_in, g_in, x1_out, xnT_out=None, id_in=None, pfx="res"):
    nc, p, E = cx.nc, cx.p, cx.E
    ld, sc, ve, te, so = cx.ld, cx.sc, cx.ve, cx.te, cx.sop[0]
    if True:
        xt = [E(nc.sbuf_tensor(pfx + f"_xt{i}", [128, D], F32)) for i in range(2)]
        stt = [E(nc.sbuf_tensor(pfx + f"_st{i}", [128, D], F32)) for i in range(2)]
        gt = [E(nc.sbuf_tensor(pfx + f"_gt{i}", [128, D], F32)) for i in range(2)]
        gs2 = [E(nc.sbuf_tensor(pfx + f"_gs{i}", [128, D], F32)) for i in range(2)]
        x1t = [E(nc.sbuf_tensor(pfx + f"_x1t{i}", [128, D], F32)) for i in range(2)]
        if with_ln:
            xnt = E(nc.sbuf_tensor(pfx + "_xnt", [128, D], F32))
            sq = E(nc.sbuf_tensor(pfx + "_sq", [128, D], F32))
            cols = [E(nc.sbuf_tensor(pfx + f"_c{i}", [128, 1], F32)) for i in range(5)]
            ident = E(nc.sbuf_tensor(pfx + "_ident", [128, 128], F32))
            xnT_sb = E(nc.sbuf_tensor(pfx + "_xnTsb", [128, KD * S], BF16))
            pt = [E(nc.psum_tensor(pfx + f"_pt{i}", [128, 512], F32)) for i in range(2)]
            scr = {"sq": sq[:], "c0": cols[0][:], "c1": cols[1][:], "c2": cols[2][:],
                   "c3": cols[3][:], "c4": cols[4][:]}
        if with_ln:
            _dma(p, "y", ld, ident[:], id_in)
            p.wait("y", ld)
            p.wait("t", ld)

        ve_per = 1 + (3 if with_ln else 0)
        so_after_tile = {}
        ve0 = ve.n
        for b in range(ROWS_PER_CORE):
            for ti in range(NT):
                it = b * NT + ti
                bx, bs, bg, bo = xt[it % 2][:], stt[it % 2][:], gt[it % 2][:], x1t[it % 2][:]
                sl = slice(ti * 128, (ti + 1) * 128)
                gbuf = gs2[it % 2][:]
                if it >= 2:
                    p.wait("y", ve, ve0 + (it - 2) * ve_per + ve_per)

                _dma(p, "y", ld, bx, x_in[b, sl, :])
                _dma(p, "y", ld, bs, s_in[b, sl, :])
                _dma(p, "y", ld, bg, g_in[b, sl, :])
                p.wait("y", ld)
                if it >= 2:
                    p.wait("s", ve, ve0 + (it - 2) * ve_per + 1)
                p.wait("s", ld)
                p.emit("s", lambda e, g=bg, gb=gbuf: e.activation(gb, g, AF.Silu).then_inc(sc.h, 1))
                sc.bump(1)
                p.wait("v", sc)
                p.wait("v", ld)
                if it >= 2:
                    p.wait("v", so, so_after_tile[it - 2])
                def resid(e, bs=bs, bx=bx, bo=bo, gb=gbuf):
                    e.tensor_tensor(gb, bs, gb, ALU.mult)
                    e.drain()
                    e.tensor_tensor(bo, bx, gb, ALU.add).then_inc(ve.h, 1)
                    e.drain()
                p.emit("v", resid)
                ve.bump(1)
                p.wait("y", ve)
                _dma(p, "y", so, x1_out[b, sl, :], bo)
                so_after_tile[it] = so.n
                if with_ln:
                    if ti == 0 and b > 0:
                        p.wait("v", so)
                    p.wait("s", ve)
                    _emit_ln_tile(p, bo, xnt[:], scr, sc, ve)
                    p.wait("t", ve)
                    def tchain(e):
                        for kt in range(KD):
                            bank = pt[kt // 4]
                            ins = e.transpose(bank[:, (kt % 4) * 128:(kt % 4 + 1) * 128],
                                              xnt[:, kt * 128:(kt + 1) * 128], ident[:])
                            if kt == KD - 1:
                                ins.then_inc(te.h, 1)
                    p.emit("t", tchain)
                    te.bump(1)
                    p.wait("v", te)
                    def evac(e, ti=ti):
                        for g in range(2):
                            for j in range(4):
                                kt = g * 4 + j
                                ins = e.tensor_copy(
                                    xnT_sb[:, kt * S + ti * 128: kt * S + (ti + 1) * 128],
                                    pt[g][:, j * 128:(j + 1) * 128])
                        ins.then_inc(ve.h, 1)
                    p.emit("v", evac)
                    ve.bump(1)
                    p.wait("t", ve)
            if with_ln:
                p.wait("y", ve)
                for kt in range(KD):
                    _dma(p, "y", so, xnT_out[b, kt], xnT_sb[:, kt * S:(kt + 1) * S])
        p.wait("y", so)



def prep_l0ffn_host(d):
    f32 = np.float32
    out = {
        "Wenc": d["L0_ffn_W_enc"].reshape(8, 128, 2048).transpose(1, 0, 2).reshape(128, 16384).copy(),
        "benc": d["L0_ffn_b_enc"].reshape(16, 128).T.copy(),
        "Ws2c": d["L0_ffn_W_s2c"].reshape(16, 128, 64).transpose(1, 0, 2).reshape(128, 1024).copy(),
        "bs2c": d["L0_ffn_b_s2c"].reshape(64, 1).copy(),
        "Wr": d["L0_ffn_Wr"].copy(),
        "br_row": d["L0_ffn_br"].reshape(1, 8).copy(),
        "W1": d["L0_ffn_W1"].transpose(1, 0, 2).reshape(64, 8192).copy(),
        "b1c": d["L0_ffn_b1"].reshape(8, 8, 128).transpose(2, 0, 1).reshape(128, 64).copy(),
        "W2": d["L0_ffn_W2"].reshape(8, 8, 128, 64).transpose(2, 0, 1, 3).reshape(128, 4096).copy(),
        "b2_row": d["L0_ffn_b2"].reshape(1, 512).copy(),
        "Wc2s": d["L0_ffn_W_c2s"].copy(),
        "bc2s": d["L0_ffn_b_c2s"].reshape(16, 128).T.copy(),
        "Wdec": d["L0_ffn_W_dec"].reshape(16, 128, 1024).transpose(1, 0, 2).reshape(128, 16384).copy(),
        "bdec_row": d["L0_ffn_b_dec"].reshape(1, 1024).copy(),
        "lng_row": d["L0_ffn_ln_g"].reshape(1, 1024).copy(),
        "lnb_row": d["L0_ffn_ln_b"].reshape(1, 1024).copy(),
        "ones_col": np.ones((1, 128), f32),
        "ident_in": np.eye(128, dtype=f32),
    }
    return {k: np.ascontiguousarray(v.astype(f32)) for k, v in out.items()}


L0FFN_SHAPES = [("Wenc", [128, 16384]), ("benc", [128, 16]),
                ("Ws2c", [128, 1024]), ("bs2c", [64, 1]),
                ("Wr", [64, 8]), ("br_row", [1, 8]),
                ("W1", [64, 8192]), ("b1c", [128, 64]),
                ("W2", [128, 4096]), ("b2_row", [1, 512]),
                ("Wc2s", [64, 2048]), ("bc2s", [128, 16]),
                ("Wdec", [128, 16384]), ("bdec_row", [1, 1024]),
                ("lng_row", [1, 1024]), ("lnb_row", [1, 1024]),
                ("ones_col", [1, 128]), ("ident_in", [128, 128])]


def emit_l0ffn(cx, din, xnT_in, g_out, pfx="l0f"):
    nc, p, E = cx.nc, cx.p, cx.E
    ld, lg, sc, ve, te = cx.ld, cx.lg, cx.sc, cx.ve, cx.te
    sop = cx.sop
    if True:
        Wenc = E(nc.sbuf_tensor(pfx + "_Wenc", [128, 16384], BF16))
        Ws2c = E(nc.sbuf_tensor(pfx + "_Ws2c", [128, 1024], BF16))
        Wr = E(nc.sbuf_tensor(pfx + "_Wr", [64, 8], BF16))
        W1 = E(nc.sbuf_tensor(pfx + "_W1", [64, 8192], BF16))
        W2 = E(nc.sbuf_tensor(pfx + "_W2", [128, 4096], BF16))
        Wc2s = E(nc.sbuf_tensor(pfx + "_Wc2s", [64, 2048], BF16))
        Wdec = E(nc.sbuf_tensor(pfx + "_Wdec", [128, 16384], BF16))
        benc = E(nc.sbuf_tensor(pfx + "_benc", [128, 16], F32))
        bs2c = E(nc.sbuf_tensor(pfx + "_bs2c", [64, 1], F32))
        br_row = E(nc.sbuf_tensor(pfx + "_br_row", [1, 8], F32))
        b1c = E(nc.sbuf_tensor(pfx + "_b1c", [128, 64], F32))
        b2_row = E(nc.sbuf_tensor(pfx + "_b2_row", [1, 512], F32))
        bc2s = E(nc.sbuf_tensor(pfx + "_bc2s", [128, 16], F32))
        bdec_row = E(nc.sbuf_tensor(pfx + "_bdec_row", [1, 1024], F32))
        lng_row = E(nc.sbuf_tensor(pfx + "_lng_row", [1, 1024], F32))
        lnb_row = E(nc.sbuf_tensor(pfx + "_lnb_row", [1, 1024], F32))
        ones_col = E(nc.sbuf_tensor(pfx + "_ones_col", [1, 128], F32))
        ident = E(nc.sbuf_tensor(pfx + "_ident", [128, 128], F32))
        lng_brd = E(nc.sbuf_tensor(pfx + "_lng_brd", [128, 1024], F32))
        lnb_brd = E(nc.sbuf_tensor(pfx + "_lnb_brd", [128, 1024], F32))
        xnT = E(nc.sbuf_tensor(pfx + "_xnT_sb", [128, KD * 512], BF16))
        enc = E(nc.sbuf_tensor(pfx + "_enc", [128, 16 * 512], BF16))
        cont = E(nc.sbuf_tensor(pfx + "_cont", [128, 512], BF16))
        hfe = E(nc.sbuf_tensor(pfx + "_hfe", [128, 8 * 512], BF16))
        moe = E(nc.sbuf_tensor(pfx + "_moe", [128, 16 * 512], BF16))
        acc_tok = E(nc.sbuf_tensor(pfx + "_acc_tok", [128, 256], F32))
        acc_feat = E(nc.sbuf_tensor(pfx + "_acc_feat", [128, 512], BF16))
        wT = E(nc.sbuf_tensor(pfx + "_wT", [128, 512], F32))
        w_s = E(nc.sbuf_tensor(pfx + "_w_s", [128, 32], F32))
        lgc = E(nc.sbuf_tensor(pfx + "_lgc", [128, 8], F32))
        elg = E(nc.sbuf_tensor(pfx + "_elg", [128, 8], F32))
        inds = E(nc.sbuf_tensor(pfx + "_inds", [128, 8], F32))
        ccc = E(nc.sbuf_tensor(pfx + "_ccc", [128, 16], F32))
        dec_f32 = E(nc.sbuf_tensor(pfx + "_dec_f32", [128, 1024], F32))
        sqb = E(nc.sbuf_tensor(pfx + "_sqb", [128, 1024], F32))
        gout = [E(nc.sbuf_tensor(pfx + f"_gout{i}", [128, 1024], F32)) for i in range(2)]
        pA = E(nc.psum_tensor(pfx + "_pA", [128, 512], F32))
        pB = E(nc.psum_tensor(pfx + "_pB", [128, 512], F32))
        pR = E(nc.psum_tensor(pfx + "_pR", [128, 512], F32))
        pY = E(nc.psum_tensor(pfx + "_pY", [128, 512], F32))
        pD0 = E(nc.psum_tensor(pfx + "_pD0", [128, 512], F32))
        pD1 = E(nc.psum_tensor(pfx + "_pD1", [128, 512], F32))
        pT = E(nc.psum_tensor(pfx + "_pT", [128, 512], F32))

        for name, dst in [("Wenc", Wenc), ("Ws2c", Ws2c), ("Wr", Wr), ("W1", W1),
                          ("W2", W2), ("Wc2s", Wc2s), ("Wdec", Wdec)]:
            _dma(p, "g", lg, dst[:], din[name])
        for name, dst in [("benc", benc), ("bs2c", bs2c), ("br_row", br_row),
                          ("b1c", b1c), ("b2_row", b2_row), ("bc2s", bc2s),
                          ("bdec_row", bdec_row), ("lng_row", lng_row),
                          ("lnb_row", lnb_row), ("ones_col", ones_col),
                          ("ident_in", ident)]:
            _dma(p, "y", ld, dst[:], din[name])
        p.wait("y", ld)
        p.wait("g", lg)
        ld_w = ld.n
        p.wait("t", ld, ld_w)
        p.wait("t", lg)
        for i, (row, dst) in enumerate([(lng_row, lng_brd), (lnb_row, lnb_brd)]):
            for n2 in range(2):
                p.emit("t", lambda e, r=row, n2=n2: e.matmul(
                    pT[:], ones_col[0:1, :], r[0:1, n2 * 512:(n2 + 1) * 512],
                    start=True, stop=True).then_inc(te.h, 1))
                te.bump(1)
                p.wait("v", te)
                p.emit("v", lambda e, d=dst, n2=n2: e.tensor_copy(
                    d[:, n2 * 512:(n2 + 1) * 512], pT[:]).then_inc(ve.h, 1))
                ve.bump(1)
                p.wait("t", ve)

        free_at = {"pA": ("sc", 0), "pB": ("sc", 0), "pD0": ("sc", 0), "pD1": ("sc", 0)}
        sem_of = {"sc": sc, "ve": ve}
        so_mark = {}

        NSLAB = 4
        for b in range(ROWS_PER_CORE):
            for sl in range(NSLAB):
                tok0 = sl * 512
                if b + sl > 0:
                    p.wait("y", te)
                for kt in range(KD):
                    _dma(p, "y", ld, xnT[:, kt * 512:(kt + 1) * 512],
                         xnT_in[b, kt][:, tok0:tok0 + 512])
                p.wait("y", ld)
                p.wait("t", ld)
                for mt in range(16):
                    bank, bname = [(pA, "pA"), (pB, "pB"), (pD0, "pD0"), (pD1, "pD1")][mt % 4]
                    s_, v_ = free_at[bname]
                    p.wait("t", sem_of[s_], v_)
                    def enc_mm(e, mt=mt, bank=bank):
                        for kt in range(KD):
                            ins = e.matmul(bank[:], Wenc[:, kt * 2048 + mt * 128: kt * 2048 + (mt + 1) * 128],
                                           xnT[:, kt * 512:(kt + 1) * 512],
                                           start=(kt == 0), stop=(kt == KD - 1))
                        ins.then_inc(te.h, 1)
                    p.emit("t", enc_mm)
                    te.bump(1)
                    p.wait("s", te)
                    p.emit("s", lambda e, mt=mt, bank=bank: e.activation(
                        enc[:, mt * 512:(mt + 1) * 512], bank[:], AF.Sigmoid,
                        bias=benc[:, mt:mt + 1]).then_inc(sc.h, 1))
                    sc.bump(1)
                    free_at[bname] = ("sc", sc.n)
                p.wait("t", sc)
                p.wait("t", ve)
                def cont_mm(e):
                    for kt in range(16):
                        ins = e.matmul(pR[0:64, 0:512], Ws2c[:, kt * 64:(kt + 1) * 64],
                                       enc[:, kt * 512:(kt + 1) * 512],
                                       start=(kt == 0), stop=(kt == 15))
                    ins.then_inc(te.h, 1)
                p.emit("t", cont_mm)
                te.bump(1)
                p.wait("v", te)
                p.emit("v", lambda e: e.tensor_scalar(cont[0:64, :], pR[0:64, 0:512],
                                                      bs2c[0:64, :], None, ALU.add).then_inc(ve.h, 1))
                ve.bump(1)
                cont_done = ve.n
                p.wait("t", ve, cont_done)
                def rt_mm(e):
                    for tt in range(4):
                        e.matmul(pY[:, tt * 8:(tt + 1) * 8], cont[0:64, tt * 128:(tt + 1) * 128],
                                 Wr[0:64, :], start=True, stop=False)
                        ins = e.matmul(pY[:, tt * 8:(tt + 1) * 8], ones_col[0:1, 0:128],
                                       br_row[0:1, :], start=False, stop=True)
                    ins.then_inc(te.h, 1)
                p.emit("t", rt_mm)
                te.bump(1)
                p.wait("v", te)
                for tt in range(4):
                    def rt_sm1(e, tt=tt):
                        e.tensor_copy(lgc[:], pY[:, tt * 8:(tt + 1) * 8])
                        e.drain()
                        e.reduce_max(ccc[:, 0:1], lgc[:], axis=AX)
                        e.drain()
                        ins = e.tensor_scalar(lgc[:], lgc[:], ccc[:, 0:1], None, ALU.subtract)
                        ins.then_inc(ve.h, 1)
                        e.drain()
                    p.emit("v", rt_sm1)
                    ve.bump(1)
                    p.wait("s", ve)
                    p.emit("s", lambda e: e.activation(elg[:], lgc[:], AF.Exp).then_inc(sc.h, 1))
                    sc.bump(1)
                    p.wait("v", sc)
                    def rt_sm2(e, tt=tt):
                        e.reduce_sum(ccc[:, 1:2], elg[:], axis=AX)
                        e.drain()
                        e.reciprocal(ccc[:, 2:3], ccc[:, 1:2])
                        e.drain()
                        e.tensor_scalar(lgc[:], elg[:], ccc[:, 2:3], None, ALU.mult)
                        e.drain()
                        e.reduce_max(ccc[:, 3:4], lgc[:], axis=AX)
                        e.drain()
                        e.tensor_scalar(inds[:], lgc[:], ccc[:, 3:4], None, ALU.is_ge)
                        e.drain()
                        e.scalar_tensor_tensor(elg[:], inds[:], -1e9, lgc[:], ALU.mult, ALU.add)
                        e.drain()
                        e.reduce_max(ccc[:, 4:5], elg[:], axis=AX)
                        e.drain()
                        e.tensor_scalar(inds[:], lgc[:], ccc[:, 4:5], None, ALU.is_ge)
                        e.tensor_tensor(ccc[:, 5:6], ccc[:, 3:4], ccc[:, 4:5], ALU.add)
                        e.drain()
                        e.reciprocal(ccc[:, 6:7], ccc[:, 5:6])
                        e.drain()
                        ins = e.scalar_tensor_tensor(w_s[:, tt * 8:(tt + 1) * 8], lgc[:],
                                                     ccc[:, 6:7], inds[:], ALU.mult, ALU.mult)
                        ins.then_inc(ve.h, 1)
                        e.drain()
                    p.emit("v", rt_sm2)
                    ve.bump(1)
                w_done = ve.n
                p.wait("t", ve, w_done)
                p.wait("t", sc)
                def wt_mm(e):
                    for tt in range(4):
                        ins = e.transpose(pT[0:8, tt * 128:(tt + 1) * 128],
                                          w_s[:, tt * 8:(tt + 1) * 8], ident[:])
                    ins.then_inc(te.h, 1)
                p.emit("t", wt_mm)
                te.bump(1)
                p.wait("s", te)
                p.emit("s", lambda e: e.activation(wT[0:8, :], pT[0:8, :],
                                                   AF.Copy).then_inc(sc.h, 1))
                sc.bump(1)
                wT_done = sc.n
                accv = 0
                for ex in range(8):
                    for mt in range(8):
                        bank, bname = [(pA, "pA"), (pB, "pB"), (pD0, "pD0"), (pD1, "pD1")][mt % 4]
                        s_, v_ = free_at[bname]
                        p.wait("t", sem_of[s_], v_)
                        p.emit("t", lambda e, ex=ex, mt=mt, bank=bank: e.matmul(
                            bank[:], W1[0:64, ex * 1024 + mt * 128: ex * 1024 + (mt + 1) * 128],
                            cont[0:64, :], start=True, stop=True).then_inc(te.h, 1))
                        te.bump(1)
                        p.wait("s", te)
                        p.emit("s", lambda e, ex=ex, mt=mt, bank=bank: e.activation(
                            hfe[:, mt * 512:(mt + 1) * 512], bank[:], AF.Relu,
                            bias=b1c[:, ex * 8 + mt: ex * 8 + mt + 1]).then_inc(sc.h, 1))
                        sc.bump(1)
                        free_at[bname] = ("sc", sc.n)
                    h_done = sc.n
                    p.wait("t", sc, max(h_done, wT_done))
                    if ex > 0:
                        p.wait("t", ve, accv)
                    def y_mm(e, ex=ex):
                        for tt in range(4):
                            for kt in range(8):
                                e.matmul(pY[:, tt * 64:(tt + 1) * 64],
                                         hfe[:, kt * 512 + tt * 128: kt * 512 + (tt + 1) * 128],
                                         W2[:, (ex * 8 + kt) * 64:(ex * 8 + kt + 1) * 64],
                                         start=(kt == 0), stop=False)
                            ins = e.matmul(pY[:, tt * 64:(tt + 1) * 64], ones_col[0:1, 0:128],
                                           b2_row[0:1, ex * 64:(ex + 1) * 64], start=False, stop=True)
                        ins.then_inc(te.h, 1)
                    p.emit("t", y_mm)
                    te.bump(1)
                    p.wait("v", te)
                    def comb(e, ex=ex):
                        for tt in range(4):
                            if ex == 0:
                                ins = e.tensor_scalar(acc_tok[:, tt * 64:(tt + 1) * 64],
                                                      pY[:, tt * 64:(tt + 1) * 64],
                                                      w_s[:, tt * 8 + ex: tt * 8 + ex + 1],
                                                      None, ALU.mult)
                            else:
                                ins = e.scalar_tensor_tensor(acc_tok[:, tt * 64:(tt + 1) * 64],
                                                             pY[:, tt * 64:(tt + 1) * 64],
                                                             w_s[:, tt * 8 + ex: tt * 8 + ex + 1],
                                                             acc_tok[:, tt * 64:(tt + 1) * 64],
                                                             ALU.mult, ALU.add)
                        ins.then_inc(ve.h, 1)
                        e.drain()
                    p.emit("v", comb)
                    ve.bump(1)
                    accv = ve.n
                p.wait("t", ve, accv)
                p.wait("t", sc, wT_done)
                def at_mm(e):
                    for tt in range(4):
                        ins = e.transpose(pT[0:64, tt * 128:(tt + 1) * 128],
                                          acc_tok[:, tt * 64:(tt + 1) * 64], ident[:])
                    ins.then_inc(te.h, 1)
                p.emit("t", at_mm)
                te.bump(1)
                p.wait("s", te)
                p.emit("s", lambda e: e.activation(acc_feat[0:64, :], pT[0:64, :],
                                                   AF.Copy).then_inc(sc.h, 1))
                sc.bump(1)
                p.wait("t", sc)
                for mt in range(16):
                    bank, bname = [(pA, "pA"), (pB, "pB"), (pD0, "pD0"), (pD1, "pD1")][mt % 4]
                    s_, v_ = free_at[bname]
                    p.wait("t", sem_of[s_], v_)
                    p.emit("t", lambda e, mt=mt, bank=bank: e.matmul(
                        bank[:], Wc2s[0:64, mt * 128:(mt + 1) * 128],
                        acc_feat[0:64, :], start=True, stop=True).then_inc(te.h, 1))
                    te.bump(1)
                    p.wait("s", te)
                    p.emit("s", lambda e, mt=mt, bank=bank: e.activation(
                        moe[:, mt * 512:(mt + 1) * 512], bank[:], AF.Sigmoid,
                        bias=bc2s[:, mt:mt + 1]).then_inc(sc.h, 1))
                    sc.bump(1)
                    free_at[bname] = ("sc", sc.n)
                moe_done = sc.n
                for tt in range(4):
                    it = (b * NSLAB + sl) * 4 + tt
                    gb = gout[it % 2][:]
                    p.wait("t", sc, moe_done)
                    for bname in ("pD0", "pD1"):
                        s_, v_ = free_at[bname]
                        p.wait("t", sem_of[s_], v_)
                    if it >= 1:
                        p.wait("t", ve, dec_free)
                    def dec_mm(e, tt=tt):
                        for n2 in range(2):
                            pD = pD0 if n2 == 0 else pD1
                            for kt in range(16):
                                e.matmul(pD[:], moe[:, kt * 512 + tt * 128: kt * 512 + (tt + 1) * 128],
                                         Wdec[:, kt * 1024 + n2 * 512: kt * 1024 + (n2 + 1) * 512],
                                         start=(kt == 0), stop=False)
                            ins = e.matmul(pD[:], ones_col[0:1, 0:128],
                                           bdec_row[0:1, n2 * 512:(n2 + 1) * 512],
                                           start=False, stop=True)
                        ins.then_inc(te.h, 1)
                    p.emit("t", dec_mm)
                    te.bump(1)
                    p.wait("v", te)
                    def dcopy(e):
                        e.tensor_copy(dec_f32[:, 0:512], pD0[:])
                        ins = e.tensor_copy(dec_f32[:, 512:1024], pD1[:])
                        ins.then_inc(ve.h, 1)
                        e.drain()
                    p.emit("v", dcopy)
                    ve.bump(1)
                    dec_free = ve.n
                    free_at["pD0"] = ("ve", ve.n)
                    free_at["pD1"] = ("ve", ve.n)
                    p.wait("s", ve, dec_free)
                    p.emit("s", lambda e: e.activation(sqb[:], dec_f32[:], AF.Square).then_inc(sc.h, 1))
                    sc.bump(1)
                    p.wait("v", sc)
                    cc = ccc
                    def ln_v1(e):
                        e.reduce_sum(cc[:, 8:9], dec_f32[:], axis=AX)
                        e.reduce_sum(cc[:, 9:10], sqb[:], axis=AX)
                        e.drain()
                        e.tensor_scalar(cc[:, 8:9], cc[:, 8:9], 1.0 / D, None, ALU.mult)
                        e.tensor_scalar(cc[:, 9:10], cc[:, 9:10], 1.0 / D, None, ALU.mult)
                        e.drain()
                        e.tensor_tensor(cc[:, 10:11], cc[:, 8:9], cc[:, 8:9], ALU.mult)
                        e.drain()
                        e.tensor_tensor(cc[:, 9:10], cc[:, 9:10], cc[:, 10:11], ALU.subtract)
                        e.drain()
                        ins = e.tensor_scalar(cc[:, 9:10], cc[:, 9:10], 1e-5, None, ALU.add)
                        ins.then_inc(ve.h, 1)
                        e.drain()
                    p.emit("v", ln_v1)
                    ve.bump(1)
                    p.wait("s", ve)
                    p.emit("s", lambda e: e.activation(cc[:, 11:12], cc[:, 9:10], AF.Sqrt).then_inc(sc.h, 1))
                    sc.bump(1)
                    p.wait("v", sc)
                    if it >= 2:
                        p.wait("v", sop[it % 2], so_mark[it - 2])
                    def ln_v2(e, gb=gb):
                        e.reciprocal(cc[:, 12:13], cc[:, 11:12])
                        e.drain()
                        e.tensor_scalar(gb, dec_f32[:], cc[:, 8:9], cc[:, 12:13],
                                        ALU.subtract, ALU.mult)
                        e.drain()
                        e.tensor_tensor(gb, gb, lng_brd[:], ALU.mult)
                        e.drain()
                        ins = e.tensor_tensor(gb, gb, lnb_brd[:], ALU.add)
                        ins.then_inc(ve.h, 1)
                        e.drain()
                    p.emit("v", ln_v2)
                    ve.bump(1)
                    p.wait("y", ve)
                    ti_glob = sl * 4 + tt
                    _dma(p, "y", sop[it % 2], g_out[b, ti_glob * 128:(ti_glob + 1) * 128, :], gb)
                    so_mark[it] = sop[it % 2].n
        p.wait("y", sop[0])
        p.wait("y", sop[1])



def _emit_top2(p, psrc, wdst, lgc, elg, inds, ccc, sc, ve):
    def sm1(e):
        e.tensor_copy(lgc, psrc)
        e.drain()
        e.reduce_max(ccc[:, 0:1], lgc, axis=AX)
        e.drain()
        ins = e.tensor_scalar(lgc, lgc, ccc[:, 0:1], None, ALU.subtract)
        ins.then_inc(ve.h, 1)
        e.drain()
    p.emit("v", sm1)
    ve.bump(1)
    p.wait("s", ve)
    p.emit("s", lambda e: e.activation(elg, lgc, AF.Exp).then_inc(sc.h, 1))
    sc.bump(1)
    p.wait("v", sc)
    def sm2(e):
        e.reduce_sum(ccc[:, 1:2], elg, axis=AX)
        e.drain()
        e.reciprocal(ccc[:, 2:3], ccc[:, 1:2])
        e.drain()
        e.tensor_scalar(lgc, elg, ccc[:, 2:3], None, ALU.mult)
        e.drain()
        e.reduce_max(ccc[:, 3:4], lgc, axis=AX)
        e.drain()
        e.tensor_scalar(inds, lgc, ccc[:, 3:4], None, ALU.is_ge)
        e.drain()
        e.scalar_tensor_tensor(elg, inds, -1e9, lgc, ALU.mult, ALU.add)
        e.drain()
        e.reduce_max(ccc[:, 4:5], elg, axis=AX)
        e.drain()
        e.tensor_scalar(inds, lgc, ccc[:, 4:5], None, ALU.is_ge)
        e.tensor_tensor(ccc[:, 5:6], ccc[:, 3:4], ccc[:, 4:5], ALU.add)
        e.drain()
        e.reciprocal(ccc[:, 6:7], ccc[:, 5:6])
        e.drain()
        ins = e.scalar_tensor_tensor(wdst, lgc, ccc[:, 6:7], inds, ALU.mult, ALU.mult)
        ins.then_inc(ve.h, 1)
        e.drain()
    p.emit("v", sm2)
    ve.bump(1)


def _emit_top2_batch(p, psrc, wdst, ind_dst, scr, sc, ve, ntt):
    W = ntt * 8
    lg = scr[:, 0:W]
    eg = scr[:, W:2 * W]
    cols = scr[:, 2 * W:]
    m0 = cols[:, 0:ntt]
    m1 = cols[:, ntt:2 * ntt]
    m2 = cols[:, 2 * ntt:3 * ntt]
    den = cols[:, 3 * ntt:4 * ntt]
    def seg(ap, c=8):
        return ap.rearrange("p (h c) -> p h c", c=c)
    def bc(ap):
        return ap.rearrange("p (h c) -> p h c", c=1).broadcast_to([128, ntt, 8])
    def sm1(e):
        e.tensor_copy(lg, psrc)
        e.drain()
        e.reduce_max(seg(m0, 1), seg(lg), axis=AX)
        e.drain()
        ins = e.tensor_tensor(seg(lg), seg(lg), bc(m0), ALU.subtract)
        ins.then_inc(ve.h, 1)
        e.drain()
    p.emit("v", sm1)
    ve.bump(1)
    p.wait("s", ve)
    p.emit("s", lambda e: e.activation(eg, lg, AF.Exp).then_inc(sc.h, 1))
    sc.bump(1)
    p.wait("v", sc)
    def sm2(e):
        e.reduce_sum(seg(m0, 1), seg(eg), axis=AX)
        e.drain()
        e.reciprocal(m0, m0)
        e.drain()
        e.tensor_tensor(seg(lg), seg(eg), bc(m0), ALU.mult)
        e.drain()
        e.reduce_max(seg(m1, 1), seg(lg), axis=AX)
        e.drain()
        e.tensor_tensor(seg(eg), seg(lg), bc(m1), ALU.is_ge)
        e.drain()
        e.scalar_tensor_tensor(eg, eg, -1e9, lg, ALU.mult, ALU.add)
        e.drain()
        e.reduce_max(seg(m2, 1), seg(eg), axis=AX)
        e.drain()
        e.tensor_tensor(seg(eg), seg(lg), bc(m2), ALU.is_ge)
        e.tensor_tensor(den, m1, m2, ALU.add)
        e.drain()
        e.reciprocal(den, den)
        e.drain()
        e.tensor_tensor(seg(lg), seg(lg), bc(den), ALU.mult)
        e.drain()
        ins = e.tensor_tensor(wdst, lg, eg, ALU.mult)
        ins.then_inc(ve.h, 1)
        e.drain()
        if ind_dst is not None:
            ins = e.tensor_copy(ind_dst, eg)
            ins.then_inc(ve.h, 1)
            e.drain()
    p.emit("v", sm2)
    ve.bump(1 + (1 if ind_dst is not None else 0))


def prep_l1ffn_host(d):
    f32 = np.float32
    out = {
        "gateW": d["L1_ffn_gate_w"].reshape(8, 128, 8).transpose(1, 0, 2).reshape(128, 64).copy(),
        "Wgu": d["L1_ffn_gate_up_w"].reshape(8, 8, 128, 32, 128).transpose(0, 2, 3, 1, 4).reshape(8 * 128, 32768).copy(),
        "Wdn": d["L1_ffn_down_w"].reshape(8, 16, 128, 1024).transpose(0, 2, 1, 3).reshape(8 * 128, 16384).copy(),
    }
    return {k: np.ascontiguousarray(v.astype(f32)) for k, v in out.items()}


def emit_l1ffn(cx, gw_in, wgu_in, wdn_in, xnT_in, g_out, pfx="l1f"):
    nc, p, E = cx.nc, cx.p, cx.E
    ld, lg, sc, ve, te = cx.ld, cx.lg, cx.sc, cx.ve, cx.te
    sop = cx.sop
    if True:
        xnT = E(nc.sbuf_tensor(pfx + "_xnT_sb", [128, KD * S], BF16))
        gateW = E(nc.sbuf_tensor(pfx + "_gateW", [128, 64], BF16))
        Wgu = E(nc.sbuf_tensor(pfx + "_Wgu", [128, 32768], BF16))
        Wdn = E(nc.sbuf_tensor(pfx + "_Wdn", [128, 16384], BF16))
        hid = E(nc.sbuf_tensor(pfx + "_hid", [128, 16 * 512], BF16))
        tmp = E(nc.sbuf_tensor(pfx + "_tmpf", [128, 512], F32))
        tmp2 = E(nc.sbuf_tensor(pfx + "_tmpf2", [128, 512], F32))
        w_s = E(nc.sbuf_tensor(pfx + "_w_s", [128, 128], F32))
        lgc = E(nc.sbuf_tensor(pfx + "_lgc", [128, 8], F32))
        elg = E(nc.sbuf_tensor(pfx + "_elg", [128, 8], F32))
        inds = E(nc.sbuf_tensor(pfx + "_inds", [128, 8], F32))
        ccc = E(nc.sbuf_tensor(pfx + "_ccc", [128, 16], F32))
        gout = [E(nc.sbuf_tensor(pfx + f"_gout{i}", [128, 1024], F32)) for i in range(2)]
        pA = E(nc.psum_tensor(pfx + "_pA", [128, 512], F32))
        pB = E(nc.psum_tensor(pfx + "_pB", [128, 512], F32))
        pR = E(nc.psum_tensor(pfx + "_pR", [128, 512], F32))
        pD0 = E(nc.psum_tensor(pfx + "_pD0", [128, 512], F32))
        pD1 = E(nc.psum_tensor(pfx + "_pD1", [128, 512], F32))

        _dma(p, "g", lg, gateW[:], gw_in)
        p.wait("g", lg)
        gw_done = lg.n

        so_mark = {}
        NSLAB = 4
        for b in range(ROWS_PER_CORE):
            if b > 0:
                p.wait("y", te)
            for kt in range(KD):
                _dma(p, "y", ld, xnT[:, kt * S:(kt + 1) * S], xnT_in[b, kt])
            p.wait("y", ld)
            xn_done = ld.n
            p.wait("t", ld, xn_done)
            p.wait("t", lg, gw_done)
            if b > 0:
                p.wait("t", ve)
            def rt_mm(e):
                for tt in range(NT):
                    for kt in range(KD):
                        ins = e.matmul(pR[:, tt * 8:(tt + 1) * 8],
                                       xnT[:, kt * S + tt * 128: kt * S + (tt + 1) * 128],
                                       gateW[:, kt * 8:(kt + 1) * 8],
                                       start=(kt == 0), stop=(kt == KD - 1))
                ins.then_inc(te.h, 1)
            p.emit("t", rt_mm)
            te.bump(1)
            p.wait("v", te)
            _emit_top2_batch(p, pR[:, 0:128], w_s[:], None, tmp2[:], sc, ve, NT)
            w_done = ve.n
            pass
            for ex in range(8):
                p.wait("g", te)
                wgu_marks = []
                for ch in range(4):
                    _dma(p, "g", lg, Wgu[:, ch * 8192:(ch + 1) * 8192],
                         wgu_in[ex * 128:(ex + 1) * 128, ch * 8192:(ch + 1) * 8192])
                    p.wait("g", lg)
                    wgu_marks.append(lg.n)
                _dma(p, "g", lg, Wdn[:], wdn_in[ex * 128:(ex + 1) * 128, :])
                p.wait("g", lg)
                w_ready = lg.n
                gsc_mark = {}
                for sl in range(NSLAB):
                    tok0 = sl * 512
                    silu_mark = {}
                    hmul_mark = {}
                    for fm in range(16):
                        bg = pA if fm % 2 == 0 else pD0
                        bu = pB if fm % 2 == 0 else pD1
                        if fm >= 2:
                            p.wait("t", sc, silu_mark[fm - 2])
                            p.wait("t", ve, hmul_mark[fm - 2])
                        else:
                            p.wait("t", ve)
                        p.wait("t", lg, wgu_marks[fm // 4])
                        def gu_mm(e, fm=fm, tok0=tok0, bg=bg, bu=bu):
                            base = fm * 2048
                            for kt in range(KD):
                                e.matmul(bg[:], Wgu[:, base + kt * 128: base + (kt + 1) * 128],
                                         xnT[:, kt * S + tok0: kt * S + tok0 + 512],
                                         start=(kt == 0), stop=(kt == KD - 1))
                            for kt in range(KD):
                                ins = e.matmul(bu[:], Wgu[:, base + 1024 + kt * 128: base + 1024 + (kt + 1) * 128],
                                               xnT[:, kt * S + tok0: kt * S + tok0 + 512],
                                               start=(kt == 0), stop=(kt == KD - 1))
                            ins.then_inc(te.h, 1)
                        p.emit("t", gu_mm)
                        te.bump(1)
                        p.wait("s", te)
                        tb = tmp[:, 0:512] if fm % 2 == 0 else tmp2[:, 0:512]
                        p.emit("s", lambda e, bg=bg, tb=tb: e.activation(tb, bg[:], AF.Silu).then_inc(sc.h, 1))
                        sc.bump(1)
                        silu_mark[fm] = sc.n
                        p.wait("v", sc)
                        def hmul(e, fm=fm, bu=bu, tb=tb):
                            ins = e.tensor_tensor(hid[:, fm * 512:(fm + 1) * 512], tb, bu[:], ALU.mult)
                            ins.then_inc(ve.h, 1)
                        p.emit("v", hmul)
                        ve.bump(1)
                        hmul_mark[fm] = ve.n
                    hid_done = ve.n
                    for tt in range(4):
                        it = ((b * 8 + ex) * NSLAB + sl) * 4 + tt
                        gb = gout[it % 2][:]
                        ti_glob = sl * 4 + tt
                        bd0 = pD0 if tt % 2 == 0 else pA
                        bd1 = pD1 if tt % 2 == 0 else pB
                        p.wait("t", ve, hid_done)
                        p.wait("t", lg, w_ready)
                        if tt >= 2:
                            p.wait("t", ve, gsc_mark[tt - 2])
                        else:
                            p.wait("t", sc, silu_mark[15])
                        def dn_mm(e, tt=tt, bd0=bd0, bd1=bd1):
                            for n2 in range(2):
                                pD = bd0 if n2 == 0 else bd1
                                for kt in range(16):
                                    ins = e.matmul(pD[:], hid[:, kt * 512 + tt * 128: kt * 512 + (tt + 1) * 128],
                                                   Wdn[:, kt * 1024 + n2 * 512: kt * 1024 + (n2 + 1) * 512],
                                                   start=(kt == 0), stop=(kt == 15))
                            ins.then_inc(te.h, 1)
                        p.emit("t", dn_mm)
                        te.bump(1)
                        if it >= 2:
                            p.wait("v", sop[it % 2], so_mark[it - 2])
                        p.wait("v", te)
                        def gsc(e, gb=gb, ti_glob=ti_glob, ex=ex, bd0=bd0, bd1=bd1):
                            e.tensor_scalar(gb[:, 0:512], bd0[:],
                                            w_s[:, ti_glob * 8 + ex: ti_glob * 8 + ex + 1],
                                            None, ALU.mult)
                            ins = e.tensor_scalar(gb[:, 512:1024], bd1[:],
                                                  w_s[:, ti_glob * 8 + ex: ti_glob * 8 + ex + 1],
                                                  None, ALU.mult)
                            ins.then_inc(ve.h, 1)
                            e.drain()
                        p.emit("v", gsc)
                        ve.bump(1)
                        gsc_mark[tt] = ve.n
                        p.wait("g", ve)
                        sem = sop[it % 2]
                        if ex == 0:
                            p.emit("g", lambda e, gb=gb, b=b, t0=ti_glob * 128, s=sem.h:
                                   e.dma_start(out=g_out[b, t0:t0 + 128, :], in_=gb).then_inc(s, 16))
                        else:
                            p.emit("g", lambda e, gb=gb, b=b, t0=ti_glob * 128, s=sem.h:
                                   e.dma_start(out=g_out[b, t0:t0 + 128, :], in_=gb,
                                               accum_op=ALU.add).then_inc(s, 16))
                        sem.bump(16)
                        so_mark[it] = sem.n
        p.wait("y", sop[0])
        p.wait("y", sop[1])



def prep_l1attn_host(d):
    f32 = np.float32
    def wlay(w):
        return w.transpose(1, 0, 2).reshape(1024, 1024).reshape(8, 128, 1024).transpose(1, 0, 2).reshape(128, 8192)
    p_ = np.arange(128)[:, None]
    f_ = np.arange(512)[None, :]
    masks = np.concatenate([(f_ >= p_ + o * 128).astype(f32) for o in range(4)], axis=1)
    out = {
        "WqT": wlay(d["L1_attn_Wq"]),
        "WkT": wlay(d["L1_attn_Wk"]),
        "WvT": wlay(d["L1_attn_Wv"]),
        "WoT": d["L1_attn_Wo"].T.reshape(8, 128, 1024).transpose(1, 0, 2).reshape(128, 8192).copy(),
        "bqc": d["L1_attn_bq"].reshape(1024).reshape(8, 128).T.copy(),
        "bkc": d["L1_attn_bk"].reshape(1024).reshape(8, 128).T.copy(),
        "bv_row": d["L1_attn_bv"].reshape(1, 1024).copy(),
        "bo_row": d["L1_attn_bo"].reshape(1, 1024).copy(),
        "masks": masks,
        "ones_col": np.ones((1, 128), f32),
    }
    return {k: np.ascontiguousarray(v.astype(f32)) for k, v in out.items()}


L1ATTN_SHAPES = [("WqT", [128, 8192]), ("WkT", [128, 8192]),
                 ("WvT", [128, 8192]), ("WoT", [128, 8192]),
                 ("bqc", [128, 8]), ("bkc", [128, 8]),
                 ("bv_row", [1, 1024]), ("bo_row", [1, 1024]),
                 ("masks", [128, 2048]), ("ones_col", [1, 128])]


def emit_l1attn(cx, din, xnT_in, s_out, pfx="l1a"):
    nc, p, E = cx.nc, cx.p, cx.E
    ld, lg, sc, ve, te = cx.ld, cx.lg, cx.sc, cx.ve, cx.te
    sop = cx.sop
    if True:
        QT = E(nc.sbuf_tensor(pfx + "_QT", [128, 8 * S], BF16))
        KT = E(nc.sbuf_tensor(pfx + "_KT", [128, 8 * S], BF16))
        V65 = E(nc.sbuf_tensor(pfx + "_V65", [128, 16 * 1040], BF16))
        xctx = E(nc.sbuf_tensor(pfx + "_xctx", [128, 8 * S], BF16))
        Wst = E(nc.sbuf_tensor(pfx + "_Wst", [128, 8192], BF16))
        masks = E(nc.sbuf_tensor(pfx + "_masks", [128, 2048], F32))
        EST = [E(nc.sbuf_tensor(pfx + f"_EST{i}", [128, 512], BF16)) for i in range(2)]
        bqc = E(nc.sbuf_tensor(pfx + "_bqc", [128, 8], F32))
        bkc = E(nc.sbuf_tensor(pfx + "_bkc", [128, 8], F32))
        bv_row = E(nc.sbuf_tensor(pfx + "_bv_row", [1, 1024], F32))
        bo_row = E(nc.sbuf_tensor(pfx + "_bo_row", [1, 1024], F32))
        ones_col = E(nc.sbuf_tensor(pfx + "_ones_col", [1, 128], F32))
        dinv = E(nc.sbuf_tensor(pfx + "_dinv", [1, 512], F32))
        ctmp = E(nc.sbuf_tensor(pfx + "_ctmp", [128, 512], F32))
        s_stage = [E(nc.sbuf_tensor(pfx + f"_sst{i}", [128, D], F32)) for i in range(2)]
        pP0 = E(nc.psum_tensor(pfx + "_pP0", [128, 512], F32))
        pP1 = E(nc.psum_tensor(pfx + "_pP1", [128, 512], F32))
        pST0 = E(nc.psum_tensor(pfx + "_pST0", [128, 512], F32))
        pST1 = E(nc.psum_tensor(pfx + "_pST1", [128, 512], F32))
        pCTX = E(nc.psum_tensor(pfx + "_pCTX", [128, 512], F32))
        pBRD = E(nc.psum_tensor(pfx + "_pBRD", [128, 512], F32))

        for name, dst in [("bqc", bqc), ("bkc", bkc), ("bv_row", bv_row),
                          ("bo_row", bo_row), ("ones_col", ones_col),
                          ("masks", masks)]:
            _dma(p, "y", ld, dst[:], din[name])
        p.wait("y", ld)
        consts_done = ld.n

        so_mark = {}
        sti = 0
        for b in range(ROWS_PER_CORE):
            if b > 0:
                p.wait("y", ve)
            for kt in range(KD):
                _dma(p, "y", ld, xctx[:, kt * S:(kt + 1) * S], xnT_in[b, kt])
            p.wait("y", ld)
            xn_done = ld.n
            p.wait("t", sc)
            proj_done = {}
            for wi, (wname, bias_mode) in enumerate([("WqT", "q"), ("WkT", "k"), ("WvT", "v")]):
                p.wait("g", te)
                _dma(p, "g", lg, Wst[:], din[wname])
                p.wait("g", lg)
                w_rdy = lg.n
                p.wait("t", lg, w_rdy)
                p.wait("t", ld, xn_done)
                p.wait("t", ld, consts_done)
                if bias_mode in ("q", "k"):
                    dst = QT if bias_mode == "q" else KT
                    bcol = bqc if bias_mode == "q" else bkc
                    for mt in range(8):
                        for nt in range(4):
                            bank = pP0 if (mt * 4 + nt) % 2 == 0 else pP1
                            p.wait("t", ve)
                            def qk_mm(e, mt=mt, nt=nt, bank=bank):
                                for kt in range(KD):
                                    ins = e.matmul(bank[:], Wst[:, kt * 1024 + mt * 128:kt * 1024 + (mt + 1) * 128],
                                                   xctx[:, kt * S + nt * 512:kt * S + (nt + 1) * 512],
                                                   start=(kt == 0), stop=(kt == KD - 1))
                                ins.then_inc(te.h, 1)
                            p.emit("t", qk_mm)
                            te.bump(1)
                            p.wait("v", te)
                            def qk_ev(e, mt=mt, nt=nt, bank=bank, dst=dst, bcol=bcol):
                                ins = e.tensor_scalar(dst[:, mt * S + nt * 512:mt * S + (nt + 1) * 512],
                                                      bank[:], bcol[:, mt:mt + 1], None, ALU.add)
                                ins.then_inc(ve.h, 1)
                            p.emit("v", qk_ev)
                            ve.bump(1)
                else:
                    for ti in range(NT):
                        for n2 in range(2):
                            bank = pP0 if (ti * 2 + n2) % 2 == 0 else pP1
                            p.wait("t", ve)
                            def v_mm(e, ti=ti, n2=n2, bank=bank):
                                for kt in range(KD):
                                    e.matmul(bank[:], xctx[:, kt * S + ti * 128:kt * S + (ti + 1) * 128],
                                             Wst[:, kt * 1024 + n2 * 512:kt * 1024 + (n2 + 1) * 512],
                                             start=(kt == 0), stop=False)
                                ins = e.matmul(bank[:], ones_col[0:1, 0:128],
                                               bv_row[0:1, n2 * 512:(n2 + 1) * 512],
                                               start=False, stop=True)
                                ins.then_inc(te.h, 1)
                            p.emit("t", v_mm)
                            te.bump(1)
                            p.wait("v", te)
                            def v_ev(e, ti=ti, n2=n2, bank=bank):
                                dstv = V65[:, ti * 1040:(ti + 1) * 1040].rearrange(
                                    "p (h c) -> p h c", c=65)[:, n2 * 8:(n2 + 1) * 8, 0:64]
                                srcv = bank[:].rearrange("p (h c) -> p h c", c=64)
                                if n2 == 0:
                                    onev = V65[:, ti * 1040:(ti + 1) * 1040].rearrange(
                                        "p (h c) -> p h c", c=65)[:, :, 64:65]
                                    e.memset(onev, 1.0)
                                ins = e.tensor_copy(dstv, srcv)
                                ins.then_inc(ve.h, 1)
                            p.emit("v", v_ev)
                            ve.bump(1)
            qkv_done_te = te.n
            qkv_done_ve = ve.n
            for h in range(16):
                kt_half = h // 2
                poff = (h % 2) * 64
                for qb in range(4):
                    nkt = qb * 4 + 4
                    est_mark = {}
                    for kt in range(nkt):
                        stb = pST0 if kt % 2 == 0 else pST1
                        eb = EST[kt % 2][:]
                        if kt >= 2:
                            sm, vm = est_mark[kt - 2]
                            p.wait("t", sc, sm)
                            if vm is not None:
                                p.wait("t", ve, vm)
                        elif kt == 0:
                            p.wait("t", sc)
                            p.wait("t", ve)
                        def st_mm(e, h=h, qb=qb, kt=kt, stb=stb, kt_half=kt_half, poff=poff):
                            ins = e.matmul(stb[:], KT[poff:poff + 64, kt_half * S + kt * 128:kt_half * S + (kt + 1) * 128],
                                           QT[poff:poff + 64, kt_half * S + qb * 512:kt_half * S + (qb + 1) * 512],
                                           start=True, stop=True)
                            ins.then_inc(te.h, 1)
                        p.emit("t", st_mm)
                        te.bump(1)
                        p.wait("s", te)
                        p.emit("s", lambda e, eb=eb, stb=stb: e.activation(
                            eb, stb[:], AF.Exp, scale=0.125).then_inc(sc.h, 1))
                        sc.bump(1)
                        diag_o = kt - qb * 4
                        if diag_o >= 0:
                            p.wait("v", sc)
                            def msk(e, eb=eb, diag_o=diag_o):
                                ins = e.tensor_tensor(eb, eb, masks[:, diag_o * 512:(diag_o + 1) * 512], ALU.mult)
                                ins.then_inc(ve.h, 1)
                                e.drain()
                            p.emit("v", msk)
                            ve.bump(1)
                            est_mark[kt] = (sc.n, ve.n)
                            p.wait("t", ve)
                        else:
                            est_mark[kt] = (sc.n, None)
                            p.wait("t", sc)
                        def cx_mm(e, h=h, kt=kt, eb=eb, first=(kt == 0), last=(kt == nkt - 1)):
                            ins = e.matmul(pCTX[0:65, :], V65[:, kt * 1040 + h * 65:kt * 1040 + h * 65 + 65],
                                           eb, start=first, stop=last)
                            if last:
                                ins.then_inc(te.h, 1)
                        p.emit("t", cx_mm)
                        if kt == nkt - 1:
                            te.bump(1)
                    p.wait("v", te)
                    def dv1(e):
                        ins = e.reciprocal(dinv[0:1, :], pCTX[64:65, :])
                        ins.then_inc(ve.h, 1)
                        e.drain()
                    p.emit("v", dv1)
                    ve.bump(1)
                    p.wait("t", ve)
                    def brd_mm(e):
                        ins = e.matmul(pBRD[0:64, :], ones_col[0:1, 0:64], dinv[0:1, :],
                                       start=True, stop=True)
                        ins.then_inc(te.h, 1)
                    p.emit("t", brd_mm)
                    te.bump(1)
                    p.wait("s", te)
                    p.emit("s", lambda e: e.activation(ctmp[0:64, :], pCTX[0:64, :],
                                                       AF.Copy).then_inc(sc.h, 1))
                    sc.bump(1)
                    p.wait("v", sc)
                    def cdiv(e, h=h, qb=qb, kt_half=kt_half, poff=poff):
                        ins = e.tensor_tensor(
                            xctx[poff:poff + 64, kt_half * S + qb * 512:kt_half * S + (qb + 1) * 512],
                            ctmp[0:64, :], pBRD[0:64, :], ALU.mult)
                        ins.then_inc(ve.h, 1)
                        e.drain()
                    p.emit("v", cdiv)
                    ve.bump(1)
                    p.wait("t", ve)
            ctx_done_ve = ve.n
            p.wait("g", te)
            _dma(p, "g", lg, Wst[:], din["WoT"])
            p.wait("g", lg)
            wo_rdy = lg.n
            p.wait("t", lg, wo_rdy)
            p.wait("t", ve, ctx_done_ve)
            p.wait("t", sc)
            for ti in range(NT):
                sb = s_stage[ti % 2][:]
                for n2 in range(2):
                    bank = pP0 if n2 == 0 else pP1
                    p.wait("t", ve)
                    def wo_mm(e, ti=ti, n2=n2, bank=bank):
                        for kt in range(KD):
                            e.matmul(bank[:], xctx[:, kt * S + ti * 128:kt * S + (ti + 1) * 128],
                                     Wst[:, kt * 1024 + n2 * 512:kt * 1024 + (n2 + 1) * 512],
                                     start=(kt == 0), stop=False)
                        ins = e.matmul(bank[:], ones_col[0:1, 0:128],
                                       bo_row[0:1, n2 * 512:(n2 + 1) * 512],
                                       start=False, stop=True)
                        ins.then_inc(te.h, 1)
                    p.emit("t", wo_mm)
                    te.bump(1)
                if sti >= 2:
                    p.wait("v", sop[sti % 2], so_mark[sti - 2])
                p.wait("v", te)
                def s_ev(e, sb=sb):
                    e.tensor_copy(sb[:, 0:512], pP0[:])
                    ins = e.tensor_copy(sb[:, 512:1024], pP1[:])
                    ins.then_inc(ve.h, 1)
                    e.drain()
                p.emit("v", s_ev)
                ve.bump(1)
                p.wait("y", ve)
                _dma(p, "y", sop[sti % 2], s_out[b, ti * 128:(ti + 1) * 128, :], sb)
                so_mark[sti] = sop[sti % 2].n
                sti += 1
        p.wait("y", sop[0])
        p.wait("y", sop[1])



def prep_l0attn_host(d):
    f32 = np.float32
    def lay(wT):
        return wT.reshape(8, 128, 1024).transpose(1, 0, 2).reshape(128, 8192)
    p_ = np.arange(128)[:, None]
    f_ = np.arange(128)[None, :]
    out = {
        "gateWT": d["L0_attn_gate_w"].T.reshape(8, 128, 8).transpose(1, 0, 2).reshape(128, 64).copy(),
        "qwT": lay(d["L0_attn_q_w"].T),
        "kwT": np.stack([lay(d["L0_attn_k_w"][e].T) for e in range(8)]).reshape(8 * 128, 8192),
        "vwT": np.stack([lay(d["L0_attn_v_w"][e].T) for e in range(8)]).reshape(8 * 128, 8192),
        "fmq1T": d["L0_attn_fmq1_w"].T.copy(),
        "fmq2T": d["L0_attn_fmq2_w"].T.copy(),
        "fmk1T": d["L0_attn_fmk1_w"].T.copy(),
        "fmk2T": d["L0_attn_fmk2_w"].T.copy(),
        "normw_row": d["L0_attn_norm_w"].reshape(1, 128).copy(),
        "owT": lay(d["L0_attn_o_w"].T),
        "diagmask4": np.tile((f_ >= p_).astype(f32), (1, 4)),
        "ones_col": np.ones((1, 128), f32),
    }
    return {k: np.ascontiguousarray(v.astype(f32)) for k, v in out.items()}


L0ATTN_SHAPES = [("gateWT", [128, 64]), ("qwT", [128, 8192]),
                 ("kwT", [1024, 8192]), ("vwT", [1024, 8192]),
                 ("fmq1T", [128, 128]), ("fmq2T", [128, 128]),
                 ("fmk1T", [128, 128]), ("fmk2T", [128, 128]),
                 ("normw_row", [1, 128]), ("owT", [128, 8192]),
                 ("diagmask4", [128, 512]), ("ones_col", [1, 128])]


def emit_l0attn(cx, din, xnT_in, s_out, v_spill, pfx="l0a"):
    nc, p, E = cx.nc, cx.p, cx.E
    ld, lg, sc, ve, te, tr = cx.ld, cx.lg, cx.sc, cx.ve, cx.te, cx.tr
    sop = cx.sop
    H = 8
    if True:
        xnT = E(nc.sbuf_tensor(pfx + "_xnT_sb", [128, KD * S], BF16))
        qp = E(nc.sbuf_tensor(pfx + "_qp", [128, H * S], BF16))
        kpf = E(nc.sbuf_tensor(pfx + "_kpf", [128, H * S], BF16))
        kpt = E(nc.sbuf_tensor(pfx + "_kpt", [128, NT * D], BF16))
        oac = E(nc.sbuf_tensor(pfx + "_oac", [128, NT * D], BF16))
        Wkv = E(nc.sbuf_tensor(pfx + "_Wkv", [128, 8192], BF16))
        vsl = E(nc.sbuf_tensor(pfx + "_vsl", [128, NT * 512], BF16))
        qf = [vsl[:, 0:S], vsl[:, S:2 * S]]
        vst = [E(nc.sbuf_tensor(pfx + f"_vst{i}", [128, D], BF16)) for i in range(2)]
        aT = [E(nc.sbuf_tensor(pfx + f"_aT{i}", [128, 512], BF16)) for i in range(1)]
        Ssb = [E(nc.sbuf_tensor(pfx + f"_Ssb{i}", [128, 128], BF16)) for i in range(4)]
        gateWT = E(nc.sbuf_tensor(pfx + "_gateWT", [128, 64], BF16))
        fmw = {n: E(nc.sbuf_tensor(pfx + "_" + n, [128, 128], BF16))
               for n in ("fmq1T", "fmq2T", "fmk1T", "fmk2T")}
        diagmask4 = E(nc.sbuf_tensor(pfx + "_diagmask4", [128, 512], BF16))
        normw_row = E(nc.sbuf_tensor(pfx + "_normw_row", [1, 128], F32))
        normw_brd = E(nc.sbuf_tensor(pfx + "_normw_brd", [128, 128], F32))
        ones_col = E(nc.sbuf_tensor(pfx + "_ones_col", [1, 128], F32))
        w_s = E(nc.sbuf_tensor(pfx + "_w_s", [128, NT * 8], F32))
        ind_s = E(nc.sbuf_tensor(pfx + "_ind_s", [128, NT * 8], F32))
        lgc = E(nc.sbuf_tensor(pfx + "_lgc", [128, 8], F32))
        elg = E(nc.sbuf_tensor(pfx + "_elg", [128, 8], F32))
        indc = E(nc.sbuf_tensor(pfx + "_indc", [128, 8], F32))
        ccc = E(nc.sbuf_tensor(pfx + "_ccc", [128, 16], F32))
        tmpf = E(nc.sbuf_tensor(pfx + "_tmpf", [128, 512], F32))
        sqf = E(nc.sbuf_tensor(pfx + "_sqf", [128, 128], F32))
        s_stage = [E(nc.sbuf_tensor(pfx + f"_sst{i}", [128, D], BF16)) for i in range(1)]
        pA = E(nc.psum_tensor(pfx + "_pA", [128, 512], F32))
        pB = E(nc.psum_tensor(pfx + "_pB", [128, 512], F32))
        pAT = E(nc.psum_tensor(pfx + "_pAT", [128, 512], F32))
        pO = E(nc.psum_tensor(pfx + "_pO", [128, 512], F32))
        pS = [E(nc.psum_tensor(pfx + f"_pS{i}", [128, 512], F32)) for i in range(4)]

        banks = [pA, pB, pAT, pO, pS[0], pS[1]]
        bfree = {i: ("v", 0) for i in range(6)}
        semmap = {"v": ve, "s": sc}

        def bank_wait(i):
            s_, v_ = bfree[i]
            p.wait("t", semmap[s_], v_)

        for name, dst in [("normw_row", normw_row),
                          ("ones_col", ones_col)]:
            _dma(p, "y", ld, dst[:], din[name])
        p.wait("y", ld)
        for name, dst in [("diagmask4", diagmask4), ("gateWT", gateWT), ("fmq1T", fmw["fmq1T"]),
                          ("fmq2T", fmw["fmq2T"]), ("fmk1T", fmw["fmk1T"]),
                          ("fmk2T", fmw["fmk2T"])]:
            _dma(p, "g", lg, dst[:], din[name])
        p.wait("g", lg)
        consts_y = ld.n
        consts_g = lg.n
        p.wait("t", ld, consts_y)
        p.emit("t", lambda e: e.matmul(pA[:, 0:128], ones_col[0:1, :], normw_row[0:1, :],
                                       start=True, stop=True).then_inc(te.h, 1))
        te.bump(1)
        p.wait("v", te)
        p.emit("v", lambda e: e.tensor_copy(normw_brd[:], pA[:, 0:128]).then_inc(ve.h, 1))
        ve.bump(1)
        bfree[0] = ("v", ve.n)

        so_mark = {}
        sti = 0
        for b in range(ROWS_PER_CORE):
            if b > 0:
                p.wait("y", te)
            for kt in range(KD):
                _dma(p, "y", ld, xnT[:, kt * S:(kt + 1) * S], xnT_in[b, kt])
            p.wait("y", ld)
            xn_done = ld.n
            p.wait("t", ld, xn_done)
            p.wait("t", lg, consts_g)
            bank_wait(0)
            def rt_mm(e):
                for ti in range(NT):
                    for kt in range(KD):
                        ins = e.matmul(pA[:, ti * 8:(ti + 1) * 8],
                                       xnT[:, kt * S + ti * 128:kt * S + (ti + 1) * 128],
                                       gateWT[:, kt * 8:(kt + 1) * 8],
                                       start=(kt == 0), stop=(kt == KD - 1))
                ins.then_inc(te.h, 1)
            p.emit("t", rt_mm)
            te.bump(1)
            p.wait("v", te)
            _emit_top2_batch(p, pA[:, 0:128], w_s[:], ind_s[:], tmpf[:], sc, ve, NT)
            bfree[0] = ("v", ve.n)
            p.wait("g", te)
            _dma(p, "g", lg, Wkv[:], din["qwT"])
            p.wait("g", lg)
            p.wait("t", lg)
            def emit_fm(h, dst, fm1, fm2, scale):
                bq = qf[h % 2]
                for nt in range(4):
                    bank_wait(2)
                    bank_wait(3)
                    def fmm(e, nt=nt, bq=bq):
                        e.matmul(banks[2][:], fmw[fm1][:], bq[:, nt * 512:(nt + 1) * 512],
                                 start=True, stop=True)
                        ins = e.matmul(banks[3][:], fmw[fm2][:], bq[:, nt * 512:(nt + 1) * 512],
                                       start=True, stop=True)
                        ins.then_inc(te.h, 1)
                    p.emit("t", fmm)
                    te.bump(1)
                    p.wait("s", te)
                    if scale is None:
                        p.emit("s", lambda e: e.activation(tmpf[:], banks[2][:],
                                                           AF.Copy).then_inc(sc.h, 1))
                    else:
                        p.emit("s", lambda e, sca=scale: e.activation(
                            tmpf[:], banks[2][:], AF.Copy, scale=sca).then_inc(sc.h, 1))
                    sc.bump(1)
                    bfree[2] = ("s", sc.n)
                    p.wait("v", sc)
                    def fme(e, h=h, nt=nt, dst=dst):
                        ins = e.tensor_tensor(dst[:, h * S + nt * 512:h * S + (nt + 1) * 512],
                                              tmpf[:], banks[3][:], ALU.mult)
                        ins.then_inc(ve.h, 1)
                        e.drain()
                    p.emit("v", fme)
                    ve.bump(1)
                    bfree[3] = ("v", ve.n)
                fm_done[h] = ve.n

            def emit_projfm(dst, fm1, fm2, scale):
                for h in range(H):
                    bq = qf[h % 2]
                    if h >= 2:
                        p.wait("t", ve, fm_done[h - 2])
                    for np2 in range(2):
                        bi0 = (0, 1, 4, 5)[np2 * 2]
                        bi1 = (0, 1, 4, 5)[np2 * 2 + 1]
                        bank_wait(bi0)
                        bank_wait(bi1)
                        def prj(e, h=h, np2=np2, bk0=banks[bi0], bk1=banks[bi1]):
                            for nt, bk in ((np2 * 2, bk0), (np2 * 2 + 1, bk1)):
                                for kt in range(KD):
                                    ins = e.matmul(bk[:], Wkv[:, kt * 1024 + h * 128:kt * 1024 + (h + 1) * 128],
                                                   xnT[:, kt * S + nt * 512:kt * S + (nt + 1) * 512],
                                                   start=(kt == 0), stop=(kt == KD - 1))
                            ins.then_inc(te.h, 1)
                        p.emit("t", prj)
                        te.bump(1)
                        p.wait("v", te)
                        def prj_ev(e, np2=np2, bk0=banks[bi0], bk1=banks[bi1], bq=bq):
                            e.tensor_copy(bq[:, (np2 * 2) * 512:(np2 * 2 + 1) * 512], bk0[:])
                            ins = e.tensor_copy(bq[:, (np2 * 2 + 1) * 512:(np2 * 2 + 2) * 512], bk1[:])
                            ins.then_inc(ve.h, 1)
                        p.emit("v", prj_ev)
                        ve.bump(1)
                        bfree[bi0] = ("v", ve.n)
                        bfree[bi1] = ("v", ve.n)
                    if h >= 1:
                        emit_fm(h - 1, dst, fm1, fm2, scale)
                emit_fm(H - 1, dst, fm1, fm2, scale)
            fm_done = {}
            emit_projfm(qp[:], "fmq1T", "fmq2T", 0.08838834764831845)
            for ex in range(8):
                p.wait("g", te)
                p.wait("g", tr)
                _dma(p, "g", lg, Wkv[:], din["kwT"][ex * 128:(ex + 1) * 128, :])
                p.wait("g", lg)
                p.wait("t", lg)
                fm_done = {}
                emit_projfm(kpf[:], "fmk1T", "fmk2T", None)
                kp_done = ve.n
                p.wait("g", te)
                _dma(p, "g", lg, Wkv[:], din["vwT"][ex * 128:(ex + 1) * 128, :])
                p.wait("g", lg)
                p.wait("t", lg)
                p.wait("y", ve, kp_done)
                p.wait("y", lg)
                p.wait("y", sop[0])
                p.wait("y", sop[1])
                spill_mark = {}
                for ti in range(NT):
                    bv = vst[ti % 2][:]
                    if ti >= 2:
                        p.wait("v", ld, spill_mark[ti - 2])
                    b0 = (0, 1, 4, 5)[(ti * 2) % 4]
                    b1 = (0, 1, 4, 5)[(ti * 2 + 1) % 4]
                    bank_wait(b0)
                    bank_wait(b1)
                    def v_mm(e, ti=ti, bk0=banks[b0], bk1=banks[b1]):
                        for n2, bk in ((0, bk0), (1, bk1)):
                            for kt in range(KD):
                                ins = e.matmul(bk[:], xnT[:, kt * S + ti * 128:kt * S + (ti + 1) * 128],
                                               Wkv[:, kt * 1024 + n2 * 512:kt * 1024 + (n2 + 1) * 512],
                                               start=(kt == 0), stop=(kt == KD - 1))
                        ins.then_inc(te.h, 1)
                    p.emit("t", v_mm)
                    te.bump(1)
                    p.wait("v", te)
                    def v_ev(e, ti=ti, bk0=banks[b0], bk1=banks[b1], bv=bv, ex=ex):
                        e.tensor_scalar(vsl[:, ti * 512:(ti + 1) * 512], bk0[:],
                                        ind_s[:, ti * 8 + ex:ti * 8 + ex + 1],
                                        None, ALU.mult)
                        ins = e.tensor_scalar(bv[:, 0:512], bk1[:],
                                              ind_s[:, ti * 8 + ex:ti * 8 + ex + 1],
                                              None, ALU.mult)
                        ins.then_inc(ve.h, 1)
                        e.drain()
                    p.emit("v", v_ev)
                    ve.bump(1)
                    bfree[b0] = ("v", ve.n)
                    bfree[b1] = ("v", ve.n)
                    p.wait("y", ve)
                    _dma(p, "y", ld, v_spill[ti * 128:(ti + 1) * 128, :], bv[:, 0:512])
                    spill_mark[ti] = ld.n
                    if ti % 2 == 1 and ti // 2 < H:
                        kd = ti // 2
                        p.wait("y", ld)
                        inv = kpf[:, kd * S:(kd + 1) * S].rearrange("p (t c) -> p t c", c=128)
                        outv = kpt[:, :].rearrange("p (t c) -> p t c", c=D)[:, :, kd * 128:(kd + 1) * 128]
                        p.emit("y", lambda e, i=inv, o=outv: e.dma_start_transpose(o, i).then_inc(tr.h, 16))
                        tr.bump(16)
                        p.wait("y", tr)
                        if kd == 3:
                            tr_half = tr.n
                tr_done = tr.n
                for hg in (0, 4):
                    p.wait("t", tr, tr_half if hg == 0 else tr_done)
                    if hg > 0:
                        p.wait("y", ve)
                        p.wait("y", te)
                        vslv = vsl[:].rearrange("p (t c) -> p t c", c=512)
                        vsp = v_spill.rearrange("(t p) d -> p t d", p=128)
                        _dma(p, "y", ld, vslv, vsp)
                        p.wait("y", ld)
                        vsl_rdy = ld.n
                        p.wait("t", ld, vsl_rdy)
                    snap_mark = {}
                    oev_mark = {}
                    atev_mark = {}
                    for c in range(NT):
                        atb = aT[0][:]
                        patc = pAT if c % 2 == 0 else pB
                        poc = pO if c % 2 == 0 else pA
                        if c == 0:
                            p.wait("t", ve)
                            p.wait("t", sc)
                        if c >= 2:
                            p.wait("t", ve, atev_mark[c - 2])
                        def at_mm(e, hg=hg, c=c, patc=patc):
                            for j in range(4):
                                h = hg + j
                                ins = e.matmul(patc[:, j * 128:(j + 1) * 128],
                                               kpf[:, h * S + c * 128:h * S + (c + 1) * 128],
                                               qp[:, h * S + c * 128:h * S + (c + 1) * 128],
                                               start=True, stop=True,
                                               skip_group_check=True)
                            ins.then_inc(te.h, 1)
                        p.emit("t", at_mm)
                        te.bump(1)
                        p.wait("v", te)
                        def at_ev(e, atb=atb, patc=patc):
                            ins = e.tensor_tensor(atb, patc[:], diagmask4[:], ALU.mult)
                            ins.then_inc(ve.h, 1)
                            e.drain()
                        p.emit("v", at_ev)
                        ve.bump(1)
                        atev_mark[c] = ve.n
                        p.wait("t", ve, atev_mark[c])
                        if c >= 1:
                            p.wait("t", sc, snap_mark[c - 1][0])
                            p.wait("t", ve, snap_mark[c - 1][1])
                        if c >= 2:
                            p.wait("t", ve, oev_mark[c - 2])
                        def oi_mm(e, hg=hg, c=c, atb=atb, poc=poc):
                            for j in range(4):
                                h = hg + j
                                ins = e.matmul(pS[j][:, 0:128],
                                               kpt[:, c * D + h * 128:c * D + (h + 1) * 128],
                                               vsl[:, c * 512 + j * 128:c * 512 + (j + 1) * 128],
                                               start=(c == 0), stop=True,
                                               skip_group_check=True)
                            ins.then_inc(te.h, 1)
                            for j in range(4):
                                h = hg + j
                                ins = e.matmul(poc[:, j * 128:(j + 1) * 128],
                                               atb[:, j * 128:(j + 1) * 128],
                                               vsl[:, c * 512 + j * 128:c * 512 + (j + 1) * 128],
                                               start=True, stop=(c == 0),
                                               skip_group_check=True)
                                if c > 0:
                                    ins = e.matmul(poc[:, j * 128:(j + 1) * 128],
                                                   qp[:, h * S + c * 128:h * S + (c + 1) * 128],
                                                   Ssb[j][:], start=False, stop=True,
                                                   skip_group_check=True)
                            ins.then_inc(te.h, 1)
                        p.emit("t", oi_mm)
                        te.bump(2)
                        p.wait("s", te)
                        def snap_s(e):
                            e.activation(Ssb[0][:], pS[0][:, 0:128], AF.Copy)
                            ins = e.activation(Ssb[1][:], pS[1][:, 0:128], AF.Copy)
                            ins.then_inc(sc.h, 1)
                        p.emit("s", snap_s)
                        sc.bump(1)
                        p.wait("v", te)
                        def snap_v(e):
                            e.tensor_copy(Ssb[2][:], pS[2][:, 0:128])
                            ins = e.tensor_copy(Ssb[3][:], pS[3][:, 0:128])
                            ins.then_inc(ve.h, 1)
                            e.drain()
                        p.emit("v", snap_v)
                        ve.bump(1)
                        snap_mark[c] = (sc.n, ve.n)
                        p.wait("v", te)
                        def o_ev(e, hg=hg, c=c, ex=ex, poc=poc):
                            wc = w_s[:, c * 8 + ex:c * 8 + ex + 1]
                            dst = oac[:, c * D + hg * 128:c * D + hg * 128 + 512]
                            if ex == 0:
                                ins = e.tensor_scalar(dst, poc[:], wc, None, ALU.mult)
                            else:
                                ins = e.scalar_tensor_tensor(dst, poc[:], wc, dst,
                                                             ALU.mult, ALU.add)
                            ins.then_inc(ve.h, 1)
                            e.drain()
                        p.emit("v", o_ev)
                        ve.bump(1)
                        oev_mark[c] = ve.n
                for _bi in range(6):
                    bfree[_bi] = ("v", ve.n)
            sq_al = vsl[:, 0:1024]
            for ti in range(NT):
                tile = oac[:, ti * D:(ti + 1) * D]
                tile3 = tile.rearrange("p (h c) -> p h c", c=128)
                p.wait("s", ve)
                p.emit("s", lambda e, tile=tile, sq_al=sq_al: e.activation(
                    sq_al, tile, AF.Square).then_inc(sc.h, 1))
                sc.bump(1)
                p.wait("v", sc)
                def rms1(e, sq_al=sq_al):
                    sq3 = sq_al.rearrange("p (h c) -> p h c", c=128)
                    e.reduce_sum(ccc[:, 0:8].rearrange("p (h c) -> p h c", c=1),
                                 sq3, axis=AX)
                    e.drain()
                    ins = e.tensor_scalar(ccc[:, 0:8], ccc[:, 0:8], 1.0 / 128, 1e-5,
                                          ALU.mult, ALU.add)
                    ins.then_inc(ve.h, 1)
                    e.drain()
                p.emit("v", rms1)
                ve.bump(1)
                p.wait("s", ve)
                p.emit("s", lambda e: e.activation(ccc[:, 8:16], ccc[:, 0:8],
                                                   AF.Sqrt).then_inc(sc.h, 1))
                sc.bump(1)
                p.wait("v", sc)
                def rms2(e, tile=tile, tile3=tile3):
                    e.reciprocal(ccc[:, 0:8], ccc[:, 8:16])
                    e.drain()
                    rb = ccc[:, 0:8].rearrange("p (h c) -> p h c", c=1).broadcast_to([128, 8, 128])
                    e.tensor_tensor(tile3, tile3, rb, ALU.mult)
                    e.drain()
                    nb = normw_brd[:].rearrange("(q p) c -> p q c", q=1).broadcast_to([128, 8, 128]) if False else \
                         normw_brd[:].rearrange("p (q c) -> p q c", q=1).broadcast_to([128, 8, 128])
                    ins = e.tensor_tensor(tile3, tile3, nb, ALU.mult)
                    ins.then_inc(ve.h, 1)
                    e.drain()
                p.emit("v", rms2)
                ve.bump(1)
            onorm_done = ve.n
            p.wait("y", ve, onorm_done)
            p.wait("y", lg)
            p.wait("y", ld)
            p.wait("y", sop[0])
            p.wait("y", sop[1])
            for ti in range(NT):
                inv = oac[:, ti * D:(ti + 1) * D]
                outv = kpf[:, :].rearrange("p (kd s) -> p kd s", s=S)[:, :, ti * 128:(ti + 1) * 128]
                p.emit("y", lambda e, i=inv, o=outv: e.dma_start_transpose(o, i).then_inc(tr.h, 16))
                tr.bump(16)
                p.wait("y", tr)
            tr2_done = tr.n
            p.wait("g", tr, tr2_done)
            p.wait("g", te)
            _dma(p, "g", lg, Wkv[:], din["owT"])
            p.wait("g", lg)
            p.wait("t", lg)
            p.wait("t", tr, tr2_done)
            for ti in range(NT):
                sb = s_stage[0][:]
                for n2 in range(2):
                    bi = (0, 1, 4, 5)[(ti * 2 + n2) % 4]
                    bank_wait(bi)
                    def ow_mm(e, ti=ti, n2=n2, bk=banks[bi]):
                        for kt in range(KD):
                            ins = e.matmul(bk[:], kpf[:, kt * S + ti * 128:kt * S + (ti + 1) * 128],
                                           Wkv[:, kt * 1024 + n2 * 512:kt * 1024 + (n2 + 1) * 512],
                                           start=(kt == 0), stop=(kt == KD - 1))
                        ins.then_inc(te.h, 1)
                    p.emit("t", ow_mm)
                    te.bump(1)
                if sti >= 1:
                    p.wait("v", lg, so_mark[sti - 1])
                p.wait("v", te)
                b0 = (0, 1, 4, 5)[(ti * 2) % 4]
                b1 = (0, 1, 4, 5)[(ti * 2 + 1) % 4]
                def s_ev(e, sb=sb, b0=b0, b1=b1):
                    e.tensor_copy(sb[:, 0:512], banks[b0][:])
                    ins = e.tensor_copy(sb[:, 512:1024], banks[b1][:])
                    ins.then_inc(ve.h, 1)
                    e.drain()
                p.emit("v", s_ev)
                ve.bump(1)
                bfree[b0] = ("v", ve.n)
                bfree[b1] = ("v", ve.n)
                p.wait("g", ve)
                _dma(p, "g", lg, s_out[b, ti * 128:(ti + 1) * 128, :], sb)
                sti += 1
                so_mark[sti - 1] = lg.n
        p.wait("y", sop[0])
        p.wait("y", sop[1])



def _split_rows(a):
    return [np.ascontiguousarray(a[c * 2:(c + 1) * 2]) for c in range(N_CORES)]


def build_one_stage(which):
    import contextlib
    nc = bass.Bass("TRN2", target_bir_lowering=False)
    cx = Ctx()
    cx.nc = nc
    xnT_in = nc.dram_tensor("xnT", [ROWS_PER_CORE, KD, 128, S], BF16, kind="ExternalInput").ap()
    s_out = nc.dram_tensor("s0", [ROWS_PER_CORE, S, D], F32, kind="ExternalOutput").ap()
    v_spill = nc.dram_tensor("i_vsp", [S, 512], BF16, kind="Internal").ap()
    din = {n: nc.dram_tensor("in_" + n, sh, F32, kind="ExternalInput").ap()
           for n, sh in L0ATTN_SHAPES}
    with contextlib.ExitStack() as gst:
        for nm in ("ld", "lg", "te", "sc", "ve", "tr"):
            setattr(cx, nm, Sem(gst.enter_context(nc.semaphore(nm))))
        cx.sop = [Sem(gst.enter_context(nc.semaphore("sop0"))),
                  Sem(gst.enter_context(nc.semaphore("sop1")))]
        cx.p = Prog(nc)
        with contextlib.ExitStack() as sst:
            cx.E = sst.enter_context
            emit_l0attn(cx, din, xnT_in, s_out, v_spill, "l0a")
            with nc.Block() as blk:
                cx.p.run(blk)
    return nc


class Ctx:
    pass


def build_fused():
    import contextlib
    nc = bass.Bass("TRN2", target_bir_lowering=False)
    x_in = nc.dram_tensor("x", [ROWS_PER_CORE, S, D], F32, kind="ExternalInput").ap()
    id_in = nc.dram_tensor("ident_in", [128, 128], F32, kind="ExternalInput").ap()
    out_t = nc.dram_tensor("out", [ROWS_PER_CORE, S, D], F32, kind="ExternalOutput").ap()
    din_l0a = {n: nc.dram_tensor("w0a_" + n, sh, F32, kind="ExternalInput").ap()
               for n, sh in L0ATTN_SHAPES}
    din_l0f = {n: nc.dram_tensor("w0f_" + n, sh, F32, kind="ExternalInput").ap()
               for n, sh in L0FFN_SHAPES}
    din_l1a = {n: nc.dram_tensor("w1a_" + n, sh, F32, kind="ExternalInput").ap()
               for n, sh in L1ATTN_SHAPES}
    gw_in = nc.dram_tensor("w1f_gateW", [128, 64], F32, kind="ExternalInput").ap()
    wgu_in = nc.dram_tensor("w1f_Wgu", [8 * 128, 32768], F32, kind="ExternalInput").ap()
    wdn_in = nc.dram_tensor("w1f_Wdn", [8 * 128, 16384], F32, kind="ExternalInput").ap()
    xnT0 = nc.dram_tensor("i_xnT0", [ROWS_PER_CORE, KD, 128, S], BF16, kind="Internal").ap()
    xnT1 = nc.dram_tensor("i_xnT1", [ROWS_PER_CORE, KD, 128, S], BF16, kind="Internal").ap()
    s0_t = nc.dram_tensor("i_s0", [ROWS_PER_CORE, S, D], F32, kind="Internal").ap()
    g0_t = nc.dram_tensor("i_g0", [ROWS_PER_CORE, S, D], F32, kind="Internal").ap()
    x1_t = nc.dram_tensor("i_x1", [ROWS_PER_CORE, S, D], F32, kind="Internal").ap()
    s1_t = nc.dram_tensor("i_s1", [ROWS_PER_CORE, S, D], F32, kind="Internal").ap()
    g1_t = nc.dram_tensor("i_g1", [ROWS_PER_CORE, S, D], F32, kind="Internal").ap()
    v_spill = nc.dram_tensor("i_vsp", [S, 512], BF16, kind="Internal").ap()

    cx = Ctx()
    cx.nc = nc
    with contextlib.ExitStack() as gst:
        for nm in ("ld", "lg", "te", "sc", "ve", "tr"):
            setattr(cx, nm, Sem(gst.enter_context(nc.semaphore(nm))))
        cx.sop = [Sem(gst.enter_context(nc.semaphore("sop0"))),
                  Sem(gst.enter_context(nc.semaphore("sop1")))]
        bar = Sem(gst.enter_context(nc.semaphore("bar")))

        def stage(fn):
            cx.p = Prog(nc)
            with contextlib.ExitStack() as sst:
                cx.E = sst.enter_context
                if bar.n > 0:
                    for eng in ("t", "v", "s", "g", "y"):
                        cx.p.wait(eng, bar)
                fn(cx)
                for s_ in (cx.ld, cx.lg, cx.te, cx.sc, cx.ve, cx.tr,
                           cx.sop[0], cx.sop[1]):
                    cx.p.wait("y", s_)
                cx.p.emit("y", lambda e, h=bar.h: e.sem_inc(h, 1))
                bar.bump(1)
                with nc.Block() as blk:
                    cx.p.run(blk)

        stage(lambda cx: emit_pre(cx, x_in, id_in, xnT0, "pre"))
        stage(lambda cx: emit_l0attn(cx, din_l0a, xnT0, s0_t, v_spill, "l0a"))
        stage(lambda cx: emit_l0ffn(cx, din_l0f, xnT0, g0_t, "l0f"))
        stage(lambda cx: emit_residual(cx, True, x_in, s0_t, g0_t, x1_t, xnT1,
                                       id_in, "mid"))
        stage(lambda cx: emit_l1attn(cx, din_l1a, xnT1, s1_t, "l1a"))
        stage(lambda cx: emit_l1ffn(cx, gw_in, wgu_in, wdn_in, xnT1, g1_t, "l1f"))
        stage(lambda cx: emit_residual(cx, False, x1_t, s1_t, g1_t, out_t,
                                       None, None, "fin"))
    return nc


_EYE = None

def kernel(**inputs):
    global _EYE
    if _EYE is None:
        _EYE = np.eye(128, dtype=np.float32)
    x = inputs["L0_attn_x"]
    xs = _split_rows(x)
    cores = list(range(N_CORES))
    wmap = {}
    for k, v in prep_l0attn_host(inputs).items():
        wmap["w0a_" + k] = v
    for k, v in prep_l0ffn_host(inputs).items():
        wmap["w0f_" + k] = v
    for k, v in prep_l1attn_host(inputs).items():
        wmap["w1a_" + k] = v
    for k, v in prep_l1ffn_host(inputs).items():
        wmap["w1f_" + k] = v
    nc = build_fused()
    in_maps = [dict(x=xs[c], ident_in=_EYE, **wmap) for c in cores]
    r = run_bass_kernel_spmd(nc, in_maps, core_ids=cores)
    return np.concatenate([np.asarray(r.results[c]["out"]) for c in cores], axis=0)


# version 23 (final)
# speedup vs baseline: 1.0261x; 1.0188x over previous
import numpy as np
import ml_dtypes
import concourse.bass as bass
import concourse.mybir as mybir
from concourse.bass_utils import run_bass_kernel_spmd

F32 = mybir.dt.float32
BF16 = mybir.dt.bfloat16
AF = mybir.ActivationFunctionType
ALU = mybir.AluOpType
AX = mybir.AxisListType.X
BF = ml_dtypes.bfloat16

N_CORES = 8
ROWS_PER_CORE = 2
S = 2048
D = 1024
NT = S // 128
KD = D // 128



class Sem:
    def __init__(self, handle):
        self.h = handle
        self.n = 0

    def bump(self, k):
        self.n += k
        return self.n


class Prog:
    ENGS = ("t", "v", "s", "g", "y")

    def __init__(self, nc):
        self.nc = nc
        self.q = {e: [] for e in self.ENGS}

    def emit(self, eng, fn):
        self.q[eng].append(fn)

    def wait(self, eng, sem, val=None):
        v = sem.n if val is None else val
        self.q[eng].append(lambda e, s=sem.h, v=v: e.wait_ge(s, v))

    def run(self, block):
        if self.q["g"]:
            @block.gpsimd
            def _(eng):
                for fn in self.q["g"]:
                    fn(eng)
        if self.q["t"]:
            @block.tensor
            def _(eng):
                for fn in self.q["t"]:
                    fn(eng)
        if self.q["s"]:
            @block.scalar
            def _(eng):
                for fn in self.q["s"]:
                    fn(eng)
        if self.q["v"]:
            @block.vector
            def _(eng):
                for fn in self.q["v"]:
                    fn(eng)
        if self.q["y"]:
            @block.sync
            def _(eng):
                for fn in self.q["y"]:
                    fn(eng)


def _dma(p, eng, sem, out, in_):
    p.emit(eng, lambda e, o=out, i=in_, s=sem.h: e.dma_start(out=o, in_=i).then_inc(s, 16))
    return sem.bump(16)


def _emit_ln_tile(p, x_f32, xn_bf, scr, sc, ve, eps=1e-5, gcol=None, bcol=None,
                  out_extra=None):
    nc = p.nc
    sq, c0, c1, c2, c3, c4 = scr["sq"], scr["c0"], scr["c1"], scr["c2"], scr["c3"], scr["c4"]
    p.emit("s", lambda e: e.activation(sq, x_f32, AF.Square).then_inc(sc.h, 1))
    sc.bump(1)
    p.wait("v", sc)
    def vchain(e):
        e.reduce_sum(c0, x_f32, axis=AX)
        e.reduce_sum(c1, sq, axis=AX)
        e.drain()
        e.tensor_scalar(c0, c0, 1.0 / D, None, ALU.mult)
        e.tensor_scalar(c1, c1, 1.0 / D, None, ALU.mult)
        e.drain()
        e.tensor_tensor(c2, c0, c0, ALU.mult)
        e.drain()
        e.tensor_tensor(c1, c1, c2, ALU.subtract)
        e.drain()
        e.tensor_scalar(c1, c1, eps, None, ALU.add).then_inc(ve.h, 1)
    p.emit("v", vchain)
    ve.bump(1)
    p.wait("s", ve)
    p.emit("s", lambda e: e.activation(c3, c1, AF.Sqrt).then_inc(sc.h, 1))
    sc.bump(1)
    p.wait("v", sc)
    def vnorm(e):
        e.reciprocal(c4, c3)
        e.drain()
        e.tensor_scalar(xn_bf, x_f32, c0, c4, ALU.subtract, ALU.mult).then_inc(ve.h, 1)
    p.emit("v", vnorm)
    ve.bump(1)



def emit_pre(cx, x_in, id_in, xnT_out, pfx="pre"):
    nc, p, E = cx.nc, cx.p, cx.E
    ld, sc, ve, te, so = cx.ld, cx.sc, cx.ve, cx.te, cx.sop[0]
    lg, so2 = cx.lg, cx.sop[1]
    if True:
        xt = [E(nc.sbuf_tensor(pfx + f"_xt{i}", [128, D], F32)) for i in range(2)]
        xnt = E(nc.sbuf_tensor(pfx + "_xnt", [128, D], F32))
        sq = E(nc.sbuf_tensor(pfx + "_sq", [128, D], F32))
        cols = [E(nc.sbuf_tensor(pfx + f"_c{i}", [128, 1], F32)) for i in range(5)]
        ident = E(nc.sbuf_tensor(pfx + "_ident", [128, 128], F32))
        xnT_sb = E(nc.sbuf_tensor(pfx + "_xnTsb", [128, KD * S], BF16))
        pt = [E(nc.psum_tensor(pfx + f"_pt{i}", [128, 512], F32)) for i in range(2)]
        scr = {"sq": sq[:], "c0": cols[0][:], "c1": cols[1][:], "c2": cols[2][:],
               "c3": cols[3][:], "c4": cols[4][:]}

        _dma(p, "y", ld, ident[:], id_in)
        p.wait("y", ld)
        p.wait("t", ld)
        ve0 = ve.n

        for b in range(ROWS_PER_CORE):
            for ti in range(NT):
                it = b * NT + ti
                buf = xt[it % 2][:]
                qe = "y" if it % 2 == 0 else "g"
                qs = ld if it % 2 == 0 else lg
                if it >= 2:
                    p.wait(qe, ve, ve0 + (it - 2) * 3 + 2)
                _dma(p, qe, qs, buf, x_in[b, ti * 128:(ti + 1) * 128, :])
                p.wait(qe, qs)
                p.wait("s", qs)
                p.wait("v", qs)
                _emit_ln_tile(p, buf, xnt[:], scr, sc, ve)
                p.wait("t", ve)
                def tchain(e):
                    for kt in range(KD):
                        bank = pt[kt // 4]
                        ins = e.transpose(bank[:, (kt % 4) * 128:(kt % 4 + 1) * 128],
                                          xnt[:, kt * 128:(kt + 1) * 128], ident[:])
                        if kt == KD - 1:
                            ins.then_inc(te.h, 1)
                p.emit("t", tchain)
                te.bump(1)
                if ti == 0 and b > 0:
                    p.wait("v", so)
                    p.wait("v", so2)
                p.wait("v", te)
                def evac(e, ti=ti):
                    for g in range(2):
                        for j in range(4):
                            kt = g * 4 + j
                            ins = e.tensor_copy(
                                xnT_sb[:, kt * S + ti * 128: kt * S + (ti + 1) * 128],
                                pt[g][:, j * 128:(j + 1) * 128])
                    ins.then_inc(ve.h, 1)
                p.emit("v", evac)
                ve.bump(1)
                p.wait("t", ve)
            p.wait("y", ve)
            p.wait("g", ve)
            for kt in range(KD):
                if kt % 2 == 0:
                    _dma(p, "y", so, xnT_out[b, kt], xnT_sb[:, kt * S:(kt + 1) * S])
                else:
                    _dma(p, "g", so2, xnT_out[b, kt], xnT_sb[:, kt * S:(kt + 1) * S])
            p.wait("y", so)
            p.wait("g", so2)
        p.wait("y", so)
        p.wait("y", so2)



def emit_residual(cx, with_ln, x_in, s_in, g_in, x1_out, xnT_out=None, id_in=None, pfx="res"):
    nc, p, E = cx.nc, cx.p, cx.E
    ld, sc, ve, te, so = cx.ld, cx.sc, cx.ve, cx.te, cx.sop[0]
    lg, so2 = cx.lg, cx.sop[1]
    if True:
        xt = [E(nc.sbuf_tensor(pfx + f"_xt{i}", [128, D], F32)) for i in range(2)]
        stt = [E(nc.sbuf_tensor(pfx + f"_st{i}", [128, D], F32)) for i in range(2)]
        gt = [E(nc.sbuf_tensor(pfx + f"_gt{i}", [128, D], F32)) for i in range(2)]
        gs2 = [E(nc.sbuf_tensor(pfx + f"_gs{i}", [128, D], F32)) for i in range(2)]
        x1t = [E(nc.sbuf_tensor(pfx + f"_x1t{i}", [128, D], F32)) for i in range(2)]
        if with_ln:
            xnt = E(nc.sbuf_tensor(pfx + "_xnt", [128, D], F32))
            sq = E(nc.sbuf_tensor(pfx + "_sq", [128, D], F32))
            cols = [E(nc.sbuf_tensor(pfx + f"_c{i}", [128, 1], F32)) for i in range(5)]
            ident = E(nc.sbuf_tensor(pfx + "_ident", [128, 128], F32))
            xnT_sb = E(nc.sbuf_tensor(pfx + "_xnTsb", [128, KD * S], BF16))
            pt = [E(nc.psum_tensor(pfx + f"_pt{i}", [128, 512], F32)) for i in range(2)]
            scr = {"sq": sq[:], "c0": cols[0][:], "c1": cols[1][:], "c2": cols[2][:],
                   "c3": cols[3][:], "c4": cols[4][:]}
        if with_ln:
            _dma(p, "y", ld, ident[:], id_in)
            p.wait("y", ld)
            p.wait("t", ld)

        ve_per = 1 + (3 if with_ln else 0)
        so_after_tile = {}
        ve0 = ve.n
        for b in range(ROWS_PER_CORE):
            for ti in range(NT):
                it = b * NT + ti
                bx, bs, bg, bo = xt[it % 2][:], stt[it % 2][:], gt[it % 2][:], x1t[it % 2][:]
                sl = slice(ti * 128, (ti + 1) * 128)
                gbuf = gs2[it % 2][:]
                if it >= 2:
                    p.wait("y", ve, ve0 + (it - 2) * ve_per + ve_per)
                    p.wait("g", ve, ve0 + (it - 2) * ve_per + ve_per)

                _dma(p, "y", ld, bx, x_in[b, sl, :])
                _dma(p, "g", lg, bs, s_in[b, sl, :])
                _dma(p, "g", lg, bg, g_in[b, sl, :])
                p.wait("y", ld)
                p.wait("g", lg)
                if it >= 2:
                    p.wait("s", ve, ve0 + (it - 2) * ve_per + 1)
                p.wait("s", lg)
                p.emit("s", lambda e, g=bg, gb=gbuf: e.activation(gb, g, AF.Silu).then_inc(sc.h, 1))
                sc.bump(1)
                p.wait("v", sc)
                p.wait("v", ld)
                p.wait("v", lg)
                if it >= 2:
                    p.wait("v", so, so_after_tile[it - 2])
                def resid(e, bs=bs, bx=bx, bo=bo, gb=gbuf):
                    e.tensor_tensor(gb, bs, gb, ALU.mult)
                    e.drain()
                    e.tensor_tensor(bo, bx, gb, ALU.add).then_inc(ve.h, 1)
                    e.drain()
                p.emit("v", resid)
                ve.bump(1)
                p.wait("y", ve)
                _dma(p, "y", so, x1_out[b, sl, :], bo)
                so_after_tile[it] = so.n
                if with_ln:
                    if ti == 0 and b > 0:
                        p.wait("v", so)
                        p.wait("v", so2)
                    p.wait("s", ve)
                    _emit_ln_tile(p, bo, xnt[:], scr, sc, ve)
                    p.wait("t", ve)
                    def tchain(e):
                        for kt in range(KD):
                            bank = pt[kt // 4]
                            ins = e.transpose(bank[:, (kt % 4) * 128:(kt % 4 + 1) * 128],
                                              xnt[:, kt * 128:(kt + 1) * 128], ident[:])
                            if kt == KD - 1:
                                ins.then_inc(te.h, 1)
                    p.emit("t", tchain)
                    te.bump(1)
                    p.wait("v", te)
                    def evac(e, ti=ti):
                        for g in range(2):
                            for j in range(4):
                                kt = g * 4 + j
                                ins = e.tensor_copy(
                                    xnT_sb[:, kt * S + ti * 128: kt * S + (ti + 1) * 128],
                                    pt[g][:, j * 128:(j + 1) * 128])
                        ins.then_inc(ve.h, 1)
                    p.emit("v", evac)
                    ve.bump(1)
                    p.wait("t", ve)
            if with_ln:
                p.wait("y", ve)
                p.wait("g", ve)
                for kt in range(KD):
                    if kt % 2 == 0:
                        _dma(p, "y", so, xnT_out[b, kt], xnT_sb[:, kt * S:(kt + 1) * S])
                    else:
                        _dma(p, "g", so2, xnT_out[b, kt], xnT_sb[:, kt * S:(kt + 1) * S])
        p.wait("y", so)
        p.wait("y", so2)



def prep_l0ffn_host(d):
    f32 = np.float32
    out = {
        "Wenc": d["L0_ffn_W_enc"].reshape(8, 128, 2048).transpose(1, 0, 2).reshape(128, 16384).copy(),
        "benc": d["L0_ffn_b_enc"].reshape(16, 128).T.copy(),
        "Ws2c": d["L0_ffn_W_s2c"].reshape(16, 128, 64).transpose(1, 0, 2).reshape(128, 1024).copy(),
        "bs2c": d["L0_ffn_b_s2c"].reshape(64, 1).copy(),
        "Wr": d["L0_ffn_Wr"].copy(),
        "br_row": d["L0_ffn_br"].reshape(1, 8).copy(),
        "W1": d["L0_ffn_W1"].transpose(1, 0, 2).reshape(64, 8192).copy(),
        "b1c": d["L0_ffn_b1"].reshape(8, 8, 128).transpose(2, 0, 1).reshape(128, 64).copy(),
        "W2": d["L0_ffn_W2"].reshape(8, 8, 128, 64).transpose(2, 0, 1, 3).reshape(128, 4096).copy(),
        "b2_row": d["L0_ffn_b2"].reshape(1, 512).copy(),
        "Wc2s": d["L0_ffn_W_c2s"].copy(),
        "bc2s": d["L0_ffn_b_c2s"].reshape(16, 128).T.copy(),
        "Wdec": d["L0_ffn_W_dec"].reshape(16, 128, 1024).transpose(1, 0, 2).reshape(128, 16384).copy(),
        "bdec_row": d["L0_ffn_b_dec"].reshape(1, 1024).copy(),
        "lng_row": d["L0_ffn_ln_g"].reshape(1, 1024).copy(),
        "lnb_row": d["L0_ffn_ln_b"].reshape(1, 1024).copy(),
        "ones_col": np.ones((1, 128), f32),
        "ident_in": np.eye(128, dtype=f32),
    }
    return {k: np.ascontiguousarray(v.astype(f32)) for k, v in out.items()}


L0FFN_SHAPES = [("Wenc", [128, 16384]), ("benc", [128, 16]),
                ("Ws2c", [128, 1024]), ("bs2c", [64, 1]),
                ("Wr", [64, 8]), ("br_row", [1, 8]),
                ("W1", [64, 8192]), ("b1c", [128, 64]),
                ("W2", [128, 4096]), ("b2_row", [1, 512]),
                ("Wc2s", [64, 2048]), ("bc2s", [128, 16]),
                ("Wdec", [128, 16384]), ("bdec_row", [1, 1024]),
                ("lng_row", [1, 1024]), ("lnb_row", [1, 1024]),
                ("ones_col", [1, 128]), ("ident_in", [128, 128])]


def emit_l0ffn(cx, din, xnT_in, g_out, pfx="l0f"):
    nc, p, E = cx.nc, cx.p, cx.E
    ld, lg, sc, ve, te = cx.ld, cx.lg, cx.sc, cx.ve, cx.te
    sop = cx.sop
    if True:
        Wenc = E(nc.sbuf_tensor(pfx + "_Wenc", [128, 16384], BF16))
        Ws2c = E(nc.sbuf_tensor(pfx + "_Ws2c", [128, 1024], BF16))
        Wr = E(nc.sbuf_tensor(pfx + "_Wr", [64, 8], BF16))
        W1 = E(nc.sbuf_tensor(pfx + "_W1", [64, 8192], BF16))
        W2 = E(nc.sbuf_tensor(pfx + "_W2", [128, 4096], BF16))
        Wc2s = E(nc.sbuf_tensor(pfx + "_Wc2s", [64, 2048], BF16))
        Wdec = E(nc.sbuf_tensor(pfx + "_Wdec", [128, 16384], BF16))
        benc = E(nc.sbuf_tensor(pfx + "_benc", [128, 16], F32))
        bs2c = E(nc.sbuf_tensor(pfx + "_bs2c", [64, 1], F32))
        br_row = E(nc.sbuf_tensor(pfx + "_br_row", [1, 8], F32))
        b1c = E(nc.sbuf_tensor(pfx + "_b1c", [128, 64], F32))
        b2_row = E(nc.sbuf_tensor(pfx + "_b2_row", [1, 512], F32))
        bc2s = E(nc.sbuf_tensor(pfx + "_bc2s", [128, 16], F32))
        bdec_row = E(nc.sbuf_tensor(pfx + "_bdec_row", [1, 1024], F32))
        lng_row = E(nc.sbuf_tensor(pfx + "_lng_row", [1, 1024], F32))
        lnb_row = E(nc.sbuf_tensor(pfx + "_lnb_row", [1, 1024], F32))
        ones_col = E(nc.sbuf_tensor(pfx + "_ones_col", [1, 128], F32))
        ident = E(nc.sbuf_tensor(pfx + "_ident", [128, 128], F32))
        lng_brd = E(nc.sbuf_tensor(pfx + "_lng_brd", [128, 1024], F32))
        lnb_brd = E(nc.sbuf_tensor(pfx + "_lnb_brd", [128, 1024], F32))
        xnT = E(nc.sbuf_tensor(pfx + "_xnT_sb", [128, KD * 512], BF16))
        enc = E(nc.sbuf_tensor(pfx + "_enc", [128, 16 * 512], BF16))
        cont = E(nc.sbuf_tensor(pfx + "_cont", [128, 512], BF16))
        hfe = E(nc.sbuf_tensor(pfx + "_hfe", [128, 8 * 512], BF16))
        moe = E(nc.sbuf_tensor(pfx + "_moe", [128, 16 * 512], BF16))
        acc_tok = E(nc.sbuf_tensor(pfx + "_acc_tok", [128, 256], F32))
        acc_feat = E(nc.sbuf_tensor(pfx + "_acc_feat", [128, 512], BF16))
        wT = E(nc.sbuf_tensor(pfx + "_wT", [128, 512], F32))
        w_s = E(nc.sbuf_tensor(pfx + "_w_s", [128, 32], F32))
        lgc = E(nc.sbuf_tensor(pfx + "_lgc", [128, 8], F32))
        elg = E(nc.sbuf_tensor(pfx + "_elg", [128, 8], F32))
        inds = E(nc.sbuf_tensor(pfx + "_inds", [128, 8], F32))
        ccc = E(nc.sbuf_tensor(pfx + "_ccc", [128, 16], F32))
        dec_f32 = E(nc.sbuf_tensor(pfx + "_dec_f32", [128, 1024], F32))
        sqb = E(nc.sbuf_tensor(pfx + "_sqb", [128, 1024], F32))
        gout = [E(nc.sbuf_tensor(pfx + f"_gout{i}", [128, 1024], F32)) for i in range(2)]
        pA = E(nc.psum_tensor(pfx + "_pA", [128, 512], F32))
        pB = E(nc.psum_tensor(pfx + "_pB", [128, 512], F32))
        pR = E(nc.psum_tensor(pfx + "_pR", [128, 512], F32))
        pY = E(nc.psum_tensor(pfx + "_pY", [128, 512], F32))
        pD0 = E(nc.psum_tensor(pfx + "_pD0", [128, 512], F32))
        pD1 = E(nc.psum_tensor(pfx + "_pD1", [128, 512], F32))
        pT = E(nc.psum_tensor(pfx + "_pT", [128, 512], F32))

        for name, dst in [("Wenc", Wenc), ("Ws2c", Ws2c), ("Wr", Wr), ("W1", W1),
                          ("W2", W2), ("Wc2s", Wc2s), ("Wdec", Wdec)]:
            _dma(p, "g", lg, dst[:], din[name])
        for name, dst in [("benc", benc), ("bs2c", bs2c), ("br_row", br_row),
                          ("b1c", b1c), ("b2_row", b2_row), ("bc2s", bc2s),
                          ("bdec_row", bdec_row), ("lng_row", lng_row),
                          ("lnb_row", lnb_row), ("ones_col", ones_col),
                          ("ident_in", ident)]:
            _dma(p, "y", ld, dst[:], din[name])
        p.wait("y", ld)
        p.wait("g", lg)
        ld_w = ld.n
        p.wait("t", ld, ld_w)
        p.wait("t", lg)
        for i, (row, dst) in enumerate([(lng_row, lng_brd), (lnb_row, lnb_brd)]):
            for n2 in range(2):
                p.emit("t", lambda e, r=row, n2=n2: e.matmul(
                    pT[:], ones_col[0:1, :], r[0:1, n2 * 512:(n2 + 1) * 512],
                    start=True, stop=True).then_inc(te.h, 1))
                te.bump(1)
                p.wait("v", te)
                p.emit("v", lambda e, d=dst, n2=n2: e.tensor_copy(
                    d[:, n2 * 512:(n2 + 1) * 512], pT[:]).then_inc(ve.h, 1))
                ve.bump(1)
                p.wait("t", ve)

        free_at = {"pA": ("sc", 0), "pB": ("sc", 0), "pD0": ("sc", 0), "pD1": ("sc", 0)}
        sem_of = {"sc": sc, "ve": ve}
        so_mark = {}

        NSLAB = 4
        for b in range(ROWS_PER_CORE):
            for sl in range(NSLAB):
                tok0 = sl * 512
                if b + sl > 0:
                    p.wait("y", te)
                for kt in range(KD):
                    _dma(p, "y", ld, xnT[:, kt * 512:(kt + 1) * 512],
                         xnT_in[b, kt][:, tok0:tok0 + 512])
                p.wait("y", ld)
                p.wait("t", ld)
                for mt in range(16):
                    bank, bname = [(pA, "pA"), (pB, "pB"), (pD0, "pD0"), (pD1, "pD1")][mt % 4]
                    s_, v_ = free_at[bname]
                    p.wait("t", sem_of[s_], v_)
                    def enc_mm(e, mt=mt, bank=bank):
                        for kt in range(KD):
                            ins = e.matmul(bank[:], Wenc[:, kt * 2048 + mt * 128: kt * 2048 + (mt + 1) * 128],
                                           xnT[:, kt * 512:(kt + 1) * 512],
                                           start=(kt == 0), stop=(kt == KD - 1))
                        ins.then_inc(te.h, 1)
                    p.emit("t", enc_mm)
                    te.bump(1)
                    p.wait("s", te)
                    p.emit("s", lambda e, mt=mt, bank=bank: e.activation(
                        enc[:, mt * 512:(mt + 1) * 512], bank[:], AF.Sigmoid,
                        bias=benc[:, mt:mt + 1]).then_inc(sc.h, 1))
                    sc.bump(1)
                    free_at[bname] = ("sc", sc.n)
                p.wait("t", sc)
                p.wait("t", ve)
                def cont_mm(e):
                    for kt in range(16):
                        ins = e.matmul(pR[0:64, 0:512], Ws2c[:, kt * 64:(kt + 1) * 64],
                                       enc[:, kt * 512:(kt + 1) * 512],
                                       start=(kt == 0), stop=(kt == 15))
                    ins.then_inc(te.h, 1)
                p.emit("t", cont_mm)
                te.bump(1)
                p.wait("v", te)
                p.emit("v", lambda e: e.tensor_scalar(cont[0:64, :], pR[0:64, 0:512],
                                                      bs2c[0:64, :], None, ALU.add).then_inc(ve.h, 1))
                ve.bump(1)
                cont_done = ve.n
                p.wait("t", ve, cont_done)
                def rt_mm(e):
                    for tt in range(4):
                        e.matmul(pY[:, tt * 8:(tt + 1) * 8], cont[0:64, tt * 128:(tt + 1) * 128],
                                 Wr[0:64, :], start=True, stop=False)
                        ins = e.matmul(pY[:, tt * 8:(tt + 1) * 8], ones_col[0:1, 0:128],
                                       br_row[0:1, :], start=False, stop=True)
                    ins.then_inc(te.h, 1)
                p.emit("t", rt_mm)
                te.bump(1)
                p.wait("v", te)
                for tt in range(4):
                    def rt_sm1(e, tt=tt):
                        e.tensor_copy(lgc[:], pY[:, tt * 8:(tt + 1) * 8])
                        e.drain()
                        e.reduce_max(ccc[:, 0:1], lgc[:], axis=AX)
                        e.drain()
                        ins = e.tensor_scalar(lgc[:], lgc[:], ccc[:, 0:1], None, ALU.subtract)
                        ins.then_inc(ve.h, 1)
                        e.drain()
                    p.emit("v", rt_sm1)
                    ve.bump(1)
                    p.wait("s", ve)
                    p.emit("s", lambda e: e.activation(elg[:], lgc[:], AF.Exp).then_inc(sc.h, 1))
                    sc.bump(1)
                    p.wait("v", sc)
                    def rt_sm2(e, tt=tt):
                        e.reduce_sum(ccc[:, 1:2], elg[:], axis=AX)
                        e.drain()
                        e.reciprocal(ccc[:, 2:3], ccc[:, 1:2])
                        e.drain()
                        e.tensor_scalar(lgc[:], elg[:], ccc[:, 2:3], None, ALU.mult)
                        e.drain()
                        e.reduce_max(ccc[:, 3:4], lgc[:], axis=AX)
                        e.drain()
                        e.tensor_scalar(inds[:], lgc[:], ccc[:, 3:4], None, ALU.is_ge)
                        e.drain()
                        e.scalar_tensor_tensor(elg[:], inds[:], -1e9, lgc[:], ALU.mult, ALU.add)
                        e.drain()
                        e.reduce_max(ccc[:, 4:5], elg[:], axis=AX)
                        e.drain()
                        e.tensor_scalar(inds[:], lgc[:], ccc[:, 4:5], None, ALU.is_ge)
                        e.tensor_tensor(ccc[:, 5:6], ccc[:, 3:4], ccc[:, 4:5], ALU.add)
                        e.drain()
                        e.reciprocal(ccc[:, 6:7], ccc[:, 5:6])
                        e.drain()
                        ins = e.scalar_tensor_tensor(w_s[:, tt * 8:(tt + 1) * 8], lgc[:],
                                                     ccc[:, 6:7], inds[:], ALU.mult, ALU.mult)
                        ins.then_inc(ve.h, 1)
                        e.drain()
                    p.emit("v", rt_sm2)
                    ve.bump(1)
                w_done = ve.n
                p.wait("t", ve, w_done)
                p.wait("t", sc)
                def wt_mm(e):
                    for tt in range(4):
                        ins = e.transpose(pT[0:8, tt * 128:(tt + 1) * 128],
                                          w_s[:, tt * 8:(tt + 1) * 8], ident[:])
                    ins.then_inc(te.h, 1)
                p.emit("t", wt_mm)
                te.bump(1)
                p.wait("s", te)
                p.emit("s", lambda e: e.activation(wT[0:8, :], pT[0:8, :],
                                                   AF.Copy).then_inc(sc.h, 1))
                sc.bump(1)
                wT_done = sc.n
                accv = 0
                for ex in range(8):
                    for mt in range(8):
                        bank, bname = [(pA, "pA"), (pB, "pB"), (pD0, "pD0"), (pD1, "pD1")][mt % 4]
                        s_, v_ = free_at[bname]
                        p.wait("t", sem_of[s_], v_)
                        p.emit("t", lambda e, ex=ex, mt=mt, bank=bank: e.matmul(
                            bank[:], W1[0:64, ex * 1024 + mt * 128: ex * 1024 + (mt + 1) * 128],
                            cont[0:64, :], start=True, stop=True).then_inc(te.h, 1))
                        te.bump(1)
                        p.wait("s", te)
                        p.emit("s", lambda e, ex=ex, mt=mt, bank=bank: e.activation(
                            hfe[:, mt * 512:(mt + 1) * 512], bank[:], AF.Relu,
                            bias=b1c[:, ex * 8 + mt: ex * 8 + mt + 1]).then_inc(sc.h, 1))
                        sc.bump(1)
                        free_at[bname] = ("sc", sc.n)
                    h_done = sc.n
                    p.wait("t", sc, max(h_done, wT_done))
                    if ex > 0:
                        p.wait("t", ve, accv)
                    def y_mm(e, ex=ex):
                        for tt in range(4):
                            for kt in range(8):
                                e.matmul(pY[:, tt * 64:(tt + 1) * 64],
                                         hfe[:, kt * 512 + tt * 128: kt * 512 + (tt + 1) * 128],
                                         W2[:, (ex * 8 + kt) * 64:(ex * 8 + kt + 1) * 64],
                                         start=(kt == 0), stop=False)
                            ins = e.matmul(pY[:, tt * 64:(tt + 1) * 64], ones_col[0:1, 0:128],
                                           b2_row[0:1, ex * 64:(ex + 1) * 64], start=False, stop=True)
                        ins.then_inc(te.h, 1)
                    p.emit("t", y_mm)
                    te.bump(1)
                    p.wait("v", te)
                    def comb(e, ex=ex):
                        for tt in range(4):
                            if ex == 0:
                                ins = e.tensor_scalar(acc_tok[:, tt * 64:(tt + 1) * 64],
                                                      pY[:, tt * 64:(tt + 1) * 64],
                                                      w_s[:, tt * 8 + ex: tt * 8 + ex + 1],
                                                      None, ALU.mult)
                            else:
                                ins = e.scalar_tensor_tensor(acc_tok[:, tt * 64:(tt + 1) * 64],
                                                             pY[:, tt * 64:(tt + 1) * 64],
                                                             w_s[:, tt * 8 + ex: tt * 8 + ex + 1],
                                                             acc_tok[:, tt * 64:(tt + 1) * 64],
                                                             ALU.mult, ALU.add)
                        ins.then_inc(ve.h, 1)
                        e.drain()
                    p.emit("v", comb)
                    ve.bump(1)
                    accv = ve.n
                p.wait("t", ve, accv)
                p.wait("t", sc, wT_done)
                def at_mm(e):
                    for tt in range(4):
                        ins = e.transpose(pT[0:64, tt * 128:(tt + 1) * 128],
                                          acc_tok[:, tt * 64:(tt + 1) * 64], ident[:])
                    ins.then_inc(te.h, 1)
                p.emit("t", at_mm)
                te.bump(1)
                p.wait("s", te)
                p.emit("s", lambda e: e.activation(acc_feat[0:64, :], pT[0:64, :],
                                                   AF.Copy).then_inc(sc.h, 1))
                sc.bump(1)
                p.wait("t", sc)
                for mt in range(16):
                    bank, bname = [(pA, "pA"), (pB, "pB"), (pD0, "pD0"), (pD1, "pD1")][mt % 4]
                    s_, v_ = free_at[bname]
                    p.wait("t", sem_of[s_], v_)
                    p.emit("t", lambda e, mt=mt, bank=bank: e.matmul(
                        bank[:], Wc2s[0:64, mt * 128:(mt + 1) * 128],
                        acc_feat[0:64, :], start=True, stop=True).then_inc(te.h, 1))
                    te.bump(1)
                    p.wait("s", te)
                    p.emit("s", lambda e, mt=mt, bank=bank: e.activation(
                        moe[:, mt * 512:(mt + 1) * 512], bank[:], AF.Sigmoid,
                        bias=bc2s[:, mt:mt + 1]).then_inc(sc.h, 1))
                    sc.bump(1)
                    free_at[bname] = ("sc", sc.n)
                moe_done = sc.n
                for tt in range(4):
                    it = (b * NSLAB + sl) * 4 + tt
                    gb = gout[it % 2][:]
                    p.wait("t", sc, moe_done)
                    for bname in ("pD0", "pD1"):
                        s_, v_ = free_at[bname]
                        p.wait("t", sem_of[s_], v_)
                    if it >= 1:
                        p.wait("t", ve, dec_free)
                    def dec_mm(e, tt=tt):
                        for n2 in range(2):
                            pD = pD0 if n2 == 0 else pD1
                            for kt in range(16):
                                e.matmul(pD[:], moe[:, kt * 512 + tt * 128: kt * 512 + (tt + 1) * 128],
                                         Wdec[:, kt * 1024 + n2 * 512: kt * 1024 + (n2 + 1) * 512],
                                         start=(kt == 0), stop=False)
                            ins = e.matmul(pD[:], ones_col[0:1, 0:128],
                                           bdec_row[0:1, n2 * 512:(n2 + 1) * 512],
                                           start=False, stop=True)
                        ins.then_inc(te.h, 1)
                    p.emit("t", dec_mm)
                    te.bump(1)
                    p.wait("v", te)
                    def dcopy(e):
                        e.tensor_copy(dec_f32[:, 0:512], pD0[:])
                        ins = e.tensor_copy(dec_f32[:, 512:1024], pD1[:])
                        ins.then_inc(ve.h, 1)
                        e.drain()
                    p.emit("v", dcopy)
                    ve.bump(1)
                    dec_free = ve.n
                    free_at["pD0"] = ("ve", ve.n)
                    free_at["pD1"] = ("ve", ve.n)
                    p.wait("s", ve, dec_free)
                    p.emit("s", lambda e: e.activation(sqb[:], dec_f32[:], AF.Square).then_inc(sc.h, 1))
                    sc.bump(1)
                    p.wait("v", sc)
                    cc = ccc
                    def ln_v1(e):
                        e.reduce_sum(cc[:, 8:9], dec_f32[:], axis=AX)
                        e.reduce_sum(cc[:, 9:10], sqb[:], axis=AX)
                        e.drain()
                        e.tensor_scalar(cc[:, 8:9], cc[:, 8:9], 1.0 / D, None, ALU.mult)
                        e.tensor_scalar(cc[:, 9:10], cc[:, 9:10], 1.0 / D, None, ALU.mult)
                        e.drain()
                        e.tensor_tensor(cc[:, 10:11], cc[:, 8:9], cc[:, 8:9], ALU.mult)
                        e.drain()
                        e.tensor_tensor(cc[:, 9:10], cc[:, 9:10], cc[:, 10:11], ALU.subtract)
                        e.drain()
                        ins = e.tensor_scalar(cc[:, 9:10], cc[:, 9:10], 1e-5, None, ALU.add)
                        ins.then_inc(ve.h, 1)
                        e.drain()
                    p.emit("v", ln_v1)
                    ve.bump(1)
                    p.wait("s", ve)
                    p.emit("s", lambda e: e.activation(cc[:, 11:12], cc[:, 9:10], AF.Sqrt).then_inc(sc.h, 1))
                    sc.bump(1)
                    p.wait("v", sc)
                    if it >= 2:
                        p.wait("v", sop[it % 2], so_mark[it - 2])
                    def ln_v2(e, gb=gb):
                        e.reciprocal(cc[:, 12:13], cc[:, 11:12])
                        e.drain()
                        e.tensor_scalar(gb, dec_f32[:], cc[:, 8:9], cc[:, 12:13],
                                        ALU.subtract, ALU.mult)
                        e.drain()
                        e.tensor_tensor(gb, gb, lng_brd[:], ALU.mult)
                        e.drain()
                        ins = e.tensor_tensor(gb, gb, lnb_brd[:], ALU.add)
                        ins.then_inc(ve.h, 1)
                        e.drain()
                    p.emit("v", ln_v2)
                    ve.bump(1)
                    p.wait("y", ve)
                    ti_glob = sl * 4 + tt
                    _dma(p, "y", sop[it % 2], g_out[b, ti_glob * 128:(ti_glob + 1) * 128, :], gb)
                    so_mark[it] = sop[it % 2].n
        p.wait("y", sop[0])
        p.wait("y", sop[1])



def _emit_top2(p, psrc, wdst, lgc, elg, inds, ccc, sc, ve):
    def sm1(e):
        e.tensor_copy(lgc, psrc)
        e.drain()
        e.reduce_max(ccc[:, 0:1], lgc, axis=AX)
        e.drain()
        ins = e.tensor_scalar(lgc, lgc, ccc[:, 0:1], None, ALU.subtract)
        ins.then_inc(ve.h, 1)
        e.drain()
    p.emit("v", sm1)
    ve.bump(1)
    p.wait("s", ve)
    p.emit("s", lambda e: e.activation(elg, lgc, AF.Exp).then_inc(sc.h, 1))
    sc.bump(1)
    p.wait("v", sc)
    def sm2(e):
        e.reduce_sum(ccc[:, 1:2], elg, axis=AX)
        e.drain()
        e.reciprocal(ccc[:, 2:3], ccc[:, 1:2])
        e.drain()
        e.tensor_scalar(lgc, elg, ccc[:, 2:3], None, ALU.mult)
        e.drain()
        e.reduce_max(ccc[:, 3:4], lgc, axis=AX)
        e.drain()
        e.tensor_scalar(inds, lgc, ccc[:, 3:4], None, ALU.is_ge)
        e.drain()
        e.scalar_tensor_tensor(elg, inds, -1e9, lgc, ALU.mult, ALU.add)
        e.drain()
        e.reduce_max(ccc[:, 4:5], elg, axis=AX)
        e.drain()
        e.tensor_scalar(inds, lgc, ccc[:, 4:5], None, ALU.is_ge)
        e.tensor_tensor(ccc[:, 5:6], ccc[:, 3:4], ccc[:, 4:5], ALU.add)
        e.drain()
        e.reciprocal(ccc[:, 6:7], ccc[:, 5:6])
        e.drain()
        ins = e.scalar_tensor_tensor(wdst, lgc, ccc[:, 6:7], inds, ALU.mult, ALU.mult)
        ins.then_inc(ve.h, 1)
        e.drain()
    p.emit("v", sm2)
    ve.bump(1)


def _emit_top2_batch(p, psrc, wdst, ind_dst, scr, sc, ve, ntt):
    W = ntt * 8
    lg = scr[:, 0:W]
    eg = scr[:, W:2 * W]
    cols = scr[:, 2 * W:]
    m0 = cols[:, 0:ntt]
    m1 = cols[:, ntt:2 * ntt]
    m2 = cols[:, 2 * ntt:3 * ntt]
    den = cols[:, 3 * ntt:4 * ntt]
    def seg(ap, c=8):
        return ap.rearrange("p (h c) -> p h c", c=c)
    def bc(ap):
        return ap.rearrange("p (h c) -> p h c", c=1).broadcast_to([128, ntt, 8])
    def sm1(e):
        e.tensor_copy(lg, psrc)
        e.drain()
        e.reduce_max(seg(m0, 1), seg(lg), axis=AX)
        e.drain()
        ins = e.tensor_tensor(seg(lg), seg(lg), bc(m0), ALU.subtract)
        ins.then_inc(ve.h, 1)
        e.drain()
    p.emit("v", sm1)
    ve.bump(1)
    p.wait("s", ve)
    p.emit("s", lambda e: e.activation(eg, lg, AF.Exp).then_inc(sc.h, 1))
    sc.bump(1)
    p.wait("v", sc)
    def sm2(e):
        e.reduce_sum(seg(m0, 1), seg(eg), axis=AX)
        e.drain()
        e.reciprocal(m0, m0)
        e.drain()
        e.tensor_tensor(seg(lg), seg(eg), bc(m0), ALU.mult)
        e.drain()
        e.reduce_max(seg(m1, 1), seg(lg), axis=AX)
        e.drain()
        e.tensor_tensor(seg(eg), seg(lg), bc(m1), ALU.is_ge)
        e.drain()
        e.scalar_tensor_tensor(eg, eg, -1e9, lg, ALU.mult, ALU.add)
        e.drain()
        e.reduce_max(seg(m2, 1), seg(eg), axis=AX)
        e.drain()
        e.tensor_tensor(seg(eg), seg(lg), bc(m2), ALU.is_ge)
        e.tensor_tensor(den, m1, m2, ALU.add)
        e.drain()
        e.reciprocal(den, den)
        e.drain()
        e.tensor_tensor(seg(lg), seg(lg), bc(den), ALU.mult)
        e.drain()
        ins = e.tensor_tensor(wdst, lg, eg, ALU.mult)
        ins.then_inc(ve.h, 1)
        e.drain()
        if ind_dst is not None:
            ins = e.tensor_copy(ind_dst, eg)
            ins.then_inc(ve.h, 1)
            e.drain()
    p.emit("v", sm2)
    ve.bump(1 + (1 if ind_dst is not None else 0))


def prep_l1ffn_host(d):
    f32 = np.float32
    out = {
        "gateW": d["L1_ffn_gate_w"].reshape(8, 128, 8).transpose(1, 0, 2).reshape(128, 64).copy(),
        "Wgu": d["L1_ffn_gate_up_w"].reshape(8, 8, 128, 32, 128).transpose(0, 2, 3, 1, 4).reshape(8 * 128, 32768).copy(),
        "Wdn": d["L1_ffn_down_w"].reshape(8, 16, 128, 1024).transpose(0, 2, 1, 3).reshape(8 * 128, 16384).copy(),
    }
    return {k: np.ascontiguousarray(v.astype(f32)) for k, v in out.items()}


def emit_l1ffn(cx, gw_in, wgu_in, wdn_in, xnT_in, g_out, pfx="l1f"):
    nc, p, E = cx.nc, cx.p, cx.E
    ld, lg, sc, ve, te = cx.ld, cx.lg, cx.sc, cx.ve, cx.te
    sop = cx.sop
    if True:
        xnT = E(nc.sbuf_tensor(pfx + "_xnT_sb", [128, KD * S], BF16))
        gateW = E(nc.sbuf_tensor(pfx + "_gateW", [128, 64], BF16))
        Wgu = E(nc.sbuf_tensor(pfx + "_Wgu", [128, 32768], BF16))
        Wdn = E(nc.sbuf_tensor(pfx + "_Wdn", [128, 16384], BF16))
        hid = E(nc.sbuf_tensor(pfx + "_hid", [128, 16 * 512], BF16))
        tmp = E(nc.sbuf_tensor(pfx + "_tmpf", [128, 512], F32))
        tmp2 = E(nc.sbuf_tensor(pfx + "_tmpf2", [128, 512], F32))
        w_s = E(nc.sbuf_tensor(pfx + "_w_s", [128, 128], F32))
        lgc = E(nc.sbuf_tensor(pfx + "_lgc", [128, 8], F32))
        elg = E(nc.sbuf_tensor(pfx + "_elg", [128, 8], F32))
        inds = E(nc.sbuf_tensor(pfx + "_inds", [128, 8], F32))
        ccc = E(nc.sbuf_tensor(pfx + "_ccc", [128, 16], F32))
        gout = [E(nc.sbuf_tensor(pfx + f"_gout{i}", [128, 1024], F32)) for i in range(2)]
        pA = E(nc.psum_tensor(pfx + "_pA", [128, 512], F32))
        pB = E(nc.psum_tensor(pfx + "_pB", [128, 512], F32))
        pR = E(nc.psum_tensor(pfx + "_pR", [128, 512], F32))
        pD0 = E(nc.psum_tensor(pfx + "_pD0", [128, 512], F32))
        pD1 = E(nc.psum_tensor(pfx + "_pD1", [128, 512], F32))

        _dma(p, "g", lg, gateW[:], gw_in)
        p.wait("g", lg)
        gw_done = lg.n

        so_mark = {}
        NSLAB = 4
        for b in range(ROWS_PER_CORE):
            if b > 0:
                p.wait("y", te)
            for kt in range(KD):
                _dma(p, "y", ld, xnT[:, kt * S:(kt + 1) * S], xnT_in[b, kt])
            p.wait("y", ld)
            xn_done = ld.n
            p.wait("t", ld, xn_done)
            p.wait("t", lg, gw_done)
            if b > 0:
                p.wait("t", ve)
            def rt_mm(e):
                for tt in range(NT):
                    for kt in range(KD):
                        ins = e.matmul(pR[:, tt * 8:(tt + 1) * 8],
                                       xnT[:, kt * S + tt * 128: kt * S + (tt + 1) * 128],
                                       gateW[:, kt * 8:(kt + 1) * 8],
                                       start=(kt == 0), stop=(kt == KD - 1))
                ins.then_inc(te.h, 1)
            p.emit("t", rt_mm)
            te.bump(1)
            p.wait("v", te)
            _emit_top2_batch(p, pR[:, 0:128], w_s[:], None, tmp2[:], sc, ve, NT)
            w_done = ve.n
            pass
            for ex in range(8):
                p.wait("g", te)
                wgu_marks = []
                for ch in range(4):
                    _dma(p, "g", lg, Wgu[:, ch * 8192:(ch + 1) * 8192],
                         wgu_in[ex * 128:(ex + 1) * 128, ch * 8192:(ch + 1) * 8192])
                    p.wait("g", lg)
                    wgu_marks.append(lg.n)
                _dma(p, "g", lg, Wdn[:], wdn_in[ex * 128:(ex + 1) * 128, :])
                p.wait("g", lg)
                w_ready = lg.n
                gsc_mark = {}
                for sl in range(NSLAB):
                    tok0 = sl * 512
                    silu_mark = {}
                    hmul_mark = {}
                    for fm in range(16):
                        bg = pA if fm % 2 == 0 else pD0
                        bu = pB if fm % 2 == 0 else pD1
                        if fm >= 2:
                            p.wait("t", sc, silu_mark[fm - 2])
                            p.wait("t", ve, hmul_mark[fm - 2])
                        else:
                            p.wait("t", ve)
                        p.wait("t", lg, wgu_marks[fm // 4])
                        def gu_mm(e, fm=fm, tok0=tok0, bg=bg, bu=bu):
                            base = fm * 2048
                            for kt in range(KD):
                                e.matmul(bg[:], Wgu[:, base + kt * 128: base + (kt + 1) * 128],
                                         xnT[:, kt * S + tok0: kt * S + tok0 + 512],
                                         start=(kt == 0), stop=(kt == KD - 1))
                            for kt in range(KD):
                                ins = e.matmul(bu[:], Wgu[:, base + 1024 + kt * 128: base + 1024 + (kt + 1) * 128],
                                               xnT[:, kt * S + tok0: kt * S + tok0 + 512],
                                               start=(kt == 0), stop=(kt == KD - 1))
                            ins.then_inc(te.h, 1)
                        p.emit("t", gu_mm)
                        te.bump(1)
                        p.wait("s", te)
                        tb = tmp[:, 0:512] if fm % 2 == 0 else tmp2[:, 0:512]
                        p.emit("s", lambda e, bg=bg, tb=tb: e.activation(tb, bg[:], AF.Silu).then_inc(sc.h, 1))
                        sc.bump(1)
                        silu_mark[fm] = sc.n
                        p.wait("v", sc)
                        def hmul(e, fm=fm, bu=bu, tb=tb):
                            ins = e.tensor_tensor(hid[:, fm * 512:(fm + 1) * 512], tb, bu[:], ALU.mult)
                            ins.then_inc(ve.h, 1)
                        p.emit("v", hmul)
                        ve.bump(1)
                        hmul_mark[fm] = ve.n
                    hid_done = ve.n
                    for tt in range(4):
                        it = ((b * 8 + ex) * NSLAB + sl) * 4 + tt
                        gb = gout[it % 2][:]
                        ti_glob = sl * 4 + tt
                        bd0 = pD0 if tt % 2 == 0 else pA
                        bd1 = pD1 if tt % 2 == 0 else pB
                        p.wait("t", ve, hid_done)
                        p.wait("t", lg, w_ready)
                        if tt >= 2:
                            p.wait("t", ve, gsc_mark[tt - 2])
                        else:
                            p.wait("t", sc, silu_mark[15])
                        def dn_mm(e, tt=tt, bd0=bd0, bd1=bd1):
                            for n2 in range(2):
                                pD = bd0 if n2 == 0 else bd1
                                for kt in range(16):
                                    ins = e.matmul(pD[:], hid[:, kt * 512 + tt * 128: kt * 512 + (tt + 1) * 128],
                                                   Wdn[:, kt * 1024 + n2 * 512: kt * 1024 + (n2 + 1) * 512],
                                                   start=(kt == 0), stop=(kt == 15))
                            ins.then_inc(te.h, 1)
                        p.emit("t", dn_mm)
                        te.bump(1)
                        if it >= 2:
                            p.wait("v", sop[it % 2], so_mark[it - 2])
                        p.wait("v", te)
                        def gsc(e, gb=gb, ti_glob=ti_glob, ex=ex, bd0=bd0, bd1=bd1):
                            e.tensor_scalar(gb[:, 0:512], bd0[:],
                                            w_s[:, ti_glob * 8 + ex: ti_glob * 8 + ex + 1],
                                            None, ALU.mult)
                            ins = e.tensor_scalar(gb[:, 512:1024], bd1[:],
                                                  w_s[:, ti_glob * 8 + ex: ti_glob * 8 + ex + 1],
                                                  None, ALU.mult)
                            ins.then_inc(ve.h, 1)
                            e.drain()
                        p.emit("v", gsc)
                        ve.bump(1)
                        gsc_mark[tt] = ve.n
                        p.wait("g", ve)
                        sem = sop[it % 2]
                        if ex == 0:
                            p.emit("g", lambda e, gb=gb, b=b, t0=ti_glob * 128, s=sem.h:
                                   e.dma_start(out=g_out[b, t0:t0 + 128, :], in_=gb).then_inc(s, 16))
                        else:
                            p.emit("g", lambda e, gb=gb, b=b, t0=ti_glob * 128, s=sem.h:
                                   e.dma_start(out=g_out[b, t0:t0 + 128, :], in_=gb,
                                               accum_op=ALU.add).then_inc(s, 16))
                        sem.bump(16)
                        so_mark[it] = sem.n
        p.wait("y", sop[0])
        p.wait("y", sop[1])



def prep_l1attn_host(d):
    f32 = np.float32
    def wlay(w):
        return w.transpose(1, 0, 2).reshape(1024, 1024).reshape(8, 128, 1024).transpose(1, 0, 2).reshape(128, 8192)
    p_ = np.arange(128)[:, None]
    f_ = np.arange(512)[None, :]
    masks = np.concatenate([(f_ >= p_ + o * 128).astype(f32) for o in range(4)], axis=1)
    out = {
        "WqT": wlay(d["L1_attn_Wq"]),
        "WkT": wlay(d["L1_attn_Wk"]),
        "WvT": wlay(d["L1_attn_Wv"]),
        "WoT": d["L1_attn_Wo"].T.reshape(8, 128, 1024).transpose(1, 0, 2).reshape(128, 8192).copy(),
        "bqc": d["L1_attn_bq"].reshape(1024).reshape(8, 128).T.copy(),
        "bkc": d["L1_attn_bk"].reshape(1024).reshape(8, 128).T.copy(),
        "bv_row": d["L1_attn_bv"].reshape(1, 1024).copy(),
        "bo_row": d["L1_attn_bo"].reshape(1, 1024).copy(),
        "masks": masks,
        "ones_col": np.ones((1, 128), f32),
    }
    return {k: np.ascontiguousarray(v.astype(f32)) for k, v in out.items()}


L1ATTN_SHAPES = [("WqT", [128, 8192]), ("WkT", [128, 8192]),
                 ("WvT", [128, 8192]), ("WoT", [128, 8192]),
                 ("bqc", [128, 8]), ("bkc", [128, 8]),
                 ("bv_row", [1, 1024]), ("bo_row", [1, 1024]),
                 ("masks", [128, 2048]), ("ones_col", [1, 128])]


def emit_l1attn(cx, din, xnT_in, s_out, pfx="l1a"):
    nc, p, E = cx.nc, cx.p, cx.E
    ld, lg, sc, ve, te = cx.ld, cx.lg, cx.sc, cx.ve, cx.te
    sop = cx.sop
    if True:
        QT = E(nc.sbuf_tensor(pfx + "_QT", [128, 8 * S], BF16))
        KT = E(nc.sbuf_tensor(pfx + "_KT", [128, 8 * S], BF16))
        V65 = E(nc.sbuf_tensor(pfx + "_V65", [128, 16 * 1040], BF16))
        xctx = E(nc.sbuf_tensor(pfx + "_xctx", [128, 8 * S], BF16))
        Wst = E(nc.sbuf_tensor(pfx + "_Wst", [128, 8192], BF16))
        masks = E(nc.sbuf_tensor(pfx + "_masks", [128, 2048], F32))
        EST = [E(nc.sbuf_tensor(pfx + f"_EST{i}", [128, 512], BF16)) for i in range(2)]
        bqc = E(nc.sbuf_tensor(pfx + "_bqc", [128, 8], F32))
        bkc = E(nc.sbuf_tensor(pfx + "_bkc", [128, 8], F32))
        bv_row = E(nc.sbuf_tensor(pfx + "_bv_row", [1, 1024], F32))
        bo_row = E(nc.sbuf_tensor(pfx + "_bo_row", [1, 1024], F32))
        ones_col = E(nc.sbuf_tensor(pfx + "_ones_col", [1, 128], F32))
        dinv = E(nc.sbuf_tensor(pfx + "_dinv", [1, 512], F32))
        ctmp = E(nc.sbuf_tensor(pfx + "_ctmp", [128, 512], F32))
        s_stage = [E(nc.sbuf_tensor(pfx + f"_sst{i}", [128, D], F32)) for i in range(2)]
        pP0 = E(nc.psum_tensor(pfx + "_pP0", [128, 512], F32))
        pP1 = E(nc.psum_tensor(pfx + "_pP1", [128, 512], F32))
        pST0 = E(nc.psum_tensor(pfx + "_pST0", [128, 512], F32))
        pST1 = E(nc.psum_tensor(pfx + "_pST1", [128, 512], F32))
        pCTX = E(nc.psum_tensor(pfx + "_pCTX", [128, 512], F32))
        pBRD = E(nc.psum_tensor(pfx + "_pBRD", [128, 512], F32))

        for name, dst in [("bqc", bqc), ("bkc", bkc), ("bv_row", bv_row),
                          ("bo_row", bo_row), ("ones_col", ones_col),
                          ("masks", masks)]:
            _dma(p, "y", ld, dst[:], din[name])
        p.wait("y", ld)
        consts_done = ld.n

        so_mark = {}
        sti = 0
        for b in range(ROWS_PER_CORE):
            if b > 0:
                p.wait("y", ve)
            for kt in range(KD):
                _dma(p, "y", ld, xctx[:, kt * S:(kt + 1) * S], xnT_in[b, kt])
            p.wait("y", ld)
            xn_done = ld.n
            p.wait("t", sc)
            proj_done = {}
            for wi, (wname, bias_mode) in enumerate([("WqT", "q"), ("WkT", "k"), ("WvT", "v")]):
                p.wait("g", te)
                _dma(p, "g", lg, Wst[:], din[wname])
                p.wait("g", lg)
                w_rdy = lg.n
                p.wait("t", lg, w_rdy)
                p.wait("t", ld, xn_done)
                p.wait("t", ld, consts_done)
                if bias_mode in ("q", "k"):
                    dst = QT if bias_mode == "q" else KT
                    bcol = bqc if bias_mode == "q" else bkc
                    for mt in range(8):
                        for nt in range(4):
                            bank = pP0 if (mt * 4 + nt) % 2 == 0 else pP1
                            p.wait("t", ve)
                            def qk_mm(e, mt=mt, nt=nt, bank=bank):
                                for kt in range(KD):
                                    ins = e.matmul(bank[:], Wst[:, kt * 1024 + mt * 128:kt * 1024 + (mt + 1) * 128],
                                                   xctx[:, kt * S + nt * 512:kt * S + (nt + 1) * 512],
                                                   start=(kt == 0), stop=(kt == KD - 1))
                                ins.then_inc(te.h, 1)
                            p.emit("t", qk_mm)
                            te.bump(1)
                            p.wait("v", te)
                            def qk_ev(e, mt=mt, nt=nt, bank=bank, dst=dst, bcol=bcol):
                                ins = e.tensor_scalar(dst[:, mt * S + nt * 512:mt * S + (nt + 1) * 512],
                                                      bank[:], bcol[:, mt:mt + 1], None, ALU.add)
                                ins.then_inc(ve.h, 1)
                            p.emit("v", qk_ev)
                            ve.bump(1)
                else:
                    for ti in range(NT):
                        for n2 in range(2):
                            bank = pP0 if (ti * 2 + n2) % 2 == 0 else pP1
                            p.wait("t", ve)
                            def v_mm(e, ti=ti, n2=n2, bank=bank):
                                for kt in range(KD):
                                    e.matmul(bank[:], xctx[:, kt * S + ti * 128:kt * S + (ti + 1) * 128],
                                             Wst[:, kt * 1024 + n2 * 512:kt * 1024 + (n2 + 1) * 512],
                                             start=(kt == 0), stop=False)
                                ins = e.matmul(bank[:], ones_col[0:1, 0:128],
                                               bv_row[0:1, n2 * 512:(n2 + 1) * 512],
                                               start=False, stop=True)
                                ins.then_inc(te.h, 1)
                            p.emit("t", v_mm)
                            te.bump(1)
                            p.wait("v", te)
                            def v_ev(e, ti=ti, n2=n2, bank=bank):
                                dstv = V65[:, ti * 1040:(ti + 1) * 1040].rearrange(
                                    "p (h c) -> p h c", c=65)[:, n2 * 8:(n2 + 1) * 8, 0:64]
                                srcv = bank[:].rearrange("p (h c) -> p h c", c=64)
                                if n2 == 0:
                                    onev = V65[:, ti * 1040:(ti + 1) * 1040].rearrange(
                                        "p (h c) -> p h c", c=65)[:, :, 64:65]
                                    e.memset(onev, 1.0)
                                ins = e.tensor_copy(dstv, srcv)
                                ins.then_inc(ve.h, 1)
                            p.emit("v", v_ev)
                            ve.bump(1)
            qkv_done_te = te.n
            qkv_done_ve = ve.n
            for h in range(16):
                kt_half = h // 2
                poff = (h % 2) * 64
                for qb in range(4):
                    nkt = qb * 4 + 4
                    est_mark = {}
                    for kt in range(nkt):
                        stb = pST0 if kt % 2 == 0 else pST1
                        eb = EST[kt % 2][:]
                        if kt >= 2:
                            sm, vm = est_mark[kt - 2]
                            p.wait("t", sc, sm)
                            if vm is not None:
                                p.wait("t", ve, vm)
                        elif kt == 0:
                            p.wait("t", sc)
                            p.wait("t", ve)
                        def st_mm(e, h=h, qb=qb, kt=kt, stb=stb, kt_half=kt_half, poff=poff):
                            ins = e.matmul(stb[:], KT[poff:poff + 64, kt_half * S + kt * 128:kt_half * S + (kt + 1) * 128],
                                           QT[poff:poff + 64, kt_half * S + qb * 512:kt_half * S + (qb + 1) * 512],
                                           start=True, stop=True)
                            ins.then_inc(te.h, 1)
                        p.emit("t", st_mm)
                        te.bump(1)
                        p.wait("s", te)
                        p.emit("s", lambda e, eb=eb, stb=stb: e.activation(
                            eb, stb[:], AF.Exp, scale=0.125).then_inc(sc.h, 1))
                        sc.bump(1)
                        diag_o = kt - qb * 4
                        if diag_o >= 0:
                            p.wait("v", sc)
                            def msk(e, eb=eb, diag_o=diag_o):
                                ins = e.tensor_tensor(eb, eb, masks[:, diag_o * 512:(diag_o + 1) * 512], ALU.mult)
                                ins.then_inc(ve.h, 1)
                                e.drain()
                            p.emit("v", msk)
                            ve.bump(1)
                            est_mark[kt] = (sc.n, ve.n)
                            p.wait("t", ve)
                        else:
                            est_mark[kt] = (sc.n, None)
                            p.wait("t", sc)
                        def cx_mm(e, h=h, kt=kt, eb=eb, first=(kt == 0), last=(kt == nkt - 1)):
                            ins = e.matmul(pCTX[0:65, :], V65[:, kt * 1040 + h * 65:kt * 1040 + h * 65 + 65],
                                           eb, start=first, stop=last)
                            if last:
                                ins.then_inc(te.h, 1)
                        p.emit("t", cx_mm)
                        if kt == nkt - 1:
                            te.bump(1)
                    p.wait("v", te)
                    def dv1(e):
                        ins = e.reciprocal(dinv[0:1, :], pCTX[64:65, :])
                        ins.then_inc(ve.h, 1)
                        e.drain()
                    p.emit("v", dv1)
                    ve.bump(1)
                    p.wait("t", ve)
                    def brd_mm(e):
                        ins = e.matmul(pBRD[0:64, :], ones_col[0:1, 0:64], dinv[0:1, :],
                                       start=True, stop=True)
                        ins.then_inc(te.h, 1)
                    p.emit("t", brd_mm)
                    te.bump(1)
                    p.wait("s", te)
                    p.emit("s", lambda e: e.activation(ctmp[0:64, :], pCTX[0:64, :],
                                                       AF.Copy).then_inc(sc.h, 1))
                    sc.bump(1)
                    p.wait("v", sc)
                    def cdiv(e, h=h, qb=qb, kt_half=kt_half, poff=poff):
                        ins = e.tensor_tensor(
                            xctx[poff:poff + 64, kt_half * S + qb * 512:kt_half * S + (qb + 1) * 512],
                            ctmp[0:64, :], pBRD[0:64, :], ALU.mult)
                        ins.then_inc(ve.h, 1)
                        e.drain()
                    p.emit("v", cdiv)
                    ve.bump(1)
                    p.wait("t", ve)
            ctx_done_ve = ve.n
            p.wait("g", te)
            _dma(p, "g", lg, Wst[:], din["WoT"])
            p.wait("g", lg)
            wo_rdy = lg.n
            p.wait("t", lg, wo_rdy)
            p.wait("t", ve, ctx_done_ve)
            p.wait("t", sc)
            for ti in range(NT):
                sb = s_stage[ti % 2][:]
                for n2 in range(2):
                    bank = pP0 if n2 == 0 else pP1
                    p.wait("t", ve)
                    def wo_mm(e, ti=ti, n2=n2, bank=bank):
                        for kt in range(KD):
                            e.matmul(bank[:], xctx[:, kt * S + ti * 128:kt * S + (ti + 1) * 128],
                                     Wst[:, kt * 1024 + n2 * 512:kt * 1024 + (n2 + 1) * 512],
                                     start=(kt == 0), stop=False)
                        ins = e.matmul(bank[:], ones_col[0:1, 0:128],
                                       bo_row[0:1, n2 * 512:(n2 + 1) * 512],
                                       start=False, stop=True)
                        ins.then_inc(te.h, 1)
                    p.emit("t", wo_mm)
                    te.bump(1)
                if sti >= 2:
                    p.wait("v", sop[sti % 2], so_mark[sti - 2])
                p.wait("v", te)
                def s_ev(e, sb=sb):
                    e.tensor_copy(sb[:, 0:512], pP0[:])
                    ins = e.tensor_copy(sb[:, 512:1024], pP1[:])
                    ins.then_inc(ve.h, 1)
                    e.drain()
                p.emit("v", s_ev)
                ve.bump(1)
                p.wait("y", ve)
                _dma(p, "y", sop[sti % 2], s_out[b, ti * 128:(ti + 1) * 128, :], sb)
                so_mark[sti] = sop[sti % 2].n
                sti += 1
        p.wait("y", sop[0])
        p.wait("y", sop[1])



def prep_l0attn_host(d):
    f32 = np.float32
    def lay(wT):
        return wT.reshape(8, 128, 1024).transpose(1, 0, 2).reshape(128, 8192)
    p_ = np.arange(128)[:, None]
    f_ = np.arange(128)[None, :]
    out = {
        "gateWT": d["L0_attn_gate_w"].T.reshape(8, 128, 8).transpose(1, 0, 2).reshape(128, 64).copy(),
        "qwT": lay(d["L0_attn_q_w"].T),
        "kwT": np.stack([lay(d["L0_attn_k_w"][e].T) for e in range(8)]).reshape(8 * 128, 8192),
        "vwT": np.stack([lay(d["L0_attn_v_w"][e].T) for e in range(8)]).reshape(8 * 128, 8192),
        "fmq1T": d["L0_attn_fmq1_w"].T.copy(),
        "fmq2T": d["L0_attn_fmq2_w"].T.copy(),
        "fmk1T": d["L0_attn_fmk1_w"].T.copy(),
        "fmk2T": d["L0_attn_fmk2_w"].T.copy(),
        "normw_row": d["L0_attn_norm_w"].reshape(1, 128).copy(),
        "owT": lay(d["L0_attn_o_w"].T),
        "diagmask4": np.tile((f_ >= p_).astype(f32), (1, 4)),
        "ones_col": np.ones((1, 128), f32),
    }
    return {k: np.ascontiguousarray(v.astype(f32)) for k, v in out.items()}


L0ATTN_SHAPES = [("gateWT", [128, 64]), ("qwT", [128, 8192]),
                 ("kwT", [1024, 8192]), ("vwT", [1024, 8192]),
                 ("fmq1T", [128, 128]), ("fmq2T", [128, 128]),
                 ("fmk1T", [128, 128]), ("fmk2T", [128, 128]),
                 ("normw_row", [1, 128]), ("owT", [128, 8192]),
                 ("diagmask4", [128, 512]), ("ones_col", [1, 128])]


def emit_l0attn(cx, din, xnT_in, s_out, v_spill, pfx="l0a"):
    nc, p, E = cx.nc, cx.p, cx.E
    ld, lg, sc, ve, te, tr = cx.ld, cx.lg, cx.sc, cx.ve, cx.te, cx.tr
    sop = cx.sop
    H = 8
    if True:
        xnT = E(nc.sbuf_tensor(pfx + "_xnT_sb", [128, KD * S], BF16))
        qp = E(nc.sbuf_tensor(pfx + "_qp", [128, H * S], BF16))
        kpf = E(nc.sbuf_tensor(pfx + "_kpf", [128, H * S], BF16))
        kpt = E(nc.sbuf_tensor(pfx + "_kpt", [128, NT * D], BF16))
        oac = E(nc.sbuf_tensor(pfx + "_oac", [128, NT * D], BF16))
        Wkv = E(nc.sbuf_tensor(pfx + "_Wkv", [128, 8192], BF16))
        vsl = E(nc.sbuf_tensor(pfx + "_vsl", [128, NT * 512], BF16))
        qf = [vsl[:, 0:S], vsl[:, S:2 * S]]
        vst = [E(nc.sbuf_tensor(pfx + f"_vst{i}", [128, D], BF16)) for i in range(2)]
        aT = [E(nc.sbuf_tensor(pfx + f"_aT{i}", [128, 512], BF16)) for i in range(1)]
        Ssb = [E(nc.sbuf_tensor(pfx + f"_Ssb{i}", [128, 128], BF16)) for i in range(4)]
        gateWT = E(nc.sbuf_tensor(pfx + "_gateWT", [128, 64], BF16))
        fmw = {n: E(nc.sbuf_tensor(pfx + "_" + n, [128, 128], BF16))
               for n in ("fmq1T", "fmq2T", "fmk1T", "fmk2T")}
        diagmask4 = E(nc.sbuf_tensor(pfx + "_diagmask4", [128, 512], BF16))
        normw_row = E(nc.sbuf_tensor(pfx + "_normw_row", [1, 128], F32))
        normw_brd = E(nc.sbuf_tensor(pfx + "_normw_brd", [128, 128], F32))
        ones_col = E(nc.sbuf_tensor(pfx + "_ones_col", [1, 128], F32))
        w_s = E(nc.sbuf_tensor(pfx + "_w_s", [128, NT * 8], F32))
        ind_s = E(nc.sbuf_tensor(pfx + "_ind_s", [128, NT * 8], F32))
        lgc = E(nc.sbuf_tensor(pfx + "_lgc", [128, 8], F32))
        elg = E(nc.sbuf_tensor(pfx + "_elg", [128, 8], F32))
        indc = E(nc.sbuf_tensor(pfx + "_indc", [128, 8], F32))
        ccc = E(nc.sbuf_tensor(pfx + "_ccc", [128, 16], F32))
        tmpf = E(nc.sbuf_tensor(pfx + "_tmpf", [128, 512], F32))
        sqf = E(nc.sbuf_tensor(pfx + "_sqf", [128, 128], F32))
        s_stage = [E(nc.sbuf_tensor(pfx + f"_sst{i}", [128, D], BF16)) for i in range(1)]
        pA = E(nc.psum_tensor(pfx + "_pA", [128, 512], F32))
        pB = E(nc.psum_tensor(pfx + "_pB", [128, 512], F32))
        pAT = E(nc.psum_tensor(pfx + "_pAT", [128, 512], F32))
        pO = E(nc.psum_tensor(pfx + "_pO", [128, 512], F32))
        pS = [E(nc.psum_tensor(pfx + f"_pS{i}", [128, 512], F32)) for i in range(4)]

        banks = [pA, pB, pAT, pO, pS[0], pS[1]]
        bfree = {i: ("v", 0) for i in range(6)}
        semmap = {"v": ve, "s": sc}

        def bank_wait(i):
            s_, v_ = bfree[i]
            p.wait("t", semmap[s_], v_)

        for name, dst in [("normw_row", normw_row),
                          ("ones_col", ones_col)]:
            _dma(p, "y", ld, dst[:], din[name])
        p.wait("y", ld)
        for name, dst in [("diagmask4", diagmask4), ("gateWT", gateWT), ("fmq1T", fmw["fmq1T"]),
                          ("fmq2T", fmw["fmq2T"]), ("fmk1T", fmw["fmk1T"]),
                          ("fmk2T", fmw["fmk2T"])]:
            _dma(p, "g", lg, dst[:], din[name])
        p.wait("g", lg)
        consts_y = ld.n
        consts_g = lg.n
        p.wait("t", ld, consts_y)
        p.emit("t", lambda e: e.matmul(pA[:, 0:128], ones_col[0:1, :], normw_row[0:1, :],
                                       start=True, stop=True).then_inc(te.h, 1))
        te.bump(1)
        p.wait("v", te)
        p.emit("v", lambda e: e.tensor_copy(normw_brd[:], pA[:, 0:128]).then_inc(ve.h, 1))
        ve.bump(1)
        bfree[0] = ("v", ve.n)

        so_mark = {}
        sti = 0
        for b in range(ROWS_PER_CORE):
            if b > 0:
                p.wait("y", te)
            for kt in range(KD):
                _dma(p, "y", ld, xnT[:, kt * S:(kt + 1) * S], xnT_in[b, kt])
            p.wait("y", ld)
            xn_done = ld.n
            p.wait("t", ld, xn_done)
            p.wait("t", lg, consts_g)
            bank_wait(0)
            def rt_mm(e):
                for ti in range(NT):
                    for kt in range(KD):
                        ins = e.matmul(pA[:, ti * 8:(ti + 1) * 8],
                                       xnT[:, kt * S + ti * 128:kt * S + (ti + 1) * 128],
                                       gateWT[:, kt * 8:(kt + 1) * 8],
                                       start=(kt == 0), stop=(kt == KD - 1))
                ins.then_inc(te.h, 1)
            p.emit("t", rt_mm)
            te.bump(1)
            p.wait("v", te)
            _emit_top2_batch(p, pA[:, 0:128], w_s[:], ind_s[:], tmpf[:], sc, ve, NT)
            bfree[0] = ("v", ve.n)
            p.wait("g", te)
            _dma(p, "g", lg, Wkv[:], din["qwT"])
            p.wait("g", lg)
            p.wait("t", lg)
            def emit_fm(h, dst, fm1, fm2, scale):
                bq = qf[h % 2]
                for nt in range(4):
                    bank_wait(2)
                    bank_wait(3)
                    def fmm(e, nt=nt, bq=bq):
                        e.matmul(banks[2][:], fmw[fm1][:], bq[:, nt * 512:(nt + 1) * 512],
                                 start=True, stop=True)
                        ins = e.matmul(banks[3][:], fmw[fm2][:], bq[:, nt * 512:(nt + 1) * 512],
                                       start=True, stop=True)
                        ins.then_inc(te.h, 1)
                    p.emit("t", fmm)
                    te.bump(1)
                    p.wait("s", te)
                    if scale is None:
                        p.emit("s", lambda e: e.activation(tmpf[:], banks[2][:],
                                                           AF.Copy).then_inc(sc.h, 1))
                    else:
                        p.emit("s", lambda e, sca=scale: e.activation(
                            tmpf[:], banks[2][:], AF.Copy, scale=sca).then_inc(sc.h, 1))
                    sc.bump(1)
                    bfree[2] = ("s", sc.n)
                    p.wait("v", sc)
                    def fme(e, h=h, nt=nt, dst=dst):
                        ins = e.tensor_tensor(dst[:, h * S + nt * 512:h * S + (nt + 1) * 512],
                                              tmpf[:], banks[3][:], ALU.mult)
                        ins.then_inc(ve.h, 1)
                        e.drain()
                    p.emit("v", fme)
                    ve.bump(1)
                    bfree[3] = ("v", ve.n)
                fm_done[h] = ve.n

            def emit_projfm(dst, fm1, fm2, scale):
                for h in range(H):
                    bq = qf[h % 2]
                    if h >= 2:
                        p.wait("t", ve, fm_done[h - 2])
                    for np2 in range(2):
                        bi0 = (0, 1, 4, 5)[np2 * 2]
                        bi1 = (0, 1, 4, 5)[np2 * 2 + 1]
                        bank_wait(bi0)
                        bank_wait(bi1)
                        def prj(e, h=h, np2=np2, bk0=banks[bi0], bk1=banks[bi1]):
                            for nt, bk in ((np2 * 2, bk0), (np2 * 2 + 1, bk1)):
                                for kt in range(KD):
                                    ins = e.matmul(bk[:], Wkv[:, kt * 1024 + h * 128:kt * 1024 + (h + 1) * 128],
                                                   xnT[:, kt * S + nt * 512:kt * S + (nt + 1) * 512],
                                                   start=(kt == 0), stop=(kt == KD - 1))
                            ins.then_inc(te.h, 1)
                        p.emit("t", prj)
                        te.bump(1)
                        p.wait("v", te)
                        def prj_ev(e, np2=np2, bk0=banks[bi0], bk1=banks[bi1], bq=bq):
                            e.tensor_copy(bq[:, (np2 * 2) * 512:(np2 * 2 + 1) * 512], bk0[:])
                            ins = e.tensor_copy(bq[:, (np2 * 2 + 1) * 512:(np2 * 2 + 2) * 512], bk1[:])
                            ins.then_inc(ve.h, 1)
                        p.emit("v", prj_ev)
                        ve.bump(1)
                        bfree[bi0] = ("v", ve.n)
                        bfree[bi1] = ("v", ve.n)
                    if h >= 1:
                        emit_fm(h - 1, dst, fm1, fm2, scale)
                emit_fm(H - 1, dst, fm1, fm2, scale)
            fm_done = {}
            emit_projfm(qp[:], "fmq1T", "fmq2T", 0.08838834764831845)
            for ex in range(8):
                p.wait("g", te)
                p.wait("g", tr)
                _dma(p, "g", lg, Wkv[:], din["kwT"][ex * 128:(ex + 1) * 128, :])
                p.wait("g", lg)
                p.wait("t", lg)
                fm_done = {}
                emit_projfm(kpf[:], "fmk1T", "fmk2T", None)
                kp_done = ve.n
                p.wait("g", te)
                _dma(p, "g", lg, Wkv[:], din["vwT"][ex * 128:(ex + 1) * 128, :])
                p.wait("g", lg)
                p.wait("t", lg)
                p.wait("y", ve, kp_done)
                p.wait("y", lg)
                p.wait("y", sop[0])
                p.wait("y", sop[1])
                spill_mark = {}
                for ti in range(NT):
                    bv = vst[ti % 2][:]
                    if ti >= 2:
                        p.wait("v", ld, spill_mark[ti - 2])
                    b0 = (0, 1, 4, 5)[(ti * 2) % 4]
                    b1 = (0, 1, 4, 5)[(ti * 2 + 1) % 4]
                    bank_wait(b0)
                    bank_wait(b1)
                    def v_mm(e, ti=ti, bk0=banks[b0], bk1=banks[b1]):
                        for n2, bk in ((0, bk0), (1, bk1)):
                            for kt in range(KD):
                                ins = e.matmul(bk[:], xnT[:, kt * S + ti * 128:kt * S + (ti + 1) * 128],
                                               Wkv[:, kt * 1024 + n2 * 512:kt * 1024 + (n2 + 1) * 512],
                                               start=(kt == 0), stop=(kt == KD - 1))
                        ins.then_inc(te.h, 1)
                    p.emit("t", v_mm)
                    te.bump(1)
                    p.wait("v", te)
                    def v_ev(e, ti=ti, bk0=banks[b0], bk1=banks[b1], bv=bv, ex=ex):
                        e.tensor_scalar(vsl[:, ti * 512:(ti + 1) * 512], bk0[:],
                                        ind_s[:, ti * 8 + ex:ti * 8 + ex + 1],
                                        None, ALU.mult)
                        ins = e.tensor_scalar(bv[:, 0:512], bk1[:],
                                              ind_s[:, ti * 8 + ex:ti * 8 + ex + 1],
                                              None, ALU.mult)
                        ins.then_inc(ve.h, 1)
                        e.drain()
                    p.emit("v", v_ev)
                    ve.bump(1)
                    bfree[b0] = ("v", ve.n)
                    bfree[b1] = ("v", ve.n)
                    p.wait("y", ve)
                    _dma(p, "y", ld, v_spill[ti * 128:(ti + 1) * 128, :], bv[:, 0:512])
                    spill_mark[ti] = ld.n
                    if ti % 2 == 1 and ti // 2 < H:
                        kd = ti // 2
                        p.wait("y", ld)
                        inv = kpf[:, kd * S:(kd + 1) * S].rearrange("p (t c) -> p t c", c=128)
                        outv = kpt[:, :].rearrange("p (t c) -> p t c", c=D)[:, :, kd * 128:(kd + 1) * 128]
                        p.emit("y", lambda e, i=inv, o=outv: e.dma_start_transpose(o, i).then_inc(tr.h, 16))
                        tr.bump(16)
                        p.wait("y", tr)
                        if kd == 3:
                            tr_half = tr.n
                tr_done = tr.n
                for hg in (0, 4):
                    p.wait("t", tr, tr_half if hg == 0 else tr_done)
                    if hg > 0:
                        p.wait("y", ve)
                        p.wait("y", te)
                        vslv = vsl[:].rearrange("p (t c) -> p t c", c=512)
                        vsp = v_spill.rearrange("(t p) d -> p t d", p=128)
                        _dma(p, "y", ld, vslv, vsp)
                        p.wait("y", ld)
                        vsl_rdy = ld.n
                        p.wait("t", ld, vsl_rdy)
                    snap_mark = {}
                    oev_mark = {}
                    atev_mark = {}
                    for c in range(NT):
                        atb = aT[0][:]
                        patc = pAT if c % 2 == 0 else pB
                        poc = pO if c % 2 == 0 else pA
                        if c == 0:
                            p.wait("t", ve)
                            p.wait("t", sc)
                        if c >= 2:
                            p.wait("t", ve, atev_mark[c - 2])
                        def at_mm(e, hg=hg, c=c, patc=patc):
                            for j in range(4):
                                h = hg + j
                                ins = e.matmul(patc[:, j * 128:(j + 1) * 128],
                                               kpf[:, h * S + c * 128:h * S + (c + 1) * 128],
                                               qp[:, h * S + c * 128:h * S + (c + 1) * 128],
                                               start=True, stop=True,
                                               skip_group_check=True)
                            ins.then_inc(te.h, 1)
                        p.emit("t", at_mm)
                        te.bump(1)
                        p.wait("v", te)
                        def at_ev(e, atb=atb, patc=patc):
                            ins = e.tensor_tensor(atb, patc[:], diagmask4[:], ALU.mult)
                            ins.then_inc(ve.h, 1)
                            e.drain()
                        p.emit("v", at_ev)
                        ve.bump(1)
                        atev_mark[c] = ve.n
                        p.wait("t", ve, atev_mark[c])
                        if c >= 1:
                            p.wait("t", sc, snap_mark[c - 1][0])
                            p.wait("t", ve, snap_mark[c - 1][1])
                        if c >= 2:
                            p.wait("t", ve, oev_mark[c - 2])
                        def oi_mm(e, hg=hg, c=c, atb=atb, poc=poc):
                            for j in range(4):
                                h = hg + j
                                ins = e.matmul(pS[j][:, 0:128],
                                               kpt[:, c * D + h * 128:c * D + (h + 1) * 128],
                                               vsl[:, c * 512 + j * 128:c * 512 + (j + 1) * 128],
                                               start=(c == 0), stop=True,
                                               skip_group_check=True)
                            ins.then_inc(te.h, 1)
                            for j in range(4):
                                h = hg + j
                                ins = e.matmul(poc[:, j * 128:(j + 1) * 128],
                                               atb[:, j * 128:(j + 1) * 128],
                                               vsl[:, c * 512 + j * 128:c * 512 + (j + 1) * 128],
                                               start=True, stop=(c == 0),
                                               skip_group_check=True)
                                if c > 0:
                                    ins = e.matmul(poc[:, j * 128:(j + 1) * 128],
                                                   qp[:, h * S + c * 128:h * S + (c + 1) * 128],
                                                   Ssb[j][:], start=False, stop=True,
                                                   skip_group_check=True)
                            ins.then_inc(te.h, 1)
                        p.emit("t", oi_mm)
                        te.bump(2)
                        p.wait("s", te)
                        def snap_s(e):
                            e.activation(Ssb[0][:], pS[0][:, 0:128], AF.Copy)
                            ins = e.activation(Ssb[1][:], pS[1][:, 0:128], AF.Copy)
                            ins.then_inc(sc.h, 1)
                        p.emit("s", snap_s)
                        sc.bump(1)
                        p.wait("v", te)
                        def snap_v(e):
                            e.tensor_copy(Ssb[2][:], pS[2][:, 0:128])
                            ins = e.tensor_copy(Ssb[3][:], pS[3][:, 0:128])
                            ins.then_inc(ve.h, 1)
                            e.drain()
                        p.emit("v", snap_v)
                        ve.bump(1)
                        snap_mark[c] = (sc.n, ve.n)
                        p.wait("v", te)
                        def o_ev(e, hg=hg, c=c, ex=ex, poc=poc):
                            wc = w_s[:, c * 8 + ex:c * 8 + ex + 1]
                            dst = oac[:, c * D + hg * 128:c * D + hg * 128 + 512]
                            if ex == 0:
                                ins = e.tensor_scalar(dst, poc[:], wc, None, ALU.mult)
                            else:
                                ins = e.scalar_tensor_tensor(dst, poc[:], wc, dst,
                                                             ALU.mult, ALU.add)
                            ins.then_inc(ve.h, 1)
                            e.drain()
                        p.emit("v", o_ev)
                        ve.bump(1)
                        oev_mark[c] = ve.n
                for _bi in range(6):
                    bfree[_bi] = ("v", ve.n)
            sq_al = vsl[:, 0:1024]
            for ti in range(NT):
                tile = oac[:, ti * D:(ti + 1) * D]
                tile3 = tile.rearrange("p (h c) -> p h c", c=128)
                p.wait("s", ve)
                p.emit("s", lambda e, tile=tile, sq_al=sq_al: e.activation(
                    sq_al, tile, AF.Square).then_inc(sc.h, 1))
                sc.bump(1)
                p.wait("v", sc)
                def rms1(e, sq_al=sq_al):
                    sq3 = sq_al.rearrange("p (h c) -> p h c", c=128)
                    e.reduce_sum(ccc[:, 0:8].rearrange("p (h c) -> p h c", c=1),
                                 sq3, axis=AX)
                    e.drain()
                    ins = e.tensor_scalar(ccc[:, 0:8], ccc[:, 0:8], 1.0 / 128, 1e-5,
                                          ALU.mult, ALU.add)
                    ins.then_inc(ve.h, 1)
                    e.drain()
                p.emit("v", rms1)
                ve.bump(1)
                p.wait("s", ve)
                p.emit("s", lambda e: e.activation(ccc[:, 8:16], ccc[:, 0:8],
                                                   AF.Sqrt).then_inc(sc.h, 1))
                sc.bump(1)
                p.wait("v", sc)
                def rms2(e, tile=tile, tile3=tile3):
                    e.reciprocal(ccc[:, 0:8], ccc[:, 8:16])
                    e.drain()
                    rb = ccc[:, 0:8].rearrange("p (h c) -> p h c", c=1).broadcast_to([128, 8, 128])
                    e.tensor_tensor(tile3, tile3, rb, ALU.mult)
                    e.drain()
                    nb = normw_brd[:].rearrange("(q p) c -> p q c", q=1).broadcast_to([128, 8, 128]) if False else \
                         normw_brd[:].rearrange("p (q c) -> p q c", q=1).broadcast_to([128, 8, 128])
                    ins = e.tensor_tensor(tile3, tile3, nb, ALU.mult)
                    ins.then_inc(ve.h, 1)
                    e.drain()
                p.emit("v", rms2)
                ve.bump(1)
            onorm_done = ve.n
            p.wait("y", ve, onorm_done)
            p.wait("y", lg)
            p.wait("y", ld)
            p.wait("y", sop[0])
            p.wait("y", sop[1])
            for ti in range(NT):
                inv = oac[:, ti * D:(ti + 1) * D]
                outv = kpf[:, :].rearrange("p (kd s) -> p kd s", s=S)[:, :, ti * 128:(ti + 1) * 128]
                p.emit("y", lambda e, i=inv, o=outv: e.dma_start_transpose(o, i).then_inc(tr.h, 16))
                tr.bump(16)
                p.wait("y", tr)
            tr2_done = tr.n
            p.wait("g", tr, tr2_done)
            p.wait("g", te)
            _dma(p, "g", lg, Wkv[:], din["owT"])
            p.wait("g", lg)
            p.wait("t", lg)
            p.wait("t", tr, tr2_done)
            for ti in range(NT):
                sb = s_stage[0][:]
                for n2 in range(2):
                    bi = (0, 1, 4, 5)[(ti * 2 + n2) % 4]
                    bank_wait(bi)
                    def ow_mm(e, ti=ti, n2=n2, bk=banks[bi]):
                        for kt in range(KD):
                            ins = e.matmul(bk[:], kpf[:, kt * S + ti * 128:kt * S + (ti + 1) * 128],
                                           Wkv[:, kt * 1024 + n2 * 512:kt * 1024 + (n2 + 1) * 512],
                                           start=(kt == 0), stop=(kt == KD - 1))
                        ins.then_inc(te.h, 1)
                    p.emit("t", ow_mm)
                    te.bump(1)
                if sti >= 1:
                    p.wait("v", lg, so_mark[sti - 1])
                p.wait("v", te)
                b0 = (0, 1, 4, 5)[(ti * 2) % 4]
                b1 = (0, 1, 4, 5)[(ti * 2 + 1) % 4]
                def s_ev(e, sb=sb, b0=b0, b1=b1):
                    e.tensor_copy(sb[:, 0:512], banks[b0][:])
                    ins = e.tensor_copy(sb[:, 512:1024], banks[b1][:])
                    ins.then_inc(ve.h, 1)
                    e.drain()
                p.emit("v", s_ev)
                ve.bump(1)
                bfree[b0] = ("v", ve.n)
                bfree[b1] = ("v", ve.n)
                p.wait("g", ve)
                _dma(p, "g", lg, s_out[b, ti * 128:(ti + 1) * 128, :], sb)
                sti += 1
                so_mark[sti - 1] = lg.n
        p.wait("y", sop[0])
        p.wait("y", sop[1])



def _split_rows(a):
    return [np.ascontiguousarray(a[c * 2:(c + 1) * 2]) for c in range(N_CORES)]


def build_one_stage(which):
    import contextlib
    nc = bass.Bass("TRN2", target_bir_lowering=False)
    cx = Ctx()
    cx.nc = nc
    xnT_in = nc.dram_tensor("xnT", [ROWS_PER_CORE, KD, 128, S], BF16, kind="ExternalInput").ap()
    s_out = nc.dram_tensor("s0", [ROWS_PER_CORE, S, D], F32, kind="ExternalOutput").ap()
    v_spill = nc.dram_tensor("i_vsp", [S, 512], BF16, kind="Internal").ap()
    din = {n: nc.dram_tensor("in_" + n, sh, F32, kind="ExternalInput").ap()
           for n, sh in L0ATTN_SHAPES}
    with contextlib.ExitStack() as gst:
        for nm in ("ld", "lg", "te", "sc", "ve", "tr"):
            setattr(cx, nm, Sem(gst.enter_context(nc.semaphore(nm))))
        cx.sop = [Sem(gst.enter_context(nc.semaphore("sop0"))),
                  Sem(gst.enter_context(nc.semaphore("sop1")))]
        cx.p = Prog(nc)
        with contextlib.ExitStack() as sst:
            cx.E = sst.enter_context
            emit_l0attn(cx, din, xnT_in, s_out, v_spill, "l0a")
            with nc.Block() as blk:
                cx.p.run(blk)
    return nc


class Ctx:
    pass


def build_fused():
    import contextlib
    nc = bass.Bass("TRN2", target_bir_lowering=False)
    x_in = nc.dram_tensor("x", [ROWS_PER_CORE, S, D], F32, kind="ExternalInput").ap()
    id_in = nc.dram_tensor("ident_in", [128, 128], F32, kind="ExternalInput").ap()
    out_t = nc.dram_tensor("out", [ROWS_PER_CORE, S, D], F32, kind="ExternalOutput").ap()
    din_l0a = {n: nc.dram_tensor("w0a_" + n, sh, F32, kind="ExternalInput").ap()
               for n, sh in L0ATTN_SHAPES}
    din_l0f = {n: nc.dram_tensor("w0f_" + n, sh, F32, kind="ExternalInput").ap()
               for n, sh in L0FFN_SHAPES}
    din_l1a = {n: nc.dram_tensor("w1a_" + n, sh, F32, kind="ExternalInput").ap()
               for n, sh in L1ATTN_SHAPES}
    gw_in = nc.dram_tensor("w1f_gateW", [128, 64], F32, kind="ExternalInput").ap()
    wgu_in = nc.dram_tensor("w1f_Wgu", [8 * 128, 32768], F32, kind="ExternalInput").ap()
    wdn_in = nc.dram_tensor("w1f_Wdn", [8 * 128, 16384], F32, kind="ExternalInput").ap()
    xnT0 = nc.dram_tensor("i_xnT0", [ROWS_PER_CORE, KD, 128, S], BF16, kind="Internal").ap()
    xnT1 = nc.dram_tensor("i_xnT1", [ROWS_PER_CORE, KD, 128, S], BF16, kind="Internal").ap()
    s0_t = nc.dram_tensor("i_s0", [ROWS_PER_CORE, S, D], F32, kind="Internal").ap()
    g0_t = nc.dram_tensor("i_g0", [ROWS_PER_CORE, S, D], F32, kind="Internal").ap()
    x1_t = nc.dram_tensor("i_x1", [ROWS_PER_CORE, S, D], F32, kind="Internal").ap()
    s1_t = nc.dram_tensor("i_s1", [ROWS_PER_CORE, S, D], F32, kind="Internal").ap()
    g1_t = nc.dram_tensor("i_g1", [ROWS_PER_CORE, S, D], F32, kind="Internal").ap()
    v_spill = nc.dram_tensor("i_vsp", [S, 512], BF16, kind="Internal").ap()

    cx = Ctx()
    cx.nc = nc
    with contextlib.ExitStack() as gst:
        for nm in ("ld", "lg", "te", "sc", "ve", "tr"):
            setattr(cx, nm, Sem(gst.enter_context(nc.semaphore(nm))))
        cx.sop = [Sem(gst.enter_context(nc.semaphore("sop0"))),
                  Sem(gst.enter_context(nc.semaphore("sop1")))]
        bar = Sem(gst.enter_context(nc.semaphore("bar")))

        def stage(fn):
            cx.p = Prog(nc)
            with contextlib.ExitStack() as sst:
                cx.E = sst.enter_context
                if bar.n > 0:
                    for eng in ("t", "v", "s", "g", "y"):
                        cx.p.wait(eng, bar)
                fn(cx)
                for s_ in (cx.ld, cx.lg, cx.te, cx.sc, cx.ve, cx.tr,
                           cx.sop[0], cx.sop[1]):
                    cx.p.wait("y", s_)
                cx.p.emit("y", lambda e, h=bar.h: e.sem_inc(h, 1))
                bar.bump(1)
                with nc.Block() as blk:
                    cx.p.run(blk)

        stage(lambda cx: emit_pre(cx, x_in, id_in, xnT0, "pre"))
        stage(lambda cx: emit_l0attn(cx, din_l0a, xnT0, s0_t, v_spill, "l0a"))
        stage(lambda cx: emit_l0ffn(cx, din_l0f, xnT0, g0_t, "l0f"))
        stage(lambda cx: emit_residual(cx, True, x_in, s0_t, g0_t, x1_t, xnT1,
                                       id_in, "mid"))
        stage(lambda cx: emit_l1attn(cx, din_l1a, xnT1, s1_t, "l1a"))
        stage(lambda cx: emit_l1ffn(cx, gw_in, wgu_in, wdn_in, xnT1, g1_t, "l1f"))
        stage(lambda cx: emit_residual(cx, False, x1_t, s1_t, g1_t, out_t,
                                       None, None, "fin"))
    return nc


_EYE = None

def kernel(**inputs):
    global _EYE
    if _EYE is None:
        _EYE = np.eye(128, dtype=np.float32)
    x = inputs["L0_attn_x"]
    xs = _split_rows(x)
    cores = list(range(N_CORES))
    wmap = {}
    for k, v in prep_l0attn_host(inputs).items():
        wmap["w0a_" + k] = v
    for k, v in prep_l0ffn_host(inputs).items():
        wmap["w0f_" + k] = v
    for k, v in prep_l1attn_host(inputs).items():
        wmap["w1a_" + k] = v
    for k, v in prep_l1ffn_host(inputs).items():
        wmap["w1f_" + k] = v
    nc = build_fused()
    in_maps = [dict(x=xs[c], ident_in=_EYE, **wmap) for c in cores]
    r = run_bass_kernel_spmd(nc, in_maps, core_ids=cores)
    return np.concatenate([np.asarray(r.results[c]["out"]) for c in cores], axis=0)
